# version 12
# baseline (speedup 1.0000x reference)
import contextlib
import numpy as np
import ml_dtypes
import concourse.bass as bass
import concourse.mybir as mybir
from concourse.bass import IndirectOffsetOnAxis
from concourse.bass_utils import run_bass_kernel_spmd

F32 = mybir.dt.float32
BF16 = mybir.dt.bfloat16
I32 = mybir.dt.int32
U8 = mybir.dt.uint8
AF = mybir.ActivationFunctionType
ALU = mybir.AluOpType
AX = mybir.AxisListType

NCORES = 8
SEQ = 4
T = 2048
D = 1024
HD = 4096
NST = 16
NH = 16
HDIM = 64
E0, H0 = 16, 512
E1, I1 = 8, 4096
K1 = 512
TT = T // 128
DC = D // 128
HT = HD // 128

TWO_PI = float(2.0 * np.pi)
INV_2PI = float(1.0 / (2.0 * np.pi))
MAGIC = float(1.5 * 2**23)


class Buf:
    __slots__ = ("last_w", "readers", "name", "excl")

    def __init__(self, name="", excl=False):
        self.last_w = None
        self.readers = {}
        self.name = name
        self.excl = excl


class Prog:
    ENGS = ("sync", "vector", "scalar", "tensor", "gpsimd")
    PARANOID = False

    def __init__(self, nc, stack):
        self.nc = nc
        self.stack = stack
        self.sem = {e: stack.enter_context(nc.semaphore(f"prog_{e}")) for e in self.ENGS}
        self.cnt = {e: 0 for e in self.ENGS}
        self.waited = {e: {} for e in self.ENGS}
        self.queues = {e: [] for e in self.ENGS}

    def _wait(self, eng, e2, c):
        if c and self.waited[eng].get(e2, 0) < c:
            self.queues[eng].append(("wait", self.sem[e2], c))
            self.waited[eng][e2] = c

    def _deps(self, eng, reads, writes):
        need = {}
        for b in reads:
            if b.last_w:
                e2, c = b.last_w
                if need.get(e2, 0) < c:
                    need[e2] = c
            if b.excl:
                for e2, c in b.readers.items():
                    if e2 != eng and need.get(e2, 0) < c:
                        need[e2] = c
        for b in writes:
            if b.last_w:
                e2, c = b.last_w
                if need.get(e2, 0) < c:
                    need[e2] = c
            for e2, c in b.readers.items():
                if need.get(e2, 0) < c:
                    need[e2] = c
        for e2 in self.ENGS:
            if e2 in need:
                self._wait(eng, e2, need[e2])

    def _post(self, eng, reads, writes, inc):
        self.cnt[eng] += inc
        c = self.cnt[eng]
        for b in writes:
            b.last_w = (eng, c)
            b.readers = {}
        for b in reads:
            b.readers[eng] = c

    def op(self, eng, fn, reads=(), writes=(), inc=1):
        if self.PARANOID:
            self.barrier()
        self._deps(eng, reads, writes)
        self.queues[eng].append(("op", fn, self.sem[eng], inc))
        self._post(eng, reads, writes, inc)

    def group(self, eng, fns, reads=(), writes=(), inc=1):
        if self.PARANOID:
            self.barrier()
        self._deps(eng, reads, writes)
        self.queues[eng].append(("grp", fns, self.sem[eng], inc))
        self._post(eng, reads, writes, inc)

    def barrier(self):
        for e in self.ENGS:
            for e2 in self.ENGS:
                if e2 != e:
                    self._wait(e, e2, self.cnt[e2])

    def replay(self):
        nc = self.nc
        for e in self.ENGS:
            if self.cnt[e]:
                self.queues[e].append(("wait", self.sem[e], self.cnt[e]))

        def runner(eng_name):
            def run(eng_obj):
                for item in self.queues[eng_name]:
                    if item[0] == "wait":
                        eng_obj.wait_ge(item[1], item[2])
                    elif item[0] == "op":
                        item[1](eng_obj).then_inc(item[2], item[3])
                    else:
                        inst = None
                        for fn in item[1]:
                            inst = fn(eng_obj)
                        inst.then_inc(item[2], item[3])
            return run

        with nc.Block() as block:
            block.sync(runner("sync"))
            block.vector(runner("vector"))
            block.scalar(runner("scalar"))
            block.tensor(runner("tensor"))
            block.gpsimd(runner("gpsimd"))


def ts2(i, n=128):
    return slice(i * n, (i + 1) * n)



def emit_ln_to_fm(p, st, x_dram_seq, xn_fm, xn_fm_buf, tiles, xn_tm_sink=None,
                  nsq_dst=None, nsq_dst_buf=None):
    nc = p.nc
    for tt in range(tiles):
        xt, xt_b = st["xt"][tt % 2]
        p.op("sync", lambda e, xt=xt, tt=tt: e.dma_start(out=xt[:], in_=x_dram_seq[ts2(tt), :]),
             reads=(st["xsrc_buf"],), writes=(xt_b,), inc=16)
        ssum, ssum_b = st["ssum"]
        ssq, ssq_b = st["ssq"]
        sq, sq_b = st["sq"]
        p.op("vector", lambda e, xt=xt: e.reduce_sum(out=ssum[:], in_=xt[:], axis=AX.X),
             reads=(xt_b,), writes=(ssum_b,))
        p.op("scalar", lambda e, xt=xt: e.activation(out=sq[:], in_=xt[:], func=AF.Square,
                                                     accum_out=ssq[:]),
             reads=(xt_b,), writes=(sq_b, ssq_b))
        mean, mean_b = st["mean"]
        rstd, rstd_b = st["rstd"]
        tmp1, tmp1_b = st["tmp1"]
        p.op("vector", lambda e: e.tensor_scalar(out=mean[:], in0=ssum[:], scalar1=1.0 / D,
                                                 scalar2=None, op0=ALU.mult),
             reads=(ssum_b,), writes=(mean_b,))
        p.op("vector", lambda e: e.tensor_tensor(out=tmp1[:], in0=mean[:], in1=mean[:], op=ALU.mult),
             reads=(mean_b,), writes=(tmp1_b,))
        p.op("vector", lambda e: e.tensor_scalar(out=tmp1[:], in0=tmp1[:], scalar1=-1.0,
                                                 scalar2=1e-5, op0=ALU.mult, op1=ALU.add),
             reads=(tmp1_b,), writes=(tmp1_b,))
        p.op("vector", lambda e: e.scalar_tensor_tensor(out=tmp1[:], in0=ssq[:], scalar=1.0 / D,
                                                        in1=tmp1[:], op0=ALU.mult, op1=ALU.add),
             reads=(ssq_b, tmp1_b), writes=(tmp1_b,))
        p.op("scalar", lambda e: e.activation(out=tmp1[:], in_=tmp1[:], func=AF.Ln),
             reads=(tmp1_b,), writes=(tmp1_b,))
        p.op("scalar", lambda e: e.activation(out=rstd[:], in_=tmp1[:], func=AF.Exp, scale=-0.5),
             reads=(tmp1_b,), writes=(rstd_b,))
        xn, xn_b = st["xn_tm"][tt % 2]
        p.op("vector", lambda e, xt=xt: e.tensor_scalar(
            out=sq[:], in0=xt[:], scalar1=mean[:, 0:1], scalar2=None, op0=ALU.subtract),
            reads=(xt_b, mean_b), writes=(sq_b,))
        p.op("vector", lambda e, xn=xn: e.tensor_scalar(
            out=xn[:], in0=sq[:], scalar1=rstd[:, 0:1], scalar2=None, op0=ALU.mult),
            reads=(sq_b, rstd_b), writes=(xn_b,))
        if nsq_dst is not None:
            p.op("scalar", lambda e, xn=xn, tt=tt: e.activation(
                out=sq[:], in_=xn[:], func=AF.Square,
                accum_out=nsq_dst[:, tt:tt + 1]),
                reads=(xn_b,), writes=(sq_b, nsq_dst_buf))
        if xn_tm_sink is not None:
            dst_ap, dst_buf = xn_tm_sink(tt)
            p.op("sync", lambda e, xn=xn, dst_ap=dst_ap: e.dma_start(out=dst_ap, in_=xn[:]),
                 reads=(xn_b,), writes=(dst_buf,), inc=16)
        tp, tp_b = st["tp_psum"]
        idn = st["ident_bf"]
        for dc in range(DC):
            p.op("tensor", lambda e, xn=xn, dc=dc: e.transpose(tp[:, 0:128], xn[:, ts2(dc)], idn[:]),
                 reads=(xn_b, st["ident_buf"]), writes=(tp_b,))
            p.op("vector", lambda e, dc=dc, tt=tt: e.tensor_copy(
                out=xn_fm[:, dc, ts2(tt)], in_=tp[:, 0:128]),
                reads=(tp_b,), writes=(xn_fm_buf,))


def alloc_ln_scratch(p, stack, pfx=""):
    nc = p.nc
    st = {}
    t0_ = stack.enter_context(nc.sbuf_tensor(f"{pfx}xt0", [128, D], F32))
    b0_ = Buf(f"{pfx}xt0")
    st["xt"] = [(t0_, b0_), (t0_, b0_)]
    st["sq"] = (stack.enter_context(nc.sbuf_tensor(f"{pfx}sq", [128, D], F32)), Buf())
    st["ssum"] = (stack.enter_context(nc.sbuf_tensor(f"{pfx}ssum", [128, 1], F32)), Buf())
    st["ssq"] = (stack.enter_context(nc.sbuf_tensor(f"{pfx}ssq", [128, 1], F32)), Buf())
    st["mean"] = (stack.enter_context(nc.sbuf_tensor(f"{pfx}mean", [128, 1], F32)), Buf())
    st["rstd"] = (stack.enter_context(nc.sbuf_tensor(f"{pfx}rstd", [128, 1], F32)), Buf())
    st["tmp1"] = (stack.enter_context(nc.sbuf_tensor(f"{pfx}tmp1", [128, 1], F32)), Buf())
    st["xn_tm"] = []
    for i in range(2):
        t = stack.enter_context(nc.sbuf_tensor(f"{pfx}xntm{i}", [128, D], BF16))
        st["xn_tm"].append((t, Buf(f"{pfx}xntm{i}")))
    st["tp_psum"] = (stack.enter_context(nc.psum_tensor(f"{pfx}tpp", [128, 1024], BF16)), Buf(excl=True))
    return st


def psum_f32(stack, nc, name, cols):
    assert cols % 512 == 0
    return (stack.enter_context(nc.psum_tensor(name, [128, cols], F32)), Buf(name, excl=True))


def emit_setup(p, stack, ins, C, pos_stack):
    nc = p.nc

    def sbuf(name, shape, dtype):
        t = stack.enter_context(nc.sbuf_tensor(name, shape, dtype))
        b = Buf(name)
        C[name] = (t, b)
        return t, b

    ident_bf, ident_bf_b = sbuf("ident_bf", [128, 128], BF16)
    ident_f32, ident_f32_b = sbuf("ident_f32", [128, 128], F32)
    iota_row_bc, iota_row_bc_b = sbuf("iota_row_bc", [128, 128], F32)
    iota_pcol, iota_pcol_b = sbuf("iota_pcol", [128, 1], F32)
    with contextlib.ExitStack() as tst:
        tmp_i = tst.enter_context(nc.sbuf_tensor("setup_tmpi", [128, T], I32))
        tmp_i_b = Buf("setup_tmpi")
        p.op("gpsimd", lambda e: e.iota(out=tmp_i[:, 0:128], pattern=[[1, 128]], base=0,
                                        channel_multiplier=0),
             writes=(tmp_i_b,))
        p.op("vector", lambda e: e.tensor_copy(out=iota_row_bc[:], in_=tmp_i[:, 0:128]),
             reads=(tmp_i_b,), writes=(iota_row_bc_b,))
        p.op("gpsimd", lambda e: e.iota(out=tmp_i[:, 0:1], pattern=[[1, 1]], base=0,
                                        channel_multiplier=1),
             reads=(iota_row_bc_b,), writes=(tmp_i_b,))
        p.op("vector", lambda e: e.tensor_copy(out=iota_pcol[:], in_=tmp_i[:, 0:1]),
             reads=(tmp_i_b,), writes=(iota_pcol_b,))
        p.op("gpsimd", lambda e: e.iota(out=tmp_i[:, 0:128], pattern=[[1, 128]], base=0,
                                        channel_multiplier=-1),
             reads=(iota_pcol_b,), writes=(tmp_i_b,))
        p.op("vector", lambda e: e.tensor_scalar(out=ident_bf[:], in0=tmp_i[:, 0:128], scalar1=0,
                                                 scalar2=None, op0=ALU.is_equal),
             reads=(tmp_i_b,), writes=(ident_bf_b,))
        p.op("vector", lambda e: e.tensor_scalar(out=ident_f32[:], in0=tmp_i[:, 0:128], scalar1=0,
                                                 scalar2=None, op0=ALU.is_equal),
             reads=(tmp_i_b,), writes=(ident_f32_b,))
        p.barrier()

    ones_row_bf, ones_row_bf_b = sbuf("ones_row_bf", [1, 128], BF16)
    p.op("vector", lambda e: e.memset(ones_row_bf[:], 1.0), writes=(ones_row_bf_b,))
    ones_col_bf, ones_col_bf_b = sbuf("ones_col_bf", [128, 1], BF16)
    p.op("vector", lambda e: e.memset(ones_col_bf[:], 1.0), writes=(ones_col_bf_b,))
    ones_f32, ones_f32_b = sbuf("ones_f32", [1, 64], F32)
    p.op("vector", lambda e: e.memset(ones_f32[:], 1.0), writes=(ones_f32_b,))

    dt_raw, dt_raw_b = sbuf("dt_raw", [128, HT], F32)
    phs, phs_b = sbuf("phs", [128, HT], F32)
    for j in range(HT):
        p.op("sync", lambda e, j=j: e.dma_start(out=dt_raw[:, j:j + 1],
                                                in_=ins["L0_attn_dt_proj"][ts2(j)].unsqueeze(1)),
             writes=(dt_raw_b,), inc=16)
        p.op("sync", lambda e, j=j: e.dma_start(out=phs[:, j:j + 1],
                                                in_=ins["L0_attn_floquet_phases"][ts2(j)].unsqueeze(1)),
             writes=(phs_b,), inc=16)
    p.op("vector", lambda e: e.tensor_scalar(out=phs[:], in0=phs[:], scalar1=INV_2PI,
                                             scalar2=None, op0=ALU.mult),
         reads=(phs_b,), writes=(phs_b,))
    dtv, dtv_b = sbuf("dtv", [128, HT], F32)
    t1, t1_b = sbuf("sp_t1", [128, HT], F32)
    t2, t2_b = sbuf("sp_t2", [128, HT], F32)
    p.op("scalar", lambda e: e.activation(out=t1[:], in_=dt_raw[:], func=AF.Abs),
         reads=(dt_raw_b,), writes=(t1_b,))
    p.op("scalar", lambda e: e.activation(out=t1[:], in_=t1[:], func=AF.Exp, scale=-1.0),
         reads=(t1_b,), writes=(t1_b,))
    p.op("scalar", lambda e: e.activation(out=t1[:], in_=t1[:], func=AF.Ln, bias=1.0),
         reads=(t1_b,), writes=(t1_b,))
    p.op("scalar", lambda e: e.activation(out=t2[:], in_=dt_raw[:], func=AF.Relu),
         reads=(dt_raw_b,), writes=(t2_b,))
    p.op("vector", lambda e: e.tensor_tensor(out=dtv[:], in0=t1[:], in1=t2[:], op=ALU.add),
         reads=(t1_b, t2_b), writes=(dtv_b,))

    arow, arow_b = sbuf("arow", [1, NST], F32)
    p.op("sync", lambda e: e.dma_start(out=arow[:], in_=ins["L0_attn_a_log"].unsqueeze(0)),
         writes=(arow_b,), inc=16)
    p.op("scalar", lambda e: e.activation(out=arow[:], in_=arow[:], func=AF.Exp),
         reads=(arow_b,), writes=(arow_b,))
    p.op("vector", lambda e: e.tensor_scalar(out=arow[:], in0=arow[:], scalar1=-1.0,
                                             scalar2=None, op0=ALU.mult),
         reads=(arow_b,), writes=(arow_b,))
    arow_sc = nc.dram_tensor("arow_sc", [1, NST], F32, kind="Internal").ap()
    arow_sc_b = Buf("arow_sc")
    p.op("sync", lambda e: e.dma_start(out=arow_sc, in_=arow[:]),
         reads=(arow_b,), writes=(arow_sc_b,), inc=16)
    abc, abc_b = sbuf("abc", [128, NST], F32)
    p.op("sync", lambda e: e.dma_start(out=abc[:], in_=arow_sc.broadcast_to((128, NST))),
         reads=(arow_sc_b,), writes=(abc_b,), inc=16)

    decay, decay_b = sbuf("decay", [128, HT * NST], F32)
    for j in range(HT):
        p.op("vector", lambda e, j=j: e.tensor_scalar(
            out=decay[:, j * NST:(j + 1) * NST], in0=abc[:], scalar1=dtv[:, j:j + 1],
            scalar2=None, op0=ALU.mult),
            reads=(abc_b, dtv_b), writes=(decay_b,))
    p.op("scalar", lambda e: e.activation(out=decay[:], in_=decay[:], func=AF.Exp),
         reads=(decay_b,), writes=(decay_b,))

    bc_lhsT, bc_lhsT_b = sbuf("bc_lhsT", [128, HT, 2 * NST], BF16)
    bpf, bpf_b = sbuf("bpf", [128, HT, NST], F32)
    p.op("sync", lambda e: e.dma_start(
        out=bpf[:], in_=ins["L0_attn_b_proj"].rearrange("(j p) n -> p j n", p=128)),
        writes=(bpf_b,), inc=16)
    for j in range(HT):
        p.op("vector", lambda e, j=j: e.tensor_copy(out=bc_lhsT[:, j, 0:NST], in_=bpf[:, j, :]),
             reads=(bpf_b,), writes=(bc_lhsT_b,))
    p.op("sync", lambda e: e.dma_start(
        out=bpf[:], in_=ins["L0_attn_c_proj"].rearrange("(j p) n -> p j n", p=128)),
        reads=(bpf_b,), writes=(bpf_b,), inc=16)
    for j in range(HT):
        p.op("vector", lambda e, j=j: e.tensor_copy(out=bc_lhsT[:, j, NST:2 * NST], in_=bpf[:, j, :]),
             reads=(bpf_b,), writes=(bc_lhsT_b,))

    pos_f = pos_stack.enter_context(nc.sbuf_tensor("pos_f", [128, T], F32))
    pos_f_b = Buf("pos_f")
    C["pos_f"] = (pos_f, pos_f_b)
    with contextlib.ExitStack() as tst2:
        tmp_i2 = tst2.enter_context(nc.sbuf_tensor("setup_tmpi2", [128, T], I32))
        tmp_i2_b = Buf("setup_tmpi2")
        p.op("gpsimd", lambda e: e.iota(out=tmp_i2[:], pattern=[[1, T]], base=0,
                                        channel_multiplier=0),
             writes=(tmp_i2_b,))
        p.op("vector", lambda e: e.tensor_copy(out=pos_f[:], in_=tmp_i2[:]),
             reads=(tmp_i2_b,), writes=(pos_f_b,))
        p.barrier()


def emit_skipw(p, stack, ins, C, skipw_sc, skipw_sc_b, wout_sc, wout_sc_b):
    nc = p.nc
    with contextlib.ExitStack() as st:
        wout_bf = st.enter_context(nc.sbuf_tensor("su_wout", [128, HT, D], BF16))
        wout_bf_b = Buf("su_wout")
        wf = st.enter_context(nc.sbuf_tensor("su_wf", [128, D], F32))
        wf_b = Buf()
        for ec in range(HT):
            p.op("sync", lambda e, ec=ec: e.dma_start(out=wf[:], in_=ins["L0_attn_output_proj"][ts2(ec), :]),
                 reads=(wf_b,), writes=(wf_b,), inc=16)
            p.op("vector", lambda e, ec=ec: e.tensor_copy(out=wout_bf[:, ec, :], in_=wf[:]),
                 reads=(wf_b,), writes=(wout_bf_b,))
            p.op("sync", lambda e, ec=ec: e.dma_start(out=wout_sc[ec], in_=wout_bf[:, ec, :]),
                 reads=(wout_bf_b,), writes=(wout_sc_b,), inc=16)
        skf = [(st.enter_context(nc.sbuf_tensor(f"su_skf{i}", [128, 128], F32)), Buf()) for i in range(2)]
        skb = [(st.enter_context(nc.sbuf_tensor(f"su_skb{i}", [128, 128], BF16)), Buf()) for i in range(2)]
        skT = [(st.enter_context(nc.sbuf_tensor(f"su_skT{i}", [128, 128], BF16)), Buf()) for i in range(2)]
        out_bf = st.enter_context(nc.sbuf_tensor("su_obf", [128, D], BF16))
        out_bf_b = Buf()
        ps_acc, ps_acc_b = psum_f32(st, nc, "su_ps", D)
        ps_tp = (st.enter_context(nc.psum_tensor("su_tp", [128, 1024], BF16)), Buf(excl=True))
        for ht in range(HT):
            mms = []
            for ec in range(HT):
                i = ec % 2
                f, f_b = skf[i]
                b, b_b = skb[i]
                tT, tT_b = skT[i]
                p.op("sync", lambda e, ht=ht, ec=ec, f=f: e.dma_start(
                    out=f[:], in_=ins["L0_attn_skip_proj"][ts2(ht), ts2(ec)]),
                    reads=(f_b,), writes=(f_b,), inc=16)
                p.op("vector", lambda e, f=f, b=b: e.tensor_copy(out=b[:], in_=f[:]),
                     reads=(f_b,), writes=(b_b,))
                p.op("tensor", lambda e, b=b: e.transpose(ps_tp[0][:, 0:128], b[:], C["ident_bf"][0][:]),
                     reads=(b_b, C["ident_bf"][1]), writes=(ps_tp[1],))
                p.op("vector", lambda e, tT=tT: e.tensor_copy(out=tT[:], in_=ps_tp[0][:, 0:128]),
                     reads=(ps_tp[1],), writes=(tT_b,))
                p.group("tensor", [
                    (lambda e, tT=tT, ec=ec, nn=nn: e.matmul(
                        ps_acc[:, nn * 512:(nn + 1) * 512], tT[:],
                        wout_bf[:, ec, nn * 512:(nn + 1) * 512],
                        start=(ec == 0), stop=(ec == HT - 1)))
                    for nn in range(2)],
                    reads=(tT_b, wout_bf_b), writes=(ps_acc_b,))
            p.op("scalar", lambda e: e.activation(out=out_bf[:], in_=ps_acc[:], func=AF.Identity),
                 reads=(ps_acc_b,), writes=(out_bf_b,))
            p.op("sync", lambda e, ht=ht: e.dma_start(out=skipw_sc[ht], in_=out_bf[:]),
                 reads=(out_bf_b,), writes=(skipw_sc_b,), inc=16)
        p.barrier()


FRONT_PARTS = {"ln", "hd", "cos", "enc", "bc", "enc_tt", "enc_hdbf", "enc_hencb", "enc_xdt", "enc_dma_x", "enc_dma_h"}


def emit_l0a_front(p, stack, ins, C, s, xdt_sc, xdt_sc_b, hd_sc, hd_sc_b,
                   bcrows_sc, bcrows_sc_b, x_src, x_src_b, dbg=None):
    nc = p.nc
    with contextlib.ExitStack() as st_:
        lnst = alloc_ln_scratch(p, st_, pfx=f"a1s{s}_")
        lnst["ident_bf"] = C["ident_bf"][0]
        lnst["ident_buf"] = C["ident_bf"][1]
        lnst["xsrc_buf"] = x_src_b
        xn_fm = st_.enter_context(nc.sbuf_tensor(f"a2s{s}_xnfm", [128, DC, T], BF16))
        xn_fm_b = Buf("xn_fm")
        sink = None
        if dbg is not None:
            xd_ap, xd_b_ = dbg["xn_dbg"]
            sink = lambda tt: (xd_ap[s, ts2(tt), :], xd_b_)
        emit_ln_to_fm(p, lnst, x_src[s], xn_fm, xn_fm_b, TT, xn_tm_sink=sink)
        if dbg is not None:
            fm_ap, fm_b_ = dbg["xnfm_dbg"]
            for dc in range(DC):
                p.op("sync", lambda e, dc=dc: e.dma_start(out=fm_ap[s, dc], in_=xn_fm[:, dc, :]),
                     reads=(xn_fm_b,), writes=(fm_b_,), inc=16)
        if "hd" not in FRONT_PARTS:
            p.barrier()
            return

        wf = [(st_.enter_context(nc.sbuf_tensor(f"a2s{s}_wf{i}", [128, DC, 128], F32)), Buf()) for i in range(2)]
        wb = [(st_.enter_context(nc.sbuf_tensor(f"a2s{s}_wb{i}", [128, DC, 128], BF16)), Buf()) for i in range(2)]
        cosb = (st_.enter_context(nc.sbuf_tensor(f"a2s{s}_cos", [128, T], F32)), Buf())
        cri = (st_.enter_context(nc.sbuf_tensor(f"a2s{s}_cri", [128, T], I32)), Buf())
        crf = (st_.enter_context(nc.sbuf_tensor(f"a2s{s}_crf", [128, T], F32)), Buf())
        henc = (st_.enter_context(nc.sbuf_tensor(f"a2s{s}_henc", [128, T], F32)), Buf())
        hencb = (st_.enter_context(nc.sbuf_tensor(f"a2s{s}_hencb", [128, T], BF16)), Buf())
        xdtb = [(st_.enter_context(nc.sbuf_tensor(f"a2s{s}_xdt{i}", [128, T], BF16)), Buf()) for i in range(2)]
        hdb = [(st_.enter_context(nc.sbuf_tensor(f"a2s{s}_hdb{i}", [128, T], BF16)), Buf()) for i in range(2)]
        bcbf = (st_.enter_context(nc.sbuf_tensor(f"a2s{s}_bcbf", [32, T], BF16)), Buf())
        ps_hd = [psum_f32(st_, nc, f"a2s{s}_pshd{i}", 512) for i in range(2)]
        ps_bc, ps_bc_b = psum_f32(st_, nc, f"a2s{s}_psbc", T)

        dtv, dtv_b = C["dtv"]
        phs, phs_b = C["phs"]
        pos_f, pos_f_b = C["pos_f"]
        for ht in range(HT):
            i = ht % 2
            f, f_b = wf[i]
            b, b_b = wb[i]
            p.op("sync", lambda e, ht=ht, f=f: e.dma_start(
                out=f[:], in_=ins["L0_attn_input_proj"][:, ts2(ht)].rearrange("(c p) h -> p c h", p=128)),
                reads=(f_b,), writes=(f_b,), inc=16)
            p.op("vector", lambda e, f=f, b=b: e.tensor_copy(
                out=b[:].rearrange("p c h -> p (c h)"), in_=f[:].rearrange("p c h -> p (c h)")),
                reads=(f_b,), writes=(b_b,))
            if "cos" not in FRONT_PARTS:
                continue
            p.op("vector", lambda e, ht=ht: e.tensor_scalar(
                out=crf[0][:], in0=pos_f[:], scalar1=phs[:, ht:ht + 1], scalar2=0.25,
                op0=ALU.mult, op1=ALU.add),
                reads=(pos_f_b, phs_b), writes=(crf[1],))
            p.op("vector", lambda e: e.tensor_copy(out=cri[0][:], in_=crf[0][:]),
                 reads=(crf[1],), writes=(cri[1],))
            p.op("vector", lambda e: e.tensor_copy(out=cosb[0][:], in_=cri[0][:]),
                 reads=(cri[1],), writes=(cosb[1],))
            p.op("vector", lambda e: e.tensor_tensor(out=crf[0][:], in0=crf[0][:], in1=cosb[0][:],
                                                     op=ALU.subtract),
                 reads=(crf[1], cosb[1]), writes=(crf[1],))
            p.op("scalar", lambda e: e.activation(out=cosb[0][:], in_=crf[0][:], func=AF.Sin,
                                                  scale=TWO_PI),
                 reads=(crf[1],), writes=(cosb[1],))
            hdt, hdt_b = hdb[i]
            if "enc" not in FRONT_PARTS:
                continue
            for hh in range(4):
                ph, ph_b = ps_hd[hh % 2]
                p.group("tensor", [
                    (lambda e, dc=dc, b=b, hh=hh, ph=ph: e.matmul(
                        ph[:], b[:, dc, :],
                        xn_fm[:, dc, hh * 512:(hh + 1) * 512],
                        start=(dc == 0), stop=(dc == DC - 1)))
                    for dc in range(DC)],
                    reads=(b_b, xn_fm_b), writes=(ph_b,))
                if "enc_tt" in FRONT_PARTS:
                    p.op("vector", lambda e, hh=hh, ph=ph: e.tensor_tensor(
                        out=henc[0][:, hh * 512:(hh + 1) * 512], in0=ph[:],
                        in1=cosb[0][:, hh * 512:(hh + 1) * 512], op=ALU.mult),
                        reads=(ph_b, cosb[1]), writes=(henc[1],))
                if "enc_hdbf" in FRONT_PARTS:
                    p.op("scalar", lambda e, hh=hh, ph=ph, hdt=hdt: e.activation(
                        out=hdt[:, hh * 512:(hh + 1) * 512], in_=ph[:], func=AF.Identity),
                        reads=(ph_b,), writes=(hdt_b,))
            if "enc_hencb" in FRONT_PARTS:
                p.op("vector", lambda e: e.tensor_copy(out=hencb[0][:], in_=henc[0][:]),
                     reads=(henc[1],), writes=(hencb[1],))
            xd, xd_b = xdtb[i]
            if "enc_xdt" in FRONT_PARTS:
                p.op("scalar", lambda e, ht=ht, xd=xd: e.activation(
                    out=xd[:], in_=henc[0][:], func=AF.Identity, scale=dtv[:, ht:ht + 1]),
                    reads=(henc[1], dtv_b), writes=(xd_b,))
            if "enc_dma_x" in FRONT_PARTS:
                p.op("sync", lambda e, ht=ht, xd=xd: e.dma_start(out=xdt_sc[s, ht], in_=xd[:]),
                     reads=(xd_b,), writes=(xdt_sc_b,), inc=16)
            if "enc_dma_h" in FRONT_PARTS:
                p.op("sync", lambda e, ht=ht, hdt=hdt: e.dma_start(out=hd_sc[s, ht], in_=hdt[:]),
                     reads=(hdt_b,), writes=(hd_sc_b,), inc=16)
            if "bc" not in FRONT_PARTS:
                continue
            p.group("tensor", [
                (lambda e, ht=ht, nn=nn: e.matmul(
                    ps_bc[0:32, nn * 512:(nn + 1) * 512], C["bc_lhsT"][0][:, ht, :],
                    hencb[0][:, nn * 512:(nn + 1) * 512],
                    start=(ht == 0), stop=(ht == HT - 1)))
                for nn in range(4)],
                reads=(hencb[1], C["bc_lhsT"][1]), writes=(ps_bc_b,))
        p.op("scalar", lambda e: e.activation(out=bcbf[0][:], in_=ps_bc[0:32, :], func=AF.Identity),
             reads=(ps_bc_b,), writes=(bcbf[1],))
        p.op("sync", lambda e: e.dma_start(out=bcrows_sc[s], in_=bcbf[0][:]),
             reads=(bcbf[1],), writes=(bcrows_sc_b,), inc=16)
        p.barrier()


def emit_l0a_scan(p, stack, ins, C, s, xdt_sc, xdt_sc_b, bcrows_sc, bcrows_sc_b,
                  y_sc, y_sc_b):
    nc = p.nc
    with contextlib.ExitStack() as st_:
        Bb = st_.enter_context(nc.sbuf_tensor(f"a3s{s}_Bb", [128, NST, T], BF16))
        Bb_b = Buf("Bb")
        Cb = st_.enter_context(nc.sbuf_tensor(f"a3s{s}_Cb", [128, NST, T], BF16))
        Cb_b = Buf("Cb")
        row = (st_.enter_context(nc.sbuf_tensor(f"a3s{s}_row", [1, T], BF16)), Buf())
        xdt = [(st_.enter_context(nc.sbuf_tensor(f"a3s{s}_xdt{i}", [128, T], BF16)), Buf()) for i in range(2)]
        ub = [(st_.enter_context(nc.sbuf_tensor(f"a3s{s}_u{i}", [128, T], BF16)), Buf()) for i in range(2)]
        hb = [(st_.enter_context(nc.sbuf_tensor(f"a3s{s}_h{i}", [128, T], BF16)), Buf()) for i in range(2)]
        hcb = [(st_.enter_context(nc.sbuf_tensor(f"a3s{s}_hc{i}", [128, T], BF16)), Buf()) for i in range(2)]
        yb = [(st_.enter_context(nc.sbuf_tensor(f"a3s{s}_y{i}", [128, T], BF16)), Buf()) for i in range(2)]
        ps_y = [psum_f32(st_, nc, f"a3s{s}_psy0", T)]
        ps_y = [ps_y[0], ps_y[0]]
        decay, decay_b = C["decay"]

        for n in range(2 * NST):
            p.op("sync", lambda e, n=n: e.dma_start(out=row[0][:], in_=bcrows_sc[s, n].unsqueeze(0)),
                 reads=(bcrows_sc_b, row[1]), writes=(row[1],), inc=16)
            ph, ph_b = ps_y[0]
            p.group("tensor", [
                (lambda e, nn=nn: e.matmul(
                    ph[:, nn * 512:(nn + 1) * 512], C["ones_row_bf"][0][:],
                    row[0][:, nn * 512:(nn + 1) * 512], start=True, stop=True))
                for nn in range(4)],
                reads=(row[1], C["ones_row_bf"][1]), writes=(ph_b,))
            dst = Bb if n < NST else Cb
            dst_b = Bb_b if n < NST else Cb_b
            nn0 = n % NST
            p.op("scalar", lambda e, dst=dst, nn0=nn0, ph=ph: e.activation(
                out=dst[:, nn0, :], in_=ph[:], func=AF.Identity),
                reads=(ph_b,), writes=(dst_b,))

        for ht in range(HT):
            i = ht % 2
            xd, xd_b = xdt[i]
            p.op("sync", lambda e, ht=ht, xd=xd: e.dma_start(out=xd[:], in_=xdt_sc[s, ht]),
                 reads=(xdt_sc_b, xd_b), writes=(xd_b,), inc=16)
            py, py_b = ps_y[i]
            for n in range(NST):
                u, u_b = ub[n % 2]
                h, h_b = hb[n % 2]
                hc, hc_b = hcb[n % 2]
                p.op("vector", lambda e, u=u, xd=xd, n=n: e.tensor_tensor(
                    out=u[:], in0=xd[:], in1=Bb[:, n, :], op=ALU.mult),
                    reads=(xd_b, Bb_b), writes=(u_b,))
                p.op("vector", lambda e, h=h, u=u, ht=ht, n=n: e.tensor_tensor_scan(
                    out=h[:], data0=decay[:, ht * NST + n: ht * NST + n + 1].broadcast_to((128, T)),
                    data1=u[:], initial=0.0, op0=ALU.mult, op1=ALU.add),
                    reads=(u_b, decay_b), writes=(h_b,))
                p.op("vector", lambda e, hc=hc, h=h, n=n: e.tensor_tensor(
                    out=hc[:], in0=h[:], in1=Cb[:, n, :], op=ALU.mult),
                    reads=(h_b, Cb_b), writes=(hc_b,))
                p.group("tensor", [
                    (lambda e, nn=nn, hc=hc, n=n, py=py: e.matmul(
                        py[:, nn * 512:(nn + 1) * 512], C["ident_bf"][0][:],
                        hc[:, nn * 512:(nn + 1) * 512],
                        start=(n == 0), stop=(n == NST - 1)))
                    for nn in range(4)],
                    reads=(hc_b, C["ident_bf"][1]), writes=(py_b,))
            y, y_b = yb[i]
            p.op("scalar", lambda e, y=y, py=py: e.activation(out=y[:], in_=py[:], func=AF.Identity),
                 reads=(py_b,), writes=(y_b,))
            p.op("sync", lambda e, ht=ht, y=y: e.dma_start(out=y_sc[s, ht], in_=y[:]),
                 reads=(y_b,), writes=(y_sc_b,), inc=16)
        p.barrier()


def emit_l0a_out(p, stack, ins, C, skipw_sc, skipw_sc_b, wout_sc, wout_sc_b,
                 hd_sc, hd_sc_b, y_sc, y_sc_b, x_src, x_src_b, x1_sc, x1_sc_b):
    nc = p.nc
    with contextlib.ExitStack() as st_:
        wo = [(st_.enter_context(nc.sbuf_tensor(f"a4_wo{i}", [128, HT, 128], BF16)), Buf()) for i in range(2)]
        sw = [(st_.enter_context(nc.sbuf_tensor(f"a4_sw{i}", [128, HT, 128], BF16)), Buf()) for i in range(2)]
        y_tq = (st_.enter_context(nc.sbuf_tensor("a4_ytq", [128, HT, 512], BF16)), Buf())
        hd_tq = (st_.enter_context(nc.sbuf_tensor("a4_hdtq", [128, HT, 512], BF16)), Buf())
        x_tq = (st_.enter_context(nc.sbuf_tensor("a4_xtq", [128, 4, D], F32)), Buf())
        o_sb = (st_.enter_context(nc.sbuf_tensor("a4_osb", [128, 512], F32)), Buf())
        x1a = (st_.enter_context(nc.sbuf_tensor("a4_x1a", [128, 4, D], F32)), Buf())
        ps_o = [psum_f32(st_, nc, f"a4_pso{i}", 512) for i in range(2)]
        ps_tp = psum_f32(st_, nc, "a4_pstp", 512)
        for s in range(SEQ):
            for tq in range(4):
                p.op("sync", lambda e, s=s, tq=tq: e.dma_start(
                    out=y_tq[0][:], in_=y_sc[s, :, :, tq * 512:(tq + 1) * 512].rearrange("h p t -> p h t")),
                    reads=(y_sc_b, y_tq[1]), writes=(y_tq[1],), inc=16)
                p.op("sync", lambda e, s=s, tq=tq: e.dma_start(
                    out=hd_tq[0][:], in_=hd_sc[s, :, :, tq * 512:(tq + 1) * 512].rearrange("h p t -> p h t")),
                    reads=(hd_sc_b, hd_tq[1]), writes=(hd_tq[1],), inc=16)
                p.op("sync", lambda e, s=s, tq=tq: e.dma_start(
                    out=x_tq[0][:], in_=x_src[s, tq * 512:(tq + 1) * 512, :].rearrange("(b p) d -> p b d", p=128)),
                    reads=(x_src_b, x_tq[1]), writes=(x_tq[1],), inc=16)
                for dc in range(DC):
                    wi = dc % 2
                    p.op("sync", lambda e, dc=dc, wi=wi: e.dma_start(
                        out=wo[wi][0][:], in_=wout_sc[:, :, ts2(dc)].rearrange("h p d -> p h d")),
                        reads=(wout_sc_b, wo[wi][1]), writes=(wo[wi][1],), inc=16)
                    p.op("sync", lambda e, dc=dc, wi=wi: e.dma_start(
                        out=sw[wi][0][:], in_=skipw_sc[:, :, ts2(dc)].rearrange("h p d -> p h d")),
                        reads=(skipw_sc_b, sw[wi][1]), writes=(sw[wi][1],), inc=16)
                    po, po_b = ps_o[dc % 2]
                    fns = []
                    for ht in range(HT):
                        fns.append(lambda e, ht=ht, wi=wi, po=po: e.matmul(
                            po[:], wo[wi][0][:, ht, :], y_tq[0][:, ht, :],
                            start=(ht == 0), stop=False))
                        fns.append(lambda e, ht=ht, wi=wi, po=po: e.matmul(
                            po[:], sw[wi][0][:, ht, :], hd_tq[0][:, ht, :],
                            start=False, stop=(ht == HT - 1)))
                    p.group("tensor", fns, reads=(wo[wi][1], sw[wi][1], y_tq[1], hd_tq[1]),
                            writes=(po_b,))
                    p.op("scalar", lambda e, po=po: e.activation(out=o_sb[0][:], in_=po[:],
                                                                 func=AF.Identity),
                         reads=(po_b,), writes=(o_sb[1],))
                    for b in range(4):
                        p.op("tensor", lambda e, b=b: e.transpose(
                            ps_tp[0][:, 0:128], o_sb[0][:, ts2(b)], C["ident_f32"][0][:]),
                            reads=(o_sb[1], C["ident_f32"][1]), writes=(ps_tp[1],))
                        p.op("vector", lambda e, b=b, dc=dc: e.tensor_tensor(
                            out=x1a[0][:, b, ts2(dc)], in0=ps_tp[0][:, 0:128],
                            in1=x_tq[0][:, b, ts2(dc)], op=ALU.add),
                            reads=(ps_tp[1], x_tq[1]), writes=(x1a[1],))
                p.op("sync", lambda e, s=s, tq=tq: e.dma_start(
                    out=x1_sc[s, tq * 512:(tq + 1) * 512, :].rearrange("(b p) d -> p b d", p=128),
                    in_=x1a[0][:]),
                    reads=(x1a[1],), writes=(x1_sc_b,), inc=16)
        p.barrier()



INPUT_SPECS = [
    ("L0_attn_x", [SEQ, T, D]), ("L0_attn_input_proj", [D, HD]),
    ("L0_attn_output_proj", [HD, D]), ("L0_attn_a_log", [NST]),
    ("L0_attn_b_proj", [HD, NST]), ("L0_attn_c_proj", [HD, NST]),
    ("L0_attn_dt_proj", [HD]), ("L0_attn_skip_proj", [HD, HD]),
    ("L0_attn_floquet_phases", [HD]),
    ("L0_ffn_gate_w", [E0, D]), ("L0_ffn_w_gate", [E0, D, H0]),
    ("L0_ffn_w_up", [E0, D, H0]), ("L0_ffn_w_down", [E0, H0, D]),
    ("L0_ffn_sw_gate", [D, H0]), ("L0_ffn_sw_up", [D, H0]), ("L0_ffn_sw_down", [H0, D]),
    ("L1_attn_wq", [D, D]), ("L1_attn_bq", [D]), ("L1_attn_wk", [D, D]),
    ("L1_attn_bk", [D]), ("L1_attn_wv", [D, D]), ("L1_attn_bv", [D]),
    ("L1_attn_wo", [D, D]), ("L1_attn_bo", [D]), ("L1_attn_alpha", [1]),
    ("L1_ffn_gate_w", [E1, D]), ("L1_ffn_w_gate", [E1, I1, D]),
    ("L1_ffn_w_up", [E1, I1, D]), ("L1_ffn_w_down", [E1, D, I1]),
]


def emit_l1f_wtp(p, wtp_stack, ins, C):
    nc = p.nc
    IC = I1 // 128
    gsc = nc.dram_tensor("f1_wgT_sc", [E1, IC, 128, DC, 128], BF16, kind="Internal").ap()
    usc = nc.dram_tensor("f1_wuT_sc", [E1, IC, 128, DC, 128], BF16, kind="Internal").ap()
    dsc = nc.dram_tensor("f1_wdT_sc", [E1, DC, IC, 128, 128], BF16, kind="Internal").ap()
    gsc_b, usc_b, dsc_b = Buf("wgT_sc"), Buf("wuT_sc"), Buf("wdT_sc")
    wf = [(wtp_stack.enter_context(nc.sbuf_tensor(f"wtp_wf{i}", [128, D], F32)), Buf())
          for i in range(2)]
    wb = [(wtp_stack.enter_context(nc.sbuf_tensor(f"wtp_wb{i}", [128, D], BF16)), Buf())
          for i in range(2)]
    wt = [(wtp_stack.enter_context(nc.sbuf_tensor(f"wtp_wt{i}", [128, DC, 128], BF16)), Buf())
          for i in range(2)]
    pt, pt_b = psum_f32(wtp_stack, nc, "wtp_ps", 512)
    tpo = pt[:].bitcast(BF16)[:, 0:128]
    jobs = []
    for ee in range(E1):
        for ic in range(IC):
            jobs.append((ins["L1_ffn_w_gate"][ee, ts2(ic), :], gsc[ee, ic], gsc_b))
            jobs.append((ins["L1_ffn_w_up"][ee, ts2(ic), :], usc[ee, ic], usc_b))
        for dc in range(DC):
            for q in range(4):
                jobs.append((ins["L1_ffn_w_down"][ee, ts2(dc), q * 1024:(q + 1) * 1024],
                             dsc[ee, dc, q * 8:(q + 1) * 8].rearrange("c p d -> p c d"),
                             dsc_b))
    for ji, (srcap, dstap, dbuf) in enumerate(jobs):
        i = ji % 2
        f, fb = wf[i]
        b, bb = wb[i]
        t, tb = wt[i]
        p.op("gpsimd", lambda e, srcap=srcap, f=f: e.dma_start(out=f[:], in_=srcap),
             reads=(fb,), writes=(fb,), inc=16)
        p.op("scalar", lambda e, f=f, b=b: e.activation(out=b[:], in_=f[:], func=AF.Identity),
             reads=(fb,), writes=(bb,))
        for dc in range(DC):
            p.op("tensor", lambda e, b=b, dc=dc: e.transpose(tpo, b[:, ts2(dc)], C["ident_bf"][0][:]),
                 reads=(bb, C["ident_bf"][1]), writes=(pt_b,))
            p.op("scalar", lambda e, t=t, dc=dc: e.activation(
                out=t[:, dc, :], in_=tpo, func=AF.Identity),
                reads=(pt_b,), writes=(tb,))
        ncols = dstap.shape[1] if len(dstap.shape) == 3 else DC
        p.op("gpsimd", lambda e, t=t, dstap=dstap, ncols=ncols: e.dma_start(
            out=dstap, in_=t[:, 0:ncols, :]),
            reads=(tb,), writes=(dbuf,), inc=16)
    return (gsc, gsc_b), (usc, usc_b), (dsc, dsc_b)


def build_core(stop_after="ALL", debug_out=()):
    nc = bass.Bass("TRN2", target_bir_lowering=False)
    ins = {}
    for name, shape in INPUT_SPECS:
        ins[name] = nc.dram_tensor(name, shape, F32, kind="ExternalInput").ap()
    out_d = nc.dram_tensor("out", [SEQ, T, D], F32, kind="ExternalOutput").ap()
    out_b = Buf("out")

    def scr(name, shape, dtype):
        kind = "ExternalOutput" if name in debug_out else "Internal"
        return nc.dram_tensor(name, shape, dtype, kind=kind).ap(), Buf(name)

    S = {}
    S["xdt_sc"] = scr("xdt_sc", [SEQ, HT, 128, T], BF16)
    S["hd_sc"] = scr("hd_sc", [SEQ, HT, 128, T], BF16)
    S["y_sc"] = scr("y_sc", [SEQ, HT, 128, T], BF16)
    S["bcrows_sc"] = scr("bcrows_sc", [SEQ, 32, T], BF16)
    S["skipw_sc"] = scr("skipw_sc", [HT, 128, D], BF16)
    S["wout_sc"] = scr("wout_sc", [HT, 128, D], BF16)
    S["x1_sc"] = scr("x1_sc", [SEQ, T, D], F32)
    S["x2_sc"] = scr("x2_sc", [SEQ, T, D], F32)
    S["x3_sc"] = scr("x3_sc", [SEQ, T, D], F32)
    S["xn3_sc"] = scr("xn3_sc", [SEQ, T, D], BF16)
    S["xn_dbg"] = scr("xn_dbg", [SEQ, T, D], BF16)
    S["xnfm_dbg"] = scr("xnfm_dbg", [SEQ, DC, 128, T], BF16)
    cin_list = [[scr(f"cin_{si}_{ei}", [K1, D], BF16) for ei in range(E1)] for si in range(SEQ)]
    S["cin_sc"] = ([[t for t, _ in row] for row in cin_list],
                   [[b for _, b in row] for row in cin_list])
    cout_list = [[scr(f"cout_{si}_{ei}", [K1, D], BF16) for ei in range(E1)] for si in range(SEQ)]
    S["cout_sc"] = ([[t for t, _ in row] for row in cout_list],
                    [[b for _, b in row] for row in cout_list])
    S["cmb_sc"] = scr("cmb_sc", [SEQ, E0 + 1, T], BF16)
    S["gidx_sc"] = scr("gidx_sc", [SEQ, 2, E1, T], F32)
    S["out"] = (out_d, out_b)
    x_src = ins["L0_attn_x"]
    x_src_b = Buf("x_in")
    S["x_src"] = (x_src, x_src_b)

    with contextlib.ExitStack() as stack:
        p = Prog(nc, stack)
        C = {}
        pos_stack = contextlib.ExitStack()
        emit_setup(p, stack, ins, C, pos_stack)
        p.barrier()
        if stop_after != "SETUP":
            emit_skipw(p, stack, ins, C, *S["skipw_sc"], *S["wout_sc"])
        if stop_after not in ("SETUP", "SKIPW"):
            for s in range(SEQ):
                emit_l0a_front(p, stack, ins, C, s, *S["xdt_sc"], *S["hd_sc"],
                               *S["bcrows_sc"], x_src, x_src_b,
                               dbg=S if debug_out else None)
        pos_stack.close()
        wtp_stack = contextlib.ExitStack()
        S["wtp"] = emit_l1f_wtp(p, wtp_stack, ins, C)
        if stop_after not in ("SETUP", "SKIPW", "FRONT"):
            for s in range(SEQ):
                emit_l0a_scan(p, stack, ins, C, s, *S["xdt_sc"], *S["bcrows_sc"],
                              *S["y_sc"])
        wtp_stack.close()
        if stop_after in ("L0A",) or stop_after == "ALL" or stop_after not in ("SETUP", "SKIPW", "FRONT", "SCAN"):
            x1_dst = S["out"] if stop_after == "L0A" else S["x1_sc"]
            emit_l0a_out(p, stack, ins, C, *S["skipw_sc"], *S["wout_sc"], *S["hd_sc"],
                         *S["y_sc"], x_src, x_src_b, *x1_dst)
        if stop_after in ("SETUP", "SKIPW", "FRONT", "SCAN"):
            with contextlib.ExitStack() as dst_:
                z = dst_.enter_context(nc.sbuf_tensor("dummyz", [128, D], F32))
                zb = Buf("z")
                p.op("vector", lambda e: e.memset(z[:], 0.0), writes=(zb,))
                p.op("sync", lambda e: e.dma_start(out=out_d[0, 0:128, :], in_=z[:]),
                     reads=(zb,), writes=(out_b,), inc=16)
        if stop_after in ("L0F", "L1A", "L1F", "ALL"):
            x2_dst = S["out"] if stop_after == "L0F" else S["x2_sc"]
            emit_l0f(p, stack, ins, C, *S["x1_sc"], *x2_dst, *S["cmb_sc"])
        if stop_after in ("L1A", "L1F", "ALL"):
            x3_dst = S["out"] if stop_after == "L1A" else S["x3_sc"]
            emit_l1a(p, stack, ins, C, *S["x2_sc"], *x3_dst)
        if stop_after in ("L1F", "ALL"):
            emit_l1f(p, stack, ins, C, S)
        p.barrier()
        p.replay()
    return nc


def emit_l0f(p, stack, ins, C, x1_sc, x1_sc_b, x2_sc, x2_sc_b, cmb_sc, cmb_sc_b):
    nc = p.nc
    with contextlib.ExitStack() as st_:
        gwT = st_.enter_context(nc.sbuf_tensor("f0_gwT", [128, DC, E0], BF16))
        gwT_b = Buf("gwT")
        gwf = (st_.enter_context(nc.sbuf_tensor("f0_gwf", [E0, D], F32)), Buf())
        gwb = (st_.enter_context(nc.sbuf_tensor("f0_gwb", [128, D], BF16)), Buf())
        gn = (st_.enter_context(nc.sbuf_tensor("f0_gn", [E0, 1], F32)), Buf())
        gsq = (st_.enter_context(nc.sbuf_tensor("f0_gsq", [E0, D], F32)), Buf())
        gpsum_stack = contextlib.ExitStack()
        ps_tp, ps_tp_b = psum_f32(gpsum_stack, nc, "f0_pstp", 512)
        p.op("sync", lambda e: e.dma_start(out=gwf[0][:], in_=ins["L0_ffn_gate_w"]),
             writes=(gwf[1],), inc=16)
        p.op("scalar", lambda e: e.activation(out=gsq[0][:], in_=gwf[0][:], func=AF.Square,
                                              accum_out=gn[0][:]),
             reads=(gwf[1],), writes=(gsq[1], gn[1]))
        p.op("scalar", lambda e: e.activation(out=gn[0][:], in_=gn[0][:], func=AF.Sqrt),
             reads=(gn[1],), writes=(gn[1],))
        p.op("vector", lambda e: e.tensor_scalar(out=gn[0][:], in0=gn[0][:], scalar1=1e-12,
                                                 scalar2=None, op0=ALU.max),
             reads=(gn[1],), writes=(gn[1],))
        p.op("vector", lambda e: e.reciprocal(out=gn[0][:], in_=gn[0][:]),
             reads=(gn[1],), writes=(gn[1],))
        p.op("vector", lambda e: e.memset(gwb[0][:], 0.0), writes=(gwb[1],))
        p.op("vector", lambda e: e.tensor_scalar(out=gwb[0][0:E0, :], in0=gwf[0][:],
                                                 scalar1=gn[0][:, 0:1], scalar2=None, op0=ALU.mult),
             reads=(gwf[1], gn[1]), writes=(gwb[1],))
        tp_out_ap = ps_tp[:].bitcast(BF16)[:, 0:128]
        tp_in_ap = ps_tp[:].bitcast(BF16)[:, 0:E0]
        for dc in range(DC):
            p.op("tensor", lambda e, dc=dc, a=tp_out_ap: e.transpose(
                a, gwb[0][:, ts2(dc)], C["ident_bf"][0][:]),
                reads=(gwb[1], C["ident_bf"][1]), writes=(ps_tp_b,))
            p.op("vector", lambda e, dc=dc, a=tp_in_ap: e.tensor_copy(
                out=gwT[:, dc, :], in_=a),
                reads=(ps_tp_b,), writes=(gwT_b,))

        p.barrier()
        gpsum_stack.close()
        for s in range(SEQ):
            _emit_l0f_seq(p, nc, ins, C, s, gwT, gwT_b, x1_sc, x1_sc_b, x2_sc, x2_sc_b,
                          cmb_sc, cmb_sc_b)
        p.barrier()


def _emit_l0f_seq(p, nc, ins, C, s, gwT, gwT_b, x1_sc, x1_sc_b, x2_sc, x2_sc_b, cmb_sc, cmb_sc_b):
    with contextlib.ExitStack() as st_:
        xn_fm = st_.enter_context(nc.sbuf_tensor(f"f0s{s}_xnfm", [128, DC, T], BF16))
        xn_fm_b = Buf("xn_fm")
        racc = st_.enter_context(nc.sbuf_tensor(f"f0s{s}_racc", [128, DC, T], F32))
        racc_b = Buf("racc")
        abuf = st_.enter_context(nc.sbuf_tensor(f"f0s{s}_abuf", [128, 4, T], BF16))
        abuf_b = Buf("abuf")
        cmbf = (st_.enter_context(nc.sbuf_tensor(f"f0s{s}_cmbf", [32, T], BF16)), Buf())
        nsq = st_.enter_context(nc.sbuf_tensor(f"f0s{s}_nsq", [128, TT], F32))
        nsq_b = Buf("nsq")
        rn = (st_.enter_context(nc.sbuf_tensor(f"f0s{s}_rn", [128, TT], F32)), Buf())
        ps_tp, ps_tp_b = psum_f32(st_, nc, f"f0s{s}_pstp", 512)
        ps_g, ps_g_b = psum_f32(st_, nc, f"f0s{s}_psg", 1024)
        ps_u, ps_u_b = psum_f32(st_, nc, f"f0s{s}_psu", 1024)

        with contextlib.ExitStack() as rt_:
            ps_a, ps_a_b = psum_f32(rt_, nc, f"f0s{s}_psa", 1024)
            lnst = alloc_ln_scratch(p, rt_, pfx=f"f0s{s}_")
            lnst["ident_bf"] = C["ident_bf"][0]
            lnst["ident_buf"] = C["ident_bf"][1]
            lnst["xsrc_buf"] = x1_sc_b
            emit_ln_to_fm(p, lnst, x1_sc[s], xn_fm, xn_fm_b, TT, nsq_dst=nsq, nsq_dst_buf=nsq_b)
            p.op("scalar", lambda e: e.activation(out=rn[0][:], in_=nsq[:], func=AF.Sqrt),
                 reads=(nsq_b,), writes=(rn[1],))
            p.op("vector", lambda e: e.tensor_scalar(out=rn[0][:], in0=rn[0][:], scalar1=1e-12,
                                                     scalar2=None, op0=ALU.max),
                 reads=(rn[1],), writes=(rn[1],))
            p.op("vector", lambda e: e.reciprocal(out=rn[0][:], in_=rn[0][:]),
                 reads=(rn[1],), writes=(rn[1],))

            lgf = (rt_.enter_context(nc.sbuf_tensor(f"f0s{s}_lgf", [128, T], F32)), Buf())
            sct = (rt_.enter_context(nc.sbuf_tensor(f"f0s{s}_sct", [128, 16], F32)), Buf())
            cmp_t = (rt_.enter_context(nc.sbuf_tensor(f"f0s{s}_cmp", [128, 16], F32)), Buf())
            rankt = (rt_.enter_context(nc.sbuf_tensor(f"f0s{s}_rank", [128, 16], F32)), Buf())
            maskt = (rt_.enter_context(nc.sbuf_tensor(f"f0s{s}_mask", [128, 16], F32)), Buf())
            ws = (rt_.enter_context(nc.sbuf_tensor(f"f0s{s}_ws", [128, 1], F32)), Buf())
            p.op("vector", lambda e: e.memset(lgf[0][:], 0.0), writes=(lgf[1],))
            for th in range(2):
                p.group("tensor", [
                    (lambda e, dc=dc, tq=tq, th=th: e.matmul(
                        ps_a[0:E0, tq * 512:(tq + 1) * 512], gwT[:, dc, :],
                        xn_fm[:, dc, th * 1024 + tq * 512: th * 1024 + (tq + 1) * 512],
                        start=(dc == 0), stop=(dc == DC - 1)))
                    for tq in range(2) for dc in range(DC)],
                    reads=(gwT_b, xn_fm_b), writes=(ps_a_b,))
                p.op("scalar", lambda e, th=th: e.activation(
                    out=lgf[0][0:E0, th * 1024:(th + 1) * 1024], in_=ps_a[0:E0, :], func=AF.Identity),
                    reads=(ps_a_b,), writes=(lgf[1],))
            p.op("vector", lambda e: e.memset(cmbf[0][:], 1.0), writes=(cmbf[1],))
            for tt in range(TT):
                p.op("tensor", lambda e, tt=tt: e.transpose(
                    ps_tp[:, 0:128], lgf[0][:, ts2(tt)], C["ident_f32"][0][:]),
                    reads=(lgf[1], C["ident_f32"][1]), writes=(ps_tp_b,))
                p.op("vector", lambda e, tt=tt: e.tensor_scalar(
                    out=sct[0][:], in0=ps_tp[:, 0:E0], scalar1=rn[0][:, tt:tt + 1], scalar2=None,
                    op0=ALU.mult),
                    reads=(ps_tp_b, rn[1]), writes=(sct[1],))
                p.op("scalar", lambda e: e.activation(out=sct[0][:], in_=sct[0][:], func=AF.Sigmoid),
                     reads=(sct[1],), writes=(sct[1],))
                for ee in range(E0):
                    p.op("vector", lambda e, ee=ee: e.tensor_scalar(
                        out=cmp_t[0][:], in0=sct[0][:], scalar1=sct[0][:, ee:ee + 1], scalar2=0.0,
                        op0=ALU.is_gt, op1=ALU.add, accum_out=rankt[0][:, ee:ee + 1]),
                        reads=(sct[1],), writes=(cmp_t[1], rankt[1]))
                p.op("vector", lambda e: e.tensor_scalar(out=maskt[0][:], in0=rankt[0][:],
                                                         scalar1=7.5, scalar2=None, op0=ALU.is_lt),
                     reads=(rankt[1],), writes=(maskt[1],))
                p.op("vector", lambda e: e.tensor_tensor(out=maskt[0][:], in0=maskt[0][:],
                                                         in1=sct[0][:], op=ALU.mult),
                     reads=(maskt[1], sct[1]), writes=(maskt[1],))
                p.op("vector", lambda e: e.tensor_scalar(
                    out=cmp_t[0][:], in0=maskt[0][:], scalar1=1.0, scalar2=0.0,
                    op0=ALU.mult, op1=ALU.add, accum_out=ws[0][:]),
                    reads=(maskt[1],), writes=(cmp_t[1], ws[1]))
                p.op("vector", lambda e: e.reciprocal(out=ws[0][:], in_=ws[0][:]),
                     reads=(ws[1],), writes=(ws[1],))
                p.op("vector", lambda e: e.tensor_scalar(out=maskt[0][:], in0=maskt[0][:],
                                                         scalar1=ws[0][:, 0:1], scalar2=None, op0=ALU.mult),
                     reads=(maskt[1], ws[1]), writes=(maskt[1],))
                p.op("tensor", lambda e, tt=tt: e.transpose(
                    ps_tp[0:E0, 0:128], maskt[0][:], C["ident_f32"][0][:]),
                    reads=(maskt[1], C["ident_f32"][1]), writes=(ps_tp_b,))
                p.op("scalar", lambda e, tt=tt: e.activation(
                    out=cmbf[0][0:E0, ts2(tt)], in_=ps_tp[0:E0, 0:128], func=AF.Identity),
                    reads=(ps_tp_b,), writes=(cmbf[1],))
            p.barrier()

        p.op("vector", lambda e: e.memset(racc[:].rearrange("p c t -> p (c t)"), 0.0),
             writes=(racc_b,))
        cmbb = (st_.enter_context(nc.sbuf_tensor(f"f0s{s}_cmbb", [128, T], BF16)), Buf())
        crow = (st_.enter_context(nc.sbuf_tensor(f"f0s{s}_crow", [1, T], BF16)), Buf())
        sg = (st_.enter_context(nc.sbuf_tensor(f"f0s{s}_sg", [128, T], F32)), Buf())
        wgf = [(st_.enter_context(nc.sbuf_tensor(f"f0s{s}_wgf{i}", [128, DC, 128], F32)), Buf()) for i in range(2)]
        wgb = [(st_.enter_context(nc.sbuf_tensor(f"f0s{s}_wgb{i}", [128, DC, 128], BF16)), Buf()) for i in range(2)]
        wuf = [(st_.enter_context(nc.sbuf_tensor(f"f0s{s}_wuf{i}", [128, DC, 128], F32)), Buf()) for i in range(2)]
        wub = [(st_.enter_context(nc.sbuf_tensor(f"f0s{s}_wub{i}", [128, DC, 128], BF16)), Buf()) for i in range(2)]
        wdf = (st_.enter_context(nc.sbuf_tensor(f"f0s{s}_wdf", [128, 2, D], F32)), Buf())
        wdb = (st_.enter_context(nc.sbuf_tensor(f"f0s{s}_wdb", [128, 4, D], BF16)), Buf())
        H0C = H0 // 128

        p.op("sync", lambda e: e.dma_start(out=cmb_sc[s], in_=cmbf[0][0:E0 + 1, :]),
             reads=(cmbf[1],), writes=(cmb_sc_b,), inc=16)
        for ee in range(E0 + 1):
            p.op("sync", lambda e, ee=ee: e.dma_start(out=crow[0][:], in_=cmb_sc[s, ee].unsqueeze(0)),
                 reads=(cmb_sc_b, crow[1]), writes=(crow[1],), inc=16)
            for th in range(2):
                p.group("tensor", [
                    (lambda e, nn=nn, th=th: e.matmul(
                        ps_g[:, nn * 512:(nn + 1) * 512], C["ones_row_bf"][0][:],
                        crow[0][0:1, th * 1024 + nn * 512: th * 1024 + (nn + 1) * 512],
                        start=True, stop=True))
                    for nn in range(2)],
                    reads=(crow[1], C["ones_row_bf"][1]), writes=(ps_g_b,))
                p.op("scalar", lambda e, th=th: e.activation(
                    out=cmbb[0][:, th * 1024:(th + 1) * 1024], in_=ps_g[:], func=AF.Identity),
                    reads=(ps_g_b,), writes=(cmbb[1],))
            for hc in range(H0C):
                i = hc % 2
                if ee < E0:
                    gsrc = ins["L0_ffn_w_gate"][ee, :, ts2(hc)]
                    usrc = ins["L0_ffn_w_up"][ee, :, ts2(hc)]
                else:
                    gsrc = ins["L0_ffn_sw_gate"][:, ts2(hc)]
                    usrc = ins["L0_ffn_sw_up"][:, ts2(hc)]
                p.op("sync", lambda e, gsrc=gsrc, i=i: e.dma_start(
                    out=wgf[i][0][:], in_=gsrc.rearrange("(c p) h -> p c h", p=128)),
                    reads=(wgf[i][1],), writes=(wgf[i][1],), inc=16)
                p.op("vector", lambda e, i=i: e.tensor_copy(
                    out=wgb[i][0][:].rearrange("p c h -> p (c h)"),
                    in_=wgf[i][0][:].rearrange("p c h -> p (c h)")),
                    reads=(wgf[i][1],), writes=(wgb[i][1],))
                p.op("sync", lambda e, usrc=usrc, i=i: e.dma_start(
                    out=wuf[i][0][:], in_=usrc.rearrange("(c p) h -> p c h", p=128)),
                    reads=(wuf[i][1],), writes=(wuf[i][1],), inc=16)
                p.op("vector", lambda e, i=i: e.tensor_copy(
                    out=wub[i][0][:].rearrange("p c h -> p (c h)"),
                    in_=wuf[i][0][:].rearrange("p c h -> p (c h)")),
                    reads=(wuf[i][1],), writes=(wub[i][1],))
                for th in range(2):
                    p.group("tensor", [
                        (lambda e, dc=dc, tq=tq, i=i, th=th: e.matmul(
                            ps_g[:, tq * 512:(tq + 1) * 512], wgb[i][0][:, dc, :],
                            xn_fm[:, dc, th * 1024 + tq * 512: th * 1024 + (tq + 1) * 512],
                            start=(dc == 0), stop=(dc == DC - 1)))
                        for tq in range(2) for dc in range(DC)],
                        reads=(wgb[i][1], xn_fm_b), writes=(ps_g_b,))
                    p.group("tensor", [
                        (lambda e, dc=dc, tq=tq, i=i, th=th: e.matmul(
                            ps_u[:, tq * 512:(tq + 1) * 512], wub[i][0][:, dc, :],
                            xn_fm[:, dc, th * 1024 + tq * 512: th * 1024 + (tq + 1) * 512],
                            start=(dc == 0), stop=(dc == DC - 1)))
                        for tq in range(2) for dc in range(DC)],
                        reads=(wub[i][1], xn_fm_b), writes=(ps_u_b,))
                    p.op("scalar", lambda e, th=th: e.activation(
                        out=sg[0][:, th * 1024:(th + 1) * 1024], in_=ps_g[:], func=AF.Silu),
                        reads=(ps_g_b,), writes=(sg[1],))
                    p.op("vector", lambda e, hc=hc, th=th: e.tensor_tensor(
                        out=abuf[:, hc, th * 1024:(th + 1) * 1024], in0=ps_u[:],
                        in1=sg[0][:, th * 1024:(th + 1) * 1024], op=ALU.mult),
                        reads=(ps_u_b, sg[1]), writes=(abuf_b,))
                p.op("vector", lambda e, hc=hc: e.tensor_tensor(
                    out=abuf[:, hc, :], in0=abuf[:, hc, :], in1=cmbb[0][:], op=ALU.mult),
                    reads=(abuf_b, cmbb[1]), writes=(abuf_b,))
            dsrc = ins["L0_ffn_w_down"][ee] if ee < E0 else ins["L0_ffn_sw_down"]
            for half in range(2):
                p.op("sync", lambda e, dsrc=dsrc, half=half: e.dma_start(
                    out=wdf[0][:], in_=dsrc[half * 256:(half + 1) * 256, :].rearrange("(c p) d -> p c d", p=128)),
                    reads=(wdf[1],), writes=(wdf[1],), inc=16)
                p.op("vector", lambda e, half=half: e.tensor_copy(
                    out=wdb[0][:, half * 2:(half + 1) * 2, :].rearrange("p c d -> p (c d)"),
                    in_=wdf[0][:].rearrange("p c d -> p (c d)")),
                    reads=(wdf[1],), writes=(wdb[1],))
            for dc in range(DC):
                for th in range(2):
                    pd, pd_b = (ps_g, ps_g_b) if (dc * 2 + th) % 2 == 0 else (ps_u, ps_u_b)
                    p.group("tensor", [
                        (lambda e, hc=hc, tq=tq, dc=dc, pd=pd, th=th: e.matmul(
                            pd[:, tq * 512:(tq + 1) * 512], wdb[0][:, hc, ts2(dc)],
                            abuf[:, hc, th * 1024 + tq * 512: th * 1024 + (tq + 1) * 512],
                            start=(hc == 0), stop=(hc == H0C - 1)))
                        for tq in range(2) for hc in range(H0C)],
                        reads=(wdb[1], abuf_b), writes=(pd_b,))
                    p.op("vector", lambda e, dc=dc, pd=pd, th=th: e.tensor_tensor(
                        out=racc[:, dc, th * 1024:(th + 1) * 1024],
                        in0=racc[:, dc, th * 1024:(th + 1) * 1024], in1=pd[:], op=ALU.add),
                        reads=(pd_b, racc_b), writes=(racc_b,))

        x1t = (st_.enter_context(nc.sbuf_tensor(f"f0s{s}_x1t", [128, D], F32)), Buf())
        x2t = (st_.enter_context(nc.sbuf_tensor(f"f0s{s}_x2t", [128, D], F32)), Buf())
        for tt in range(TT):
            p.op("sync", lambda e, tt=tt: e.dma_start(out=x1t[0][:], in_=x1_sc[s, ts2(tt), :]),
                 reads=(x1_sc_b, x1t[1]), writes=(x1t[1],), inc=16)
            for dc in range(DC):
                p.op("tensor", lambda e, tt=tt, dc=dc: e.transpose(
                    ps_tp[:, 0:128], racc[:, dc, ts2(tt)], C["ident_f32"][0][:]),
                    reads=(racc_b, C["ident_f32"][1]), writes=(ps_tp_b,))
                p.op("vector", lambda e, dc=dc: e.tensor_tensor(
                    out=x2t[0][:, ts2(dc)], in0=ps_tp[:, 0:128], in1=x1t[0][:, ts2(dc)],
                    op=ALU.add),
                    reads=(ps_tp_b, x1t[1]), writes=(x2t[1],))
            p.op("sync", lambda e, tt=tt: e.dma_start(out=x2_sc[s, ts2(tt), :], in_=x2t[0][:]),
                 reads=(x2t[1],), writes=(x2_sc_b,), inc=16)


_NC_CACHE = {}




def emit_l1a(p, stack, ins, C, x2_sc, x2_sc_b, x3_sc, x3_sc_b):
    nc = p.nc
    with contextlib.ExitStack() as st_:
        alc = (st_.enter_context(nc.sbuf_tensor("a1_alc", [128, 1], F32)), Buf())
        abo = (st_.enter_context(nc.sbuf_tensor("a1_abo", [128, DC], F32)), Buf())
        p.op("sync", lambda e: e.dma_start(
            out=alc[0][:], in_=ins["L1_attn_alpha"].unsqueeze(0).broadcast_to((128, 1))),
            writes=(alc[1],), inc=16)
        for dc in range(DC):
            p.op("sync", lambda e, dc=dc: e.dma_start(
                out=abo[0][:, dc:dc + 1], in_=ins["L1_attn_bo"][ts2(dc)].unsqueeze(1)),
                writes=(abo[1],), inc=16)
        p.op("vector", lambda e: e.tensor_scalar(out=abo[0][:], in0=abo[0][:],
                                                 scalar1=alc[0][:, 0:1], scalar2=None, op0=ALU.mult),
             reads=(abo[1], alc[1]), writes=(abo[1],))
        brow = {}
        for nm in ("bq", "bk", "bv"):
            brow[nm] = (st_.enter_context(nc.sbuf_tensor(f"a1_{nm}b", [1, D], BF16)), Buf())
        with contextlib.ExitStack() as bs_:
            f = (bs_.enter_context(nc.sbuf_tensor("a1_bf_tmp", [1, D], F32)), Buf())
            for nm in ("bq", "bk", "bv"):
                b = brow[nm]
                p.op("sync", lambda e, f=f, nm=nm: e.dma_start(
                    out=f[0][:], in_=ins["L1_attn_" + nm].unsqueeze(0)),
                    reads=(f[1],), writes=(f[1],), inc=16)
                p.op("vector", lambda e, f=f, b=b: e.tensor_copy(out=b[0][:], in_=f[0][:]),
                     reads=(f[1],), writes=(b[1],))
            p.barrier()
        ones512 = (st_.enter_context(nc.sbuf_tensor("a1_ones512", [1, 512], BF16)), Buf())
        p.op("vector", lambda e: e.memset(ones512[0][:], 1.0), writes=(ones512[1],))
        wT_sc = nc.dram_tensor("a1_wT_sc", [4, 128, DC, D], BF16, kind="Internal").ap()
        wT_sc_b = Buf("wT_sc")
        for wi, wname in enumerate(("wq", "wk", "wv", "wo")):
            with contextlib.ExitStack() as w2_:
                wT0, wT0_b = _emit_l1a_wt(p, nc, C, w2_, wname,
                                          ins["L1_attn_" + wname], f"a1w_{wname}")
                p.op("sync", lambda e, wi=wi, wT0=wT0: e.dma_start(out=wT_sc[wi], in_=wT0[:]),
                     reads=(wT0_b,), writes=(wT_sc_b,), inc=16)
                p.barrier()
        for s in range(SEQ):
            _emit_l1a_seq(p, nc, ins, C, s, x2_sc, x2_sc_b, x3_sc, x3_sc_b, alc, abo,
                          brow, ones512, wT_sc, wT_sc_b)
        p.barrier()


def _emit_l1a_wt(p, nc, C, st_, name, src_ap, pfx):
    wT = st_.enter_context(nc.sbuf_tensor(pfx + "_wT", [128, DC, D], BF16))
    wT_b = Buf(pfx)
    wf = (st_.enter_context(nc.sbuf_tensor(pfx + "_wf", [128, D], F32)), Buf())
    wb = (st_.enter_context(nc.sbuf_tensor(pfx + "_wb", [128, D], BF16)), Buf())
    ps_tp, ps_tp_b = psum_f32(st_, nc, pfx + "_tp", 512)
    tp_out = ps_tp[:].bitcast(BF16)[:, 0:128]
    for ec in range(DC):
        p.op("sync", lambda e, ec=ec: e.dma_start(out=wf[0][:], in_=src_ap[ts2(ec), :]),
             reads=(wf[1],), writes=(wf[1],), inc=16)
        p.op("vector", lambda e: e.tensor_copy(out=wb[0][:], in_=wf[0][:]),
             reads=(wf[1],), writes=(wb[1],))
        for dc in range(DC):
            p.op("tensor", lambda e, dc=dc, a=tp_out: e.transpose(
                a, wb[0][:, ts2(dc)], C["ident_bf"][0][:]),
                reads=(wb[1], C["ident_bf"][1]), writes=(ps_tp_b,))
            p.op("vector", lambda e, dc=dc, ec=ec, a=tp_out: e.tensor_copy(
                out=wT[:, dc, ts2(ec)], in_=a),
                reads=(ps_tp_b,), writes=(wT_b,))
    return wT, wT_b


def _emit_l1a_seq(p, nc, ins, C, s, x2_sc, x2_sc_b, x3_sc, x3_sc_b, alc, abo, brow,
                  ones512, wT_sc, wT_sc_b):
    NHP = NH // 2
    with contextlib.ExitStack() as st_:
        Vt = st_.enter_context(nc.sbuf_tensor(f"a1s{s}_V", [128, TT, NH, 65], BF16))
        Vt_b = Buf("V")
        att = st_.enter_context(nc.sbuf_tensor(f"a1s{s}_att", [128, NH // 2, T], BF16))
        att_b = Buf("att")
        p.op("vector", lambda e: e.memset(Vt[:].rearrange("p t h c -> p (t h c)"), 1.0),
             writes=(Vt_b,))
        qk_stack = contextlib.ExitStack()
        Qb = qk_stack.enter_context(nc.sbuf_tensor(f"a1s{s}_Q", [128, NHP, T], BF16))
        Qb_b = Buf("Q")
        Kb = qk_stack.enter_context(nc.sbuf_tensor(f"a1s{s}_K", [128, NHP, T], BF16))
        Kb_b = Buf("K")
        with contextlib.ExitStack() as qs_:
            xn_fm = qs_.enter_context(nc.sbuf_tensor(f"l1s{s}_xnfm", [128, DC, T], BF16))
            xn_fm_b = Buf("xn_fm")
            with contextlib.ExitStack() as ls_:
                lnst = alloc_ln_scratch(p, ls_, pfx=f"l1s{s}_")
                lnst["ident_bf"] = C["ident_bf"][0]
                lnst["ident_buf"] = C["ident_bf"][1]
                lnst["xsrc_buf"] = x2_sc_b
                emit_ln_to_fm(p, lnst, x2_sc[s], xn_fm, xn_fm_b, TT)
                p.barrier()
            ps_q, ps_q_b = psum_f32(qs_, nc, f"a1s{s}_psq", T)
            wT = qs_.enter_context(nc.sbuf_tensor(f"a1s{s}_wT", [128, DC, D], BF16))
            wT_b = Buf("wT")
            for wi, (wname, bias, dst, dst_b) in enumerate((("wq", brow["bq"], Qb, Qb_b),
                                            ("wk", brow["bk"], Kb, Kb_b),
                                            ("wv", brow["bv"], Vt, Vt_b))):
                if True:
                    p.op("sync", lambda e, wi=wi: e.dma_start(out=wT[:], in_=wT_sc[wi]),
                         reads=(wT_sc_b, wT_b), writes=(wT_b,), inc=16)
                    if wname != "wv":
                        for hp in range(NHP):
                            fns = []
                            for tq in range(4):
                                for dc in range(DC):
                                    fns.append(lambda e, hp=hp, tq=tq, dc=dc, wT=wT: e.matmul(
                                        ps_q[:, tq * 512:(tq + 1) * 512], wT[:, dc, ts2(hp)],
                                        xn_fm[:, dc, tq * 512:(tq + 1) * 512],
                                        start=(dc == 0), stop=False))
                                fns.append(lambda e, hp=hp, tq=tq, bias=bias: e.matmul(
                                    ps_q[:, tq * 512:(tq + 1) * 512], bias[0][0:1, ts2(hp)],
                                    ones512[0][:], start=False, stop=True))
                            p.group("tensor", fns, reads=(wT_b, xn_fm_b, bias[1], ones512[1]),
                                    writes=(ps_q_b,))
                            p.op("scalar", lambda e, hp=hp, dst=dst, dst_b=dst_b: e.activation(
                                out=dst[:, hp, :], in_=ps_q[:], func=AF.Identity),
                                reads=(ps_q_b,), writes=(dst_b,))
                    else:
                        for tt in range(TT):
                            for eh in range(2):
                                fns = []
                                for dc in range(DC):
                                    fns.append(lambda e, tt=tt, eh=eh, dc=dc, wT=wT: e.matmul(
                                        ps_q[:, eh * 512:(eh + 1) * 512], xn_fm[:, dc, ts2(tt)],
                                        wT[:, dc, eh * 512:(eh + 1) * 512],
                                        start=(dc == 0), stop=False))
                                fns.append(lambda e, tt=tt, eh=eh: e.matmul(
                                    ps_q[:, eh * 512:(eh + 1) * 512], C["ones_row_bf"][0][:],
                                    brow["bv"][0][0:1, eh * 512:(eh + 1) * 512],
                                    start=False, stop=True))
                                p.group("tensor", fns,
                                        reads=(wT_b, xn_fm_b, brow["bv"][1], C["ones_row_bf"][1]),
                                        writes=(ps_q_b,))
                            for hh in range(0, NH, 2):
                                p.op("scalar", lambda e, tt=tt, hh=hh: e.activation(
                                    out=Vt[:, tt, hh:hh + 2, 0:64], in_=ps_q[:, hh * 64:(hh + 2) * 64],
                                    func=AF.Identity),
                                    reads=(ps_q_b,), writes=(Vt_b,))
            p.barrier()
        with contextlib.ExitStack() as as_:
            G = [(as_.enter_context(nc.sbuf_tensor(f"a1s{s}_G{i}", [128, 1024], BF16)), Buf()) for i in range(2)]
            dn = (as_.enter_context(nc.sbuf_tensor(f"a1s{s}_dn", [1, T], F32)), Buf())
            dnh = (as_.enter_context(nc.sbuf_tensor(f"a1s{s}_dnh", [1, T], BF16)), Buf())
            dnl = (as_.enter_context(nc.sbuf_tensor(f"a1s{s}_dnl", [1, T], F32)), Buf())
            dnl2 = (as_.enter_context(nc.sbuf_tensor(f"a1s{s}_dnl2", [1, T], BF16)), Buf())
            rb = (as_.enter_context(nc.sbuf_tensor(f"a1s{s}_rb", [64, T], F32)), Buf())
            ps_s2 = [psum_f32(as_, nc, f"a1s{s}_pss{i}", 1024) for i in range(2)]
            ps_o, ps_o_b = psum_f32(as_, nc, f"a1s{s}_pso", T)
            for h in range(NH):
                hp, off = h // 2, (h % 2) * 64

                def qk(kt, qh, h=h, hp=hp, off=off):
                    ps, ps_b = ps_s2[kt % 2]
                    p.group("tensor", [
                        (lambda e, kt=kt, nn=nn, qh=qh, ps=ps: e.matmul(
                            ps[:, nn * 512:(nn + 1) * 512],
                            Kb[off:off + 64, hp, ts2(kt)],
                            Qb[off:off + 64, hp, qh * 1024 + nn * 512: qh * 1024 + (nn + 1) * 512],
                            start=True, stop=True))
                        for nn in range(2)],
                        reads=(Kb_b, Qb_b), writes=(ps_b,))

                def sig(kt, qh):
                    i = kt % 2
                    ps, ps_b = ps_s2[i]
                    p.op("scalar", lambda e, i=i, ps=ps: e.activation(
                        out=G[i][0][:], in_=ps[:], func=AF.Sigmoid, scale=0.125),
                        reads=(ps_b,), writes=(G[i][1],))

                def av(kt, qh, h=h):
                    i = kt % 2
                    p.group("tensor", [
                        (lambda e, kt=kt, nn=nn, qh=qh, i=i: e.matmul(
                            ps_o[0:65, qh * 1024 + nn * 512: qh * 1024 + (nn + 1) * 512],
                            Vt[:, kt, h, :], G[i][0][:, nn * 512:(nn + 1) * 512],
                            start=(kt == 0), stop=(kt == TT - 1)))
                        for nn in range(2)],
                        reads=(Vt_b, G[i][1]), writes=(ps_o_b,))

                for qh in range(2):
                    qk(0, qh)
                    sig(0, qh)
                    for kt in range(1, TT):
                        qk(kt, qh)
                        sig(kt, qh)
                        av(kt - 1, qh)
                    av(TT - 1, qh)
                p.op("scalar", lambda e: e.activation(out=dn[0][:], in_=ps_o[64:65, :],
                                                      func=AF.Copy, bias=1e-6),
                     reads=(ps_o_b,), writes=(dn[1],))
                p.op("scalar", lambda e: e.activation(out=dn[0][:], in_=dn[0][:], func=AF.Ln),
                     reads=(dn[1],), writes=(dn[1],))
                p.op("scalar", lambda e: e.activation(out=dn[0][:], in_=dn[0][:], func=AF.Exp,
                                                      scale=-1.0),
                     reads=(dn[1],), writes=(dn[1],))
                p.op("vector", lambda e: e.tensor_copy(out=dnh[0][:], in_=dn[0][:]),
                     reads=(dn[1],), writes=(dnh[1],))
                p.op("vector", lambda e: e.tensor_tensor(out=dnl[0][:], in0=dn[0][:],
                                                         in1=dnh[0][:],
                                                         op=ALU.subtract),
                     reads=(dn[1], dnh[1]), writes=(dnl[1],))
                p.op("vector", lambda e: e.tensor_copy(out=dnl2[0][:], in_=dnl[0][:]),
                     reads=(dnl[1],), writes=(dnl2[1],))
                for half in range(2):
                    ps, ps_b = ps_s2[half]
                    p.group("tensor", [
                        (lambda e, nn=nn, half=half, ps=ps: e.matmul(
                            ps[0:64, nn * 512:(nn + 1) * 512], C["ones_row_bf"][0][0:1, 0:64],
                            dnh[0][0:1, half * 1024 + nn * 512: half * 1024 + (nn + 1) * 512],
                            start=True, stop=False))
                        for nn in range(2)] + [
                        (lambda e, nn=nn, half=half, ps=ps: e.matmul(
                            ps[0:64, nn * 512:(nn + 1) * 512], C["ones_row_bf"][0][0:1, 0:64],
                            dnl2[0][0:1, half * 1024 + nn * 512: half * 1024 + (nn + 1) * 512],
                            start=False, stop=True))
                        for nn in range(2)],
                        reads=(dnh[1], dnl2[1], C["ones_row_bf"][1]), writes=(ps_b,))
                    p.op("scalar", lambda e, half=half, ps=ps: e.activation(
                        out=rb[0][:, half * 1024:(half + 1) * 1024], in_=ps[0:64, :],
                        func=AF.Identity),
                        reads=(ps_b,), writes=(rb[1],))
                p.op("vector", lambda e, hp=hp, off=off: e.tensor_tensor(
                    out=att[off:off + 64, hp, :], in0=ps_o[0:64, :], in1=rb[0][:],
                    op=ALU.mult),
                    reads=(ps_o_b, rb[1]), writes=(att_b,))
            p.barrier()
        qk_stack.close()
        with contextlib.ExitStack() as os_:
            woT = os_.enter_context(nc.sbuf_tensor(f"a1s{s}_woT", [128, DC, D], BF16))
            woT_b = Buf("woT")
            p.op("sync", lambda e: e.dma_start(out=woT[:], in_=wT_sc[3]),
                 reads=(wT_sc_b,), writes=(woT_b,), inc=16)
            ps_c, ps_c_b = psum_f32(os_, nc, f"a1s{s}_psc", 512)
            ps_t2l = [psum_f32(os_, nc, f"a1s{s}_pst2{i}", 512) for i in range(2)]
            o_sb = (os_.enter_context(nc.sbuf_tensor(f"a1s{s}_osb", [128, 512], F32)), Buf())
            x2t = (os_.enter_context(nc.sbuf_tensor(f"a1s{s}_x2t", [128, 4, D], F32)), Buf())
            x3a = (os_.enter_context(nc.sbuf_tensor(f"a1s{s}_x3a", [128, 4, D], F32)), Buf())
            lsq = (os_.enter_context(nc.sbuf_tensor(f"a1s{s}_lsq", [128, D], F32)), Buf())
            lsm = (os_.enter_context(nc.sbuf_tensor(f"a1s{s}_lsm", [128, 1], F32)), Buf())
            lqq = (os_.enter_context(nc.sbuf_tensor(f"a1s{s}_lqq", [128, 1], F32)), Buf())
            lmn = (os_.enter_context(nc.sbuf_tensor(f"a1s{s}_lmn", [128, 1], F32)), Buf())
            lrs = (os_.enter_context(nc.sbuf_tensor(f"a1s{s}_lrs", [128, 1], F32)), Buf())
            lt1 = (os_.enter_context(nc.sbuf_tensor(f"a1s{s}_lt1", [128, 1], F32)), Buf())
            for tq in range(4):
                p.op("sync", lambda e, tq=tq: e.dma_start(
                    out=x2t[0][:], in_=x2_sc[s, tq * 512:(tq + 1) * 512, :].rearrange("(b p) d -> p b d", p=128)),
                    reads=(x2_sc_b, x2t[1]), writes=(x2t[1],), inc=16)
                for b in range(4):
                    p.op("vector", lambda e, b=b: e.reduce_sum(
                        out=lsm[0][:], in_=x2t[0][:, b, :], axis=AX.X),
                        reads=(x2t[1],), writes=(lsm[1],))
                    p.op("scalar", lambda e, b=b: e.activation(
                        out=lsq[0][:], in_=x2t[0][:, b, :], func=AF.Square, accum_out=lqq[0][:]),
                        reads=(x2t[1],), writes=(lsq[1], lqq[1]))
                    p.op("vector", lambda e: e.tensor_scalar(
                        out=lmn[0][:], in0=lsm[0][:], scalar1=1.0 / D, scalar2=None, op0=ALU.mult),
                        reads=(lsm[1],), writes=(lmn[1],))
                    p.op("vector", lambda e: e.tensor_tensor(
                        out=lt1[0][:], in0=lmn[0][:], in1=lmn[0][:], op=ALU.mult),
                        reads=(lmn[1],), writes=(lt1[1],))
                    p.op("vector", lambda e: e.tensor_scalar(
                        out=lt1[0][:], in0=lt1[0][:], scalar1=-1.0, scalar2=1e-5,
                        op0=ALU.mult, op1=ALU.add),
                        reads=(lt1[1],), writes=(lt1[1],))
                    p.op("vector", lambda e: e.scalar_tensor_tensor(
                        out=lt1[0][:], in0=lqq[0][:], scalar=1.0 / D, in1=lt1[0][:],
                        op0=ALU.mult, op1=ALU.add),
                        reads=(lqq[1], lt1[1]), writes=(lt1[1],))
                    p.op("scalar", lambda e: e.activation(out=lt1[0][:], in_=lt1[0][:], func=AF.Ln),
                         reads=(lt1[1],), writes=(lt1[1],))
                    p.op("scalar", lambda e: e.activation(out=lrs[0][:], in_=lt1[0][:],
                                                          func=AF.Exp, scale=-0.5),
                         reads=(lt1[1],), writes=(lrs[1],))
                    p.op("vector", lambda e, b=b: e.tensor_scalar(
                        out=lsq[0][:], in0=x2t[0][:, b, :], scalar1=lmn[0][:, 0:1],
                        scalar2=None, op0=ALU.subtract),
                        reads=(x2t[1], lmn[1]), writes=(lsq[1],))
                    p.op("vector", lambda e: e.tensor_scalar(
                        out=lsq[0][:], in0=lsq[0][:], scalar1=lrs[0][:, 0:1],
                        scalar2=None, op0=ALU.mult),
                        reads=(lsq[1], lrs[1]), writes=(lsq[1],))
                    p.op("vector", lambda e, b=b: e.tensor_tensor(
                        out=x2t[0][:, b, :], in0=x2t[0][:, b, :], in1=lsq[0][:], op=ALU.add),
                        reads=(x2t[1], lsq[1]), writes=(x2t[1],))
                for dc in range(DC):
                    p.group("tensor", [
                        (lambda e, tq=tq, dc=dc, dci=dci: e.matmul(
                            ps_c[:], woT[:, dci, ts2(dc)],
                            att[:, dci, tq * 512:(tq + 1) * 512],
                            start=(dci == 0), stop=(dci == DC - 1)))
                        for dci in range(DC)],
                        reads=(woT_b, att_b), writes=(ps_c_b,))
                    p.op("scalar", lambda e, dc=dc: e.activation(
                        out=o_sb[0][:], in_=ps_c[:], func=AF.Identity,
                        scale=alc[0][:, 0:1], bias=abo[0][:, dc:dc + 1]),
                        reads=(ps_c_b, alc[1], abo[1]), writes=(o_sb[1],))
                    for b in range(4):
                        ps_t2, ps_t2_b = ps_t2l[b % 2]
                        p.op("tensor", lambda e, b=b, ps_t2=ps_t2: e.transpose(
                            ps_t2[:, 0:128], o_sb[0][:, ts2(b)], C["ident_f32"][0][:]),
                            reads=(o_sb[1], C["ident_f32"][1]), writes=(ps_t2_b,))
                        p.op("vector", lambda e, b=b, dc=dc, ps_t2=ps_t2: e.tensor_tensor(
                            out=x3a[0][:, b, ts2(dc)], in0=ps_t2[:, 0:128],
                            in1=x2t[0][:, b, ts2(dc)], op=ALU.add),
                            reads=(ps_t2_b, x2t[1]), writes=(x3a[1],))
                p.op("sync", lambda e, tq=tq: e.dma_start(
                    out=x3_sc[s, tq * 512:(tq + 1) * 512, :].rearrange("(b p) d -> p b d", p=128),
                    in_=x3a[0][:]),
                    reads=(x3a[1],), writes=(x3_sc_b,), inc=16)
            p.barrier()


def emit_l1f(p, stack, ins, C, S):
    nc = p.nc
    x3_sc, x3_sc_b = S["x3_sc"]
    xn3_sc, xn3_sc_b = S["xn3_sc"]
    cin_sc, cin_sc_b = S["cin_sc"]
    cout_sc, cout_sc_b = S["cout_sc"]
    gidx_sc, gidx_sc_b = S["gidx_sc"]
    out_d, out_b = S["out"]
    with contextlib.ExitStack() as st_:
        gwT = st_.enter_context(nc.sbuf_tensor("f1_gwT", [128, DC, E1], BF16))
        gwT_b = Buf("f1gwT")
        with contextlib.ExitStack() as gs_:
            gwf = (gs_.enter_context(nc.sbuf_tensor("f1_gwf", [E1, D], F32)), Buf())
            gwb = (gs_.enter_context(nc.sbuf_tensor("f1_gwb", [128, D], BF16)), Buf())
            ps_tp, ps_tp_b = psum_f32(gs_, nc, "f1_pstp", 512)
            tpo = ps_tp[:].bitcast(BF16)[:, 0:128]
            tpi = ps_tp[:].bitcast(BF16)[:, 0:E1]
            p.op("sync", lambda e: e.dma_start(out=gwf[0][:], in_=ins["L1_ffn_gate_w"]),
                 writes=(gwf[1],), inc=16)
            p.op("vector", lambda e: e.memset(gwb[0][:], 0.0), writes=(gwb[1],))
            p.op("vector", lambda e: e.tensor_copy(out=gwb[0][0:E1, :], in_=gwf[0][:]),
                 reads=(gwf[1],), writes=(gwb[1],))
            for dc in range(DC):
                p.op("tensor", lambda e, dc=dc, a=tpo: e.transpose(
                    a, gwb[0][:, ts2(dc)], C["ident_bf"][0][:]),
                    reads=(gwb[1], C["ident_bf"][1]), writes=(ps_tp_b,))
                p.op("vector", lambda e, dc=dc, a=tpi: e.tensor_copy(out=gwT[:, dc, :], in_=a),
                     reads=(ps_tp_b,), writes=(gwT_b,))
            p.barrier()
        for s in range(SEQ):
            _emit_l1f_pre(p, nc, ins, C, s, gwT, gwT_b, x3_sc, x3_sc_b, xn3_sc, xn3_sc_b,
                          cin_sc, cin_sc_b, gidx_sc, gidx_sc_b)
        _emit_l1f_experts(p, nc, ins, C, cin_sc, cin_sc_b, cout_sc, cout_sc_b, S["wtp"])
        for s in range(SEQ):
            _emit_l1f_final(p, nc, C, s, x3_sc, x3_sc_b, cout_sc, cout_sc_b,
                            gidx_sc, gidx_sc_b, out_d, out_b)
        p.barrier()


def _emit_l1f_pre(p, nc, ins, C, s, gwT, gwT_b, x3_sc, x3_sc_b, xn3_sc, xn3_sc_b,
                  cin_sc, cin_sc_b, gidx_sc, gidx_sc_b):
    with contextlib.ExitStack() as st_:
        xn_fm = st_.enter_context(nc.sbuf_tensor(f"f1s{s}_xnfm", [128, DC, T], BF16))
        xn_fm_b = Buf("xn_fm")
        with contextlib.ExitStack() as ls_:
            lnst = alloc_ln_scratch(p, ls_, pfx=f"f1s{s}_")
            lnst["ident_bf"] = C["ident_bf"][0]
            lnst["ident_buf"] = C["ident_bf"][1]
            lnst["xsrc_buf"] = x3_sc_b
            emit_ln_to_fm(p, lnst, x3_sc[s], xn_fm, xn_fm_b, TT,
                          xn_tm_sink=lambda tt: (xn3_sc[s, ts2(tt), :], xn3_sc_b))
            p.barrier()
        ps_a, ps_a_b = psum_f32(st_, nc, f"f1s{s}_psa", 1024)
        ps_tp, ps_tp_b = psum_f32(st_, nc, f"f1s{s}_pstp", 512)
        lg = (st_.enter_context(nc.sbuf_tensor(f"f1s{s}_lg", [128, T], F32)), Buf())
        aff = (st_.enter_context(nc.sbuf_tensor(f"f1s{s}_aff", [128, T], F32)), Buf())
        at = (st_.enter_context(nc.sbuf_tensor(f"f1s{s}_at", [128, E1], F32)), Buf())
        rmx = (st_.enter_context(nc.sbuf_tensor(f"f1s{s}_rmx", [128, 1], F32)), Buf())
        p.op("vector", lambda e: e.memset(lg[0][:], 0.0), writes=(lg[1],))
        p.op("vector", lambda e: e.memset(aff[0][:], 0.0), writes=(aff[1],))
        for th in range(2):
            p.group("tensor", [
                (lambda e, dc=dc, tq=tq, th=th: e.matmul(
                    ps_a[0:E1, tq * 512:(tq + 1) * 512], gwT[:, dc, :],
                    xn_fm[:, dc, th * 1024 + tq * 512: th * 1024 + (tq + 1) * 512],
                    start=(dc == 0), stop=(dc == DC - 1)))
                for tq in range(2) for dc in range(DC)],
                reads=(gwT_b, xn_fm_b), writes=(ps_a_b,))
            p.op("scalar", lambda e, th=th: e.activation(
                out=lg[0][0:E1, th * 1024:(th + 1) * 1024], in_=ps_a[0:E1, :], func=AF.Identity),
                reads=(ps_a_b,), writes=(lg[1],))
        for tt in range(TT):
            p.op("tensor", lambda e, tt=tt: e.transpose(
                ps_tp[:, 0:128], lg[0][:, ts2(tt)], C["ident_f32"][0][:]),
                reads=(lg[1], C["ident_f32"][1]), writes=(ps_tp_b,))
            p.op("vector", lambda e: e.reduce_max(out=rmx[0][:], in_=ps_tp[:, 0:E1], axis=AX.X),
                 reads=(ps_tp_b,), writes=(rmx[1],))
            p.op("vector", lambda e: e.tensor_scalar(out=at[0][:], in0=ps_tp[:, 0:E1],
                                                     scalar1=rmx[0][:, 0:1], scalar2=None,
                                                     op0=ALU.subtract),
                 reads=(ps_tp_b, rmx[1]), writes=(at[1],))
            p.op("scalar", lambda e: e.activation(out=at[0][:], in_=at[0][:], func=AF.Exp),
                 reads=(at[1],), writes=(at[1],))
            p.op("vector", lambda e: e.reduce_sum(out=rmx[0][:], in_=at[0][:], axis=AX.X),
                 reads=(at[1],), writes=(rmx[1],))
            p.op("vector", lambda e: e.reciprocal(out=rmx[0][:], in_=rmx[0][:]),
                 reads=(rmx[1],), writes=(rmx[1],))
            p.op("vector", lambda e: e.tensor_scalar(out=at[0][:], in0=at[0][:],
                                                     scalar1=rmx[0][:, 0:1], scalar2=None,
                                                     op0=ALU.mult),
                 reads=(at[1], rmx[1]), writes=(at[1],))
            p.op("tensor", lambda e, tt=tt: e.transpose(
                ps_tp[0:E1, 0:128], at[0][:], C["ident_f32"][0][:]),
                reads=(at[1], C["ident_f32"][1]), writes=(ps_tp_b,))
            p.op("scalar", lambda e, tt=tt: e.activation(
                out=aff[0][0:E1, ts2(tt)], in_=ps_tp[0:E1, 0:128], func=AF.Identity),
                reads=(ps_tp_b,), writes=(aff[1],))
        lo = (st_.enter_context(nc.sbuf_tensor(f"f1s{s}_lo", [E1, 1], F32)), Buf())
        hi = (st_.enter_context(nc.sbuf_tensor(f"f1s{s}_hi", [E1, 1], F32)), Buf())
        mid = (st_.enter_context(nc.sbuf_tensor(f"f1s{s}_mid", [E1, 1], F32)), Buf())
        cnt = (st_.enter_context(nc.sbuf_tensor(f"f1s{s}_cnt", [E1, 1], F32)), Buf())
        sel = (st_.enter_context(nc.sbuf_tensor(f"f1s{s}_sel", [E1, 1], F32)), Buf())
        tmp = (st_.enter_context(nc.sbuf_tensor(f"f1s{s}_tmp", [E1, 1], F32)), Buf())
        cbuf = (st_.enter_context(nc.sbuf_tensor(f"f1s{s}_cbuf", [E1, T], F32)), Buf())
        p.op("vector", lambda e: e.memset(lo[0][:], 0.0), writes=(lo[1],))
        p.op("vector", lambda e: e.memset(hi[0][:], 1.0), writes=(hi[1],))
        for it in range(44):
            p.op("vector", lambda e: e.tensor_tensor(out=mid[0][:], in0=lo[0][:], in1=hi[0][:],
                                                     op=ALU.add),
                 reads=(lo[1], hi[1]), writes=(mid[1],))
            p.op("vector", lambda e: e.tensor_scalar(out=mid[0][:], in0=mid[0][:], scalar1=0.5,
                                                     scalar2=None, op0=ALU.mult),
                 reads=(mid[1],), writes=(mid[1],))
            p.op("vector", lambda e: e.tensor_scalar(
                out=cbuf[0][:], in0=aff[0][0:E1, :], scalar1=mid[0][:, 0:1], scalar2=0.0,
                op0=ALU.is_gt, op1=ALU.add, accum_out=cnt[0][:]),
                reads=(aff[1], mid[1]), writes=(cbuf[1], cnt[1]))
            p.op("vector", lambda e: e.tensor_scalar(out=sel[0][:], in0=cnt[0][:], scalar1=511.5,
                                                     scalar2=None, op0=ALU.is_gt),
                 reads=(cnt[1],), writes=(sel[1],))
            p.op("vector", lambda e: e.tensor_tensor(out=tmp[0][:], in0=mid[0][:], in1=lo[0][:],
                                                     op=ALU.subtract),
                 reads=(mid[1], lo[1]), writes=(tmp[1],))
            p.op("vector", lambda e: e.tensor_tensor(out=tmp[0][:], in0=tmp[0][:], in1=sel[0][:],
                                                     op=ALU.mult),
                 reads=(tmp[1], sel[1]), writes=(tmp[1],))
            p.op("vector", lambda e: e.tensor_tensor(out=lo[0][:], in0=lo[0][:], in1=tmp[0][:],
                                                     op=ALU.add),
                 reads=(lo[1], tmp[1]), writes=(lo[1],))
            p.op("vector", lambda e: e.tensor_tensor(out=tmp[0][:], in0=mid[0][:], in1=hi[0][:],
                                                     op=ALU.subtract),
                 reads=(mid[1], hi[1]), writes=(tmp[1],))
            p.op("vector", lambda e: e.tensor_scalar(out=sel[0][:], in0=sel[0][:], scalar1=-1.0,
                                                     scalar2=1.0, op0=ALU.mult, op1=ALU.add),
                 reads=(sel[1],), writes=(sel[1],))
            p.op("vector", lambda e: e.tensor_tensor(out=tmp[0][:], in0=tmp[0][:], in1=sel[0][:],
                                                     op=ALU.mult),
                 reads=(tmp[1], sel[1]), writes=(tmp[1],))
            p.op("vector", lambda e: e.tensor_tensor(out=hi[0][:], in0=hi[0][:], in1=tmp[0][:],
                                                     op=ALU.add),
                 reads=(hi[1], tmp[1]), writes=(hi[1],))
        msk = (st_.enter_context(nc.sbuf_tensor(f"f1s{s}_msk", [E1, T], F32)), Buf())
        gat = (st_.enter_context(nc.sbuf_tensor(f"f1s{s}_gat", [128, T], F32)), Buf())
        rnk = (st_.enter_context(nc.sbuf_tensor(f"f1s{s}_rnk", [E1, T], F32)), Buf())
        isc = (st_.enter_context(nc.sbuf_tensor(f"f1s{s}_isc", [128, T], F32)), Buf())
        ig = (st_.enter_context(nc.sbuf_tensor(f"f1s{s}_ig", [128, T], F32)), Buf())
        onec = (st_.enter_context(nc.sbuf_tensor(f"f1s{s}_onec", [E1, 1], F32)), Buf())
        p.op("vector", lambda e: e.memset(onec[0][:], 1.0), writes=(onec[1],))
        p.op("vector", lambda e: e.memset(gat[0][:], 0.0), writes=(gat[1],))
        p.op("vector", lambda e: e.memset(isc[0][:], 0.0), writes=(isc[1],))
        p.op("vector", lambda e: e.memset(ig[0][:], 0.0), writes=(ig[1],))
        p.op("vector", lambda e: e.tensor_scalar(out=msk[0][:], in0=aff[0][0:E1, :],
                                                 scalar1=lo[0][:, 0:1], scalar2=None, op0=ALU.is_gt),
             reads=(aff[1], lo[1]), writes=(msk[1],))
        p.op("vector", lambda e: e.tensor_tensor(out=gat[0][0:E1, :], in0=aff[0][0:E1, :],
                                                 in1=msk[0][:], op=ALU.mult),
             reads=(aff[1], msk[1]), writes=(gat[1],))
        p.op("vector", lambda e: e.tensor_tensor_scan(
            out=rnk[0][:], data0=onec[0][:, 0:1].broadcast_to((E1, T)), data1=msk[0][:],
            initial=0.0, op0=ALU.mult, op1=ALU.add),
            reads=(msk[1], onec[1]), writes=(rnk[1],))
        p.op("vector", lambda e: e.tensor_tensor(out=rnk[0][:], in0=rnk[0][:], in1=msk[0][:],
                                                 op=ALU.subtract),
             reads=(rnk[1], msk[1]), writes=(rnk[1],))
        p.op("vector", lambda e: e.tensor_tensor(out=ig[0][0:E1, :], in0=rnk[0][:], in1=msk[0][:],
                                                 op=ALU.mult),
             reads=(rnk[1], msk[1]), writes=(ig[1],))
        p.op("vector", lambda e: e.tensor_scalar(out=msk[0][:], in0=msk[0][:], scalar1=-4096.0,
                                                 scalar2=4096.0, op0=ALU.mult, op1=ALU.add),
             reads=(msk[1],), writes=(msk[1],))
        p.op("vector", lambda e: e.tensor_tensor(out=isc[0][0:E1, :], in0=rnk[0][:], in1=msk[0][:],
                                                 op=ALU.add),
             reads=(rnk[1], msk[1]), writes=(isc[1],))
        p.op("sync", lambda e: e.dma_start(out=gidx_sc[s, 0], in_=isc[0][0:E1, :]),
             reads=(isc[1],), writes=(gidx_sc_b,), inc=16)
        p.op("sync", lambda e: e.dma_start(out=gidx_sc[s, 1], in_=gat[0][0:E1, :]),
             reads=(gat[1],), writes=(gidx_sc_b,), inc=16)
        isct = (st_.enter_context(nc.sbuf_tensor(f"f1s{s}_isct", [128, TT, E1], F32)), Buf())
        for tt in range(TT):
            p.op("tensor", lambda e, tt=tt: e.transpose(
                ps_tp[:, 0:128], isc[0][:, ts2(tt)], C["ident_f32"][0][:]),
                reads=(isc[1], C["ident_f32"][1]), writes=(ps_tp_b,))
            p.op("vector", lambda e, tt=tt: e.tensor_copy(
                out=isct[0][:, tt, :], in_=ps_tp[:, 0:E1]),
                reads=(ps_tp_b,), writes=(isct[1],))
        xna = (st_.enter_context(nc.sbuf_tensor(f"f1s{s}_xna", [128, TT, D], BF16)), Buf())
        for tt in range(TT):
            p.op("sync", lambda e, tt=tt: e.dma_start(
                out=xna[0][:, tt, :], in_=xn3_sc[s, ts2(tt), :]),
                reads=(xn3_sc_b,), writes=(xna[1],), inc=16)
        shc = (st_.enter_context(nc.sbuf_tensor(f"f1s{s}_shc", [128, 1], F32)), Buf())
        pj = (st_.enter_context(nc.sbuf_tensor(f"f1s{s}_pj", [128, 128], BF16)), Buf())
        cinb = (st_.enter_context(nc.sbuf_tensor(f"f1s{s}_cinb", [128, D], BF16)), Buf())
        ps_cp, ps_cp_b = psum_f32(st_, nc, f"f1s{s}_pscp", 1024)
        for ee in range(E1):
            for jt in range(4):
                for tt in range(TT):
                    p.op("vector", lambda e, ee=ee, jt=jt, tt=tt: e.tensor_scalar(
                        out=shc[0][:], in0=isct[0][:, tt, ee:ee + 1], scalar1=float(-128 * jt),
                        scalar2=None, op0=ALU.add),
                        reads=(isct[1],), writes=(shc[1],))
                    p.op("vector", lambda e: e.tensor_scalar(
                        out=pj[0][:], in0=C["iota_row_bc"][0][:], scalar1=shc[0][:, 0:1],
                        scalar2=None, op0=ALU.is_equal),
                        reads=(shc[1], C["iota_row_bc"][1]), writes=(pj[1],))
                    p.group("tensor", [
                        (lambda e, tt=tt, nn=nn: e.matmul(
                            ps_cp[:, nn * 512:(nn + 1) * 512], pj[0][:],
                            xna[0][:, tt, nn * 512:(nn + 1) * 512],
                            start=(tt == 0), stop=(tt == TT - 1)))
                        for nn in range(2)],
                        reads=(pj[1], xna[1]), writes=(ps_cp_b,))
                p.op("scalar", lambda e: e.activation(out=cinb[0][:], in_=ps_cp[:], func=AF.Identity),
                     reads=(ps_cp_b,), writes=(cinb[1],))
                p.op("sync", lambda e, ee=ee, jt=jt: e.dma_start(
                    out=cin_sc[s][ee][ts2(jt), :], in_=cinb[0][:]),
                    reads=(cinb[1],), writes=(cin_sc_b[s][ee],), inc=16)
        p.barrier()


def _emit_l1f_experts(p, nc, ins, C, cin_sc, cin_sc_b, cout_sc, cout_sc_b, wtp):
    (gsc, gsc_b), (usc, usc_b), (dsc, dsc_b) = wtp
    IC = I1 // 128
    with contextlib.ExitStack() as st_:
        cfm = st_.enter_context(nc.sbuf_tensor("f1e_cfm", [128, DC, T], BF16))
        cfm_hb = [Buf("cfm_h0"), Buf("cfm_h1")]
        cts = [(st_.enter_context(nc.sbuf_tensor("f1e_cts0", [128, 4, D], BF16)), Buf())]
        abuf = st_.enter_context(nc.sbuf_tensor("f1e_abuf", [128, IC, 1024], BF16))
        abuf_b = Buf("abuf")
        wgT2 = [(st_.enter_context(nc.sbuf_tensor(f"f1e_wgT{i}", [128, DC, 128], BF16)), Buf())
                for i in range(2)]
        wuT2 = [(st_.enter_context(nc.sbuf_tensor(f"f1e_wuT{i}", [128, DC, 128], BF16)), Buf())
                for i in range(2)]
        sg = (st_.enter_context(nc.sbuf_tensor("f1e_sg", [128, 1024], F32)), Buf())
        wdT2 = [(st_.enter_context(nc.sbuf_tensor(f"f1e_wdT{i}", [128, IC, 128], BF16)), Buf())
                for i in range(2)]
        osb = (st_.enter_context(nc.sbuf_tensor("f1e_osb", [128, 1024], F32)), Buf())
        cot = (st_.enter_context(nc.sbuf_tensor("f1e_cot", [128, 8, D], BF16)), Buf())
        ps_g, ps_g_b = psum_f32(st_, nc, "f1e_psg", 1024)
        ps_u, ps_u_b = psum_f32(st_, nc, "f1e_psu", 1024)
        ps_tp = [psum_f32(st_, nc, f"f1e_pstp{i}", 512) for i in range(2)]
        tctr = [0]

        def tp128(src_ap, src_bufs, dst_ap, dst_buf, bf16=True):
            pt, pt_b = ps_tp[tctr[0] % 2]
            tctr[0] += 1
            a = pt[:].bitcast(BF16)[:, 0:128] if bf16 else pt[:, 0:128]
            idn = C["ident_bf"] if bf16 else C["ident_f32"]
            p.op("tensor", lambda e, a=a, src_ap=src_ap, idn=idn: e.transpose(a, src_ap, idn[0][:]),
                 reads=tuple(src_bufs) + (idn[1],), writes=(pt_b,))
            p.op("vector", lambda e, a=a, dst_ap=dst_ap: e.tensor_copy(out=dst_ap, in_=a),
                 reads=(pt_b,), writes=(dst_buf,))

        def gather(ee, half):
            for s in (2 * half, 2 * half + 1):
                ct, ct_b = cts[0]
                p.op("gpsimd", lambda e, s=s, ee=ee, ct=ct: e.dma_start(
                    out=ct[:], in_=cin_sc[s][ee].rearrange("(b p) d -> p b d", p=128)),
                    reads=(cin_sc_b[s][ee], ct_b), writes=(ct_b,), inc=16)
                for blk in range(4):
                    tb = s * 4 + blk
                    for dc in range(DC):
                        tp128(ct[:, blk, ts2(dc)], (ct_b,), cfm[:, dc, ts2(tb)], cfm_hb[half])

        def load_gu(ee, ic):
            i = ic % 2
            for (t, tb), sc, scb in ((wgT2[i], gsc, gsc_b), (wuT2[i], usc, usc_b)):
                p.op("sync", lambda e, t=t, sc=sc, ic=ic, ee=ee: e.dma_start(
                    out=t[:].rearrange("p c d -> p (c d)"),
                    in_=sc[ee, ic].rearrange("p c d -> p (c d)")),
                    reads=(scb, tb), writes=(tb,), inc=16)

        gather(0, 0)
        gather(0, 1)
        for ee in range(E1):
            for kh in range(2):
                load_gu(ee, 0)
                load_gu(ee, 1)
                for ic in range(IC):
                    i = ic % 2
                    for (wt, wtb), pg, pg_b in ((wgT2[i], ps_g, ps_g_b), (wuT2[i], ps_u, ps_u_b)):
                        p.group("tensor", [
                            (lambda e, dc=dc, tq=tq, kh=kh, wt=wt, pg=pg: e.matmul(
                                pg[:, tq * 512:(tq + 1) * 512], wt[:, dc, :],
                                cfm[:, dc, kh * 1024 + tq * 512: kh * 1024 + (tq + 1) * 512],
                                start=(dc == 0), stop=(dc == DC - 1)))
                            for tq in range(2) for dc in range(DC)],
                            reads=(wtb, cfm_hb[kh]), writes=(pg_b,))
                    p.op("scalar", lambda e: e.activation(out=sg[0][:], in_=ps_g[:], func=AF.Silu),
                         reads=(ps_g_b,), writes=(sg[1],))
                    p.op("vector", lambda e, ic=ic: e.tensor_tensor(
                        out=abuf[:, ic, :], in0=ps_u[:], in1=sg[0][:], op=ALU.mult),
                        reads=(ps_u_b, sg[1]), writes=(abuf_b,))
                    if ic + 2 < IC:
                        load_gu(ee, ic + 2)
                if ee + 1 < E1:
                    gather(ee + 1, kh)
                def load_wd(dc):
                    t, tb = wdT2[dc % 2]
                    p.op("sync", lambda e, dc=dc, ee=ee, t=t: e.dma_start(
                        out=t[:], in_=dsc[ee, dc].rearrange("c p d -> p c d")),
                        reads=(dsc_b, tb), writes=(tb,), inc=16)

                load_wd(0)
                for dc in range(DC):
                    wdT, wdT_b = wdT2[dc % 2]
                    if dc + 1 < DC:
                        load_wd(dc + 1)
                    p.group("tensor", [
                        (lambda e, ic=ic, tq=tq, wdT=wdT: e.matmul(
                            ps_g[:, tq * 512:(tq + 1) * 512], wdT[:, ic, :],
                            abuf[:, ic, tq * 512:(tq + 1) * 512],
                            start=(ic == 0), stop=(ic == IC - 1)))
                        for tq in range(2) for ic in range(IC)],
                        reads=(wdT_b, abuf_b), writes=(ps_g_b,))
                    p.op("scalar", lambda e: e.activation(out=osb[0][:], in_=ps_g[:],
                                                          func=AF.Identity),
                         reads=(ps_g_b,), writes=(osb[1],))
                    for blk in range(8):
                        tp128(osb[0][:, ts2(blk)], (osb[1],), cot[0][:, blk, ts2(dc)], cot[1],
                              bf16=False)
                for half in range(2):
                    s_out = kh * 2 + half
                    p.op("gpsimd", lambda e, s_out=s_out, half=half, ee=ee: e.dma_start(
                        out=cout_sc[s_out][ee].rearrange("(b p) d -> p b d", p=128),
                        in_=cot[0][:, half * 4:(half + 1) * 4, :]),
                        reads=(cot[1],), writes=(cout_sc_b[s_out][ee],), inc=16)
        p.barrier()

def _emit_l1f_final(p, nc, C, s, x3_sc, x3_sc_b, cout_sc, cout_sc_b,
                    gidx_sc, gidx_sc_b, out_d, out_b):
    with contextlib.ExitStack() as st_:
        acc = st_.enter_context(nc.sbuf_tensor(f"f1fs{s}_acc", [128, TT, D], F32))
        acc_b = Buf("acc")
        irow = (st_.enter_context(nc.sbuf_tensor(f"f1fs{s}_irow", [1, T], F32)), Buf())
        irh = (st_.enter_context(nc.sbuf_tensor(f"f1fs{s}_irh", [1, T], BF16)), Buf())
        irl = (st_.enter_context(nc.sbuf_tensor(f"f1fs{s}_irl", [1, T], F32)), Buf())
        irl2 = (st_.enter_context(nc.sbuf_tensor(f"f1fs{s}_irl2", [1, T], BF16)), Buf())
        grow = (st_.enter_context(nc.sbuf_tensor(f"f1fs{s}_grow", [1, T], BF16)), Buf())
        iscb = (st_.enter_context(nc.sbuf_tensor(f"f1fs{s}_iscb", [128, T], F32)), Buf())
        gatb = (st_.enter_context(nc.sbuf_tensor(f"f1fs{s}_gatb", [128, T], BF16)), Buf())
        shc = (st_.enter_context(nc.sbuf_tensor(f"f1fs{s}_shc", [128, 1], F32)), Buf())
        eq = (st_.enter_context(nc.sbuf_tensor(f"f1fs{s}_eq", [128, T], BF16)), Buf())
        Pg = [(st_.enter_context(nc.sbuf_tensor(f"f1fs{s}_Pg{j}", [128, T], BF16)), Buf())
              for j in range(4)]
        cob = (st_.enter_context(nc.sbuf_tensor(f"f1fs{s}_cob", [128, 4, D], BF16)), Buf())
        ps_b, ps_b_b = psum_f32(st_, nc, f"f1fs{s}_psb", T)
        ps_x, ps_x_b = psum_f32(st_, nc, f"f1fs{s}_psx", 1024)
        for tt in range(TT):
            p.op("sync", lambda e, tt=tt: e.dma_start(out=acc[:, tt, :], in_=x3_sc[s, ts2(tt), :]),
                 reads=(x3_sc_b,), writes=(acc_b,), inc=16)
        for ee in range(E1):
            p.op("sync", lambda e, ee=ee: e.dma_start(out=irow[0][:], in_=gidx_sc[s, 0, ee].unsqueeze(0)),
                 reads=(gidx_sc_b, irow[1]), writes=(irow[1],), inc=16)
            p.op("vector", lambda e: e.tensor_copy(out=irh[0][:], in_=irow[0][:]),
                 reads=(irow[1],), writes=(irh[1],))
            p.op("vector", lambda e: e.tensor_tensor(out=irl[0][:], in0=irow[0][:], in1=irh[0][:],
                                                     op=ALU.subtract),
                 reads=(irow[1], irh[1]), writes=(irl[1],))
            p.op("vector", lambda e: e.tensor_copy(out=irl2[0][:], in_=irl[0][:]),
                 reads=(irl[1],), writes=(irl2[1],))
            p.group("tensor", [
                (lambda e, nn=nn: e.matmul(
                    ps_b[:, nn * 512:(nn + 1) * 512], C["ones_row_bf"][0][:],
                    irh[0][0:1, nn * 512:(nn + 1) * 512], start=True, stop=False))
                for nn in range(4)] + [
                (lambda e, nn=nn: e.matmul(
                    ps_b[:, nn * 512:(nn + 1) * 512], C["ones_row_bf"][0][:],
                    irl2[0][0:1, nn * 512:(nn + 1) * 512], start=False, stop=True))
                for nn in range(4)],
                reads=(irh[1], irl2[1], C["ones_row_bf"][1]), writes=(ps_b_b,))
            p.op("vector", lambda e: e.tensor_copy(out=iscb[0][:], in_=ps_b[:]),
                 reads=(ps_b_b,), writes=(iscb[1],))
            p.op("sync", lambda e, ee=ee: e.dma_start(out=irow[0][:], in_=gidx_sc[s, 1, ee].unsqueeze(0)),
                 reads=(gidx_sc_b, irow[1]), writes=(irow[1],), inc=16)
            p.op("vector", lambda e: e.tensor_copy(out=grow[0][:], in_=irow[0][:]),
                 reads=(irow[1],), writes=(grow[1],))
            p.group("tensor", [
                (lambda e, nn=nn: e.matmul(
                    ps_b[:, nn * 512:(nn + 1) * 512], C["ones_row_bf"][0][:],
                    grow[0][0:1, nn * 512:(nn + 1) * 512], start=True, stop=True))
                for nn in range(4)],
                reads=(grow[1], C["ones_row_bf"][1]), writes=(ps_b_b,))
            p.op("scalar", lambda e: e.activation(out=gatb[0][:], in_=ps_b[:], func=AF.Identity),
                 reads=(ps_b_b,), writes=(gatb[1],))
            for jt in range(4):
                p.op("vector", lambda e, jt=jt: e.tensor_scalar(
                    out=shc[0][:], in0=C["iota_pcol"][0][:], scalar1=float(128 * jt),
                    scalar2=None, op0=ALU.add),
                    reads=(C["iota_pcol"][1],), writes=(shc[1],))
                p.op("vector", lambda e, jt=jt: e.tensor_scalar(
                    out=eq[0][:], in0=iscb[0][:], scalar1=shc[0][:, 0:1], scalar2=None,
                    op0=ALU.is_equal),
                    reads=(iscb[1], shc[1]), writes=(eq[1],))
                p.op("vector", lambda e, jt=jt: e.tensor_tensor(
                    out=Pg[jt][0][:], in0=eq[0][:], in1=gatb[0][:], op=ALU.mult),
                    reads=(eq[1], gatb[1]), writes=(Pg[jt][1],))
            p.op("sync", lambda e, ee=ee: e.dma_start(
                out=cob[0][:], in_=cout_sc[s][ee].rearrange("(b p) d -> p b d", p=128)),
                reads=(cout_sc_b[s][ee], cob[1]), writes=(cob[1],), inc=16)
            for tt in range(TT):
                p.group("tensor", [
                    (lambda e, tt=tt, jt=jt, nn=nn: e.matmul(
                        ps_x[:, nn * 512:(nn + 1) * 512], Pg[jt][0][:, ts2(tt)],
                        cob[0][:, jt, nn * 512:(nn + 1) * 512],
                        start=(jt == 0), stop=(jt == 3)))
                    for jt in range(4) for nn in range(2)],
                    reads=(Pg[0][1], Pg[1][1], Pg[2][1], Pg[3][1], cob[1]), writes=(ps_x_b,))
                p.op("vector", lambda e, tt=tt: e.tensor_tensor(
                    out=acc[:, tt, :], in0=acc[:, tt, :], in1=ps_x[:], op=ALU.add),
                    reads=(ps_x_b, acc_b), writes=(acc_b,))
        for tt in range(TT):
            p.op("sync", lambda e, tt=tt: e.dma_start(out=out_d[s, ts2(tt), :], in_=acc[:, tt, :]),
                 reads=(acc_b,), writes=(out_b,), inc=16)
        p.barrier()


_NC_CACHE = {}


def kernel(**inputs):
    if "nc" not in _NC_CACHE:
        _NC_CACHE["nc"] = build_core(stop_after="ALL")
    nc = _NC_CACHE["nc"]
    base = {}
    for name, shape in INPUT_SPECS:
        if name == "L0_attn_x":
            continue
        a = np.ascontiguousarray(np.asarray(inputs[name], dtype=np.float32).reshape(shape))
        base[name] = a
    x = np.ascontiguousarray(np.asarray(inputs["L0_attn_x"], dtype=np.float32))
    in_maps = []
    for c in range(NCORES):
        m = dict(base)
        m["L0_attn_x"] = np.ascontiguousarray(x[c * SEQ:(c + 1) * SEQ])
        in_maps.append(m)
    res = run_bass_kernel_spmd(nc, in_maps, core_ids=list(range(NCORES)))
    return np.concatenate([res.results[c]["out"] for c in range(NCORES)], axis=0)


# version 13 (final)
# speedup vs baseline: 1.0560x; 1.0560x over previous
import contextlib
import numpy as np
import ml_dtypes
import concourse.bass as bass
import concourse.mybir as mybir
from concourse.bass import IndirectOffsetOnAxis
from concourse.bass_utils import run_bass_kernel_spmd

F32 = mybir.dt.float32
BF16 = mybir.dt.bfloat16
I32 = mybir.dt.int32
U8 = mybir.dt.uint8
AF = mybir.ActivationFunctionType
ALU = mybir.AluOpType
AX = mybir.AxisListType

NCORES = 8
SEQ = 4
T = 2048
D = 1024
HD = 4096
NST = 16
NH = 16
HDIM = 64
E0, H0 = 16, 512
E1, I1 = 8, 4096
K1 = 512
TT = T // 128
DC = D // 128
HT = HD // 128

TWO_PI = float(2.0 * np.pi)
INV_2PI = float(1.0 / (2.0 * np.pi))
MAGIC = float(1.5 * 2**23)


class Buf:
    __slots__ = ("last_w", "readers", "name", "excl")

    def __init__(self, name="", excl=False):
        self.last_w = None
        self.readers = {}
        self.name = name
        self.excl = excl


class Prog:
    ENGS = ("sync", "vector", "scalar", "tensor", "gpsimd")
    PARANOID = False

    def __init__(self, nc, stack):
        self.nc = nc
        self.stack = stack
        self.sem = {e: stack.enter_context(nc.semaphore(f"prog_{e}")) for e in self.ENGS}
        self.cnt = {e: 0 for e in self.ENGS}
        self.waited = {e: {} for e in self.ENGS}
        self.queues = {e: [] for e in self.ENGS}

    def _wait(self, eng, e2, c):
        if c and self.waited[eng].get(e2, 0) < c:
            self.queues[eng].append(("wait", self.sem[e2], c))
            self.waited[eng][e2] = c

    def _deps(self, eng, reads, writes):
        need = {}
        for b in reads:
            if b.last_w:
                e2, c = b.last_w
                if need.get(e2, 0) < c:
                    need[e2] = c
            if b.excl:
                for e2, c in b.readers.items():
                    if e2 != eng and need.get(e2, 0) < c:
                        need[e2] = c
        for b in writes:
            if b.last_w:
                e2, c = b.last_w
                if need.get(e2, 0) < c:
                    need[e2] = c
            for e2, c in b.readers.items():
                if need.get(e2, 0) < c:
                    need[e2] = c
        for e2 in self.ENGS:
            if e2 in need:
                self._wait(eng, e2, need[e2])

    def _post(self, eng, reads, writes, inc):
        self.cnt[eng] += inc
        c = self.cnt[eng]
        for b in writes:
            b.last_w = (eng, c)
            b.readers = {}
        for b in reads:
            b.readers[eng] = c

    def op(self, eng, fn, reads=(), writes=(), inc=1):
        if self.PARANOID:
            self.barrier()
        self._deps(eng, reads, writes)
        self.queues[eng].append(("op", fn, self.sem[eng], inc))
        self._post(eng, reads, writes, inc)

    def group(self, eng, fns, reads=(), writes=(), inc=1):
        if self.PARANOID:
            self.barrier()
        self._deps(eng, reads, writes)
        self.queues[eng].append(("grp", fns, self.sem[eng], inc))
        self._post(eng, reads, writes, inc)

    def barrier(self):
        for e in self.ENGS:
            for e2 in self.ENGS:
                if e2 != e:
                    self._wait(e, e2, self.cnt[e2])

    def replay(self):
        nc = self.nc
        for e in self.ENGS:
            if self.cnt[e]:
                self.queues[e].append(("wait", self.sem[e], self.cnt[e]))

        def runner(eng_name):
            def run(eng_obj):
                for item in self.queues[eng_name]:
                    if item[0] == "wait":
                        eng_obj.wait_ge(item[1], item[2])
                    elif item[0] == "op":
                        item[1](eng_obj).then_inc(item[2], item[3])
                    else:
                        inst = None
                        for fn in item[1]:
                            inst = fn(eng_obj)
                        inst.then_inc(item[2], item[3])
            return run

        with nc.Block() as block:
            block.sync(runner("sync"))
            block.vector(runner("vector"))
            block.scalar(runner("scalar"))
            block.tensor(runner("tensor"))
            block.gpsimd(runner("gpsimd"))


def ts2(i, n=128):
    return slice(i * n, (i + 1) * n)



def emit_ln_to_fm(p, st, x_dram_seq, xn_fm, xn_fm_buf, tiles, xn_tm_sink=None,
                  nsq_dst=None, nsq_dst_buf=None):
    nc = p.nc
    for tt in range(tiles):
        xt, xt_b = st["xt"][tt % 2]
        p.op("sync", lambda e, xt=xt, tt=tt: e.dma_start(out=xt[:], in_=x_dram_seq[ts2(tt), :]),
             reads=(st["xsrc_buf"],), writes=(xt_b,), inc=16)
        ssum, ssum_b = st["ssum"]
        ssq, ssq_b = st["ssq"]
        sq, sq_b = st["sq"]
        p.op("vector", lambda e, xt=xt: e.reduce_sum(out=ssum[:], in_=xt[:], axis=AX.X),
             reads=(xt_b,), writes=(ssum_b,))
        p.op("scalar", lambda e, xt=xt: e.activation(out=sq[:], in_=xt[:], func=AF.Square,
                                                     accum_out=ssq[:]),
             reads=(xt_b,), writes=(sq_b, ssq_b))
        mean, mean_b = st["mean"]
        rstd, rstd_b = st["rstd"]
        tmp1, tmp1_b = st["tmp1"]
        p.op("vector", lambda e: e.tensor_scalar(out=mean[:], in0=ssum[:], scalar1=1.0 / D,
                                                 scalar2=None, op0=ALU.mult),
             reads=(ssum_b,), writes=(mean_b,))
        p.op("vector", lambda e: e.tensor_tensor(out=tmp1[:], in0=mean[:], in1=mean[:], op=ALU.mult),
             reads=(mean_b,), writes=(tmp1_b,))
        p.op("vector", lambda e: e.tensor_scalar(out=tmp1[:], in0=tmp1[:], scalar1=-1.0,
                                                 scalar2=1e-5, op0=ALU.mult, op1=ALU.add),
             reads=(tmp1_b,), writes=(tmp1_b,))
        p.op("vector", lambda e: e.scalar_tensor_tensor(out=tmp1[:], in0=ssq[:], scalar=1.0 / D,
                                                        in1=tmp1[:], op0=ALU.mult, op1=ALU.add),
             reads=(ssq_b, tmp1_b), writes=(tmp1_b,))
        p.op("scalar", lambda e: e.activation(out=tmp1[:], in_=tmp1[:], func=AF.Ln),
             reads=(tmp1_b,), writes=(tmp1_b,))
        p.op("scalar", lambda e: e.activation(out=rstd[:], in_=tmp1[:], func=AF.Exp, scale=-0.5),
             reads=(tmp1_b,), writes=(rstd_b,))
        xn, xn_b = st["xn_tm"][tt % 2]
        p.op("vector", lambda e, xt=xt: e.tensor_scalar(
            out=sq[:], in0=xt[:], scalar1=mean[:, 0:1], scalar2=None, op0=ALU.subtract),
            reads=(xt_b, mean_b), writes=(sq_b,))
        p.op("vector", lambda e, xn=xn: e.tensor_scalar(
            out=xn[:], in0=sq[:], scalar1=rstd[:, 0:1], scalar2=None, op0=ALU.mult),
            reads=(sq_b, rstd_b), writes=(xn_b,))
        if nsq_dst is not None:
            p.op("scalar", lambda e, xn=xn, tt=tt: e.activation(
                out=sq[:], in_=xn[:], func=AF.Square,
                accum_out=nsq_dst[:, tt:tt + 1]),
                reads=(xn_b,), writes=(sq_b, nsq_dst_buf))
        if xn_tm_sink is not None:
            dst_ap, dst_buf = xn_tm_sink(tt)
            p.op("sync", lambda e, xn=xn, dst_ap=dst_ap: e.dma_start(out=dst_ap, in_=xn[:]),
                 reads=(xn_b,), writes=(dst_buf,), inc=16)
        tp, tp_b = st["tp_psum"]
        idn = st["ident_bf"]
        for dc in range(DC):
            p.op("tensor", lambda e, xn=xn, dc=dc: e.transpose(tp[:, 0:128], xn[:, ts2(dc)], idn[:]),
                 reads=(xn_b, st["ident_buf"]), writes=(tp_b,))
            p.op("vector", lambda e, dc=dc, tt=tt: e.tensor_copy(
                out=xn_fm[:, dc, ts2(tt)], in_=tp[:, 0:128]),
                reads=(tp_b,), writes=(xn_fm_buf,))


def alloc_ln_scratch(p, stack, pfx=""):
    nc = p.nc
    st = {}
    t0_ = stack.enter_context(nc.sbuf_tensor(f"{pfx}xt0", [128, D], F32))
    b0_ = Buf(f"{pfx}xt0")
    st["xt"] = [(t0_, b0_), (t0_, b0_)]
    st["sq"] = (stack.enter_context(nc.sbuf_tensor(f"{pfx}sq", [128, D], F32)), Buf())
    st["ssum"] = (stack.enter_context(nc.sbuf_tensor(f"{pfx}ssum", [128, 1], F32)), Buf())
    st["ssq"] = (stack.enter_context(nc.sbuf_tensor(f"{pfx}ssq", [128, 1], F32)), Buf())
    st["mean"] = (stack.enter_context(nc.sbuf_tensor(f"{pfx}mean", [128, 1], F32)), Buf())
    st["rstd"] = (stack.enter_context(nc.sbuf_tensor(f"{pfx}rstd", [128, 1], F32)), Buf())
    st["tmp1"] = (stack.enter_context(nc.sbuf_tensor(f"{pfx}tmp1", [128, 1], F32)), Buf())
    st["xn_tm"] = []
    for i in range(2):
        t = stack.enter_context(nc.sbuf_tensor(f"{pfx}xntm{i}", [128, D], BF16))
        st["xn_tm"].append((t, Buf(f"{pfx}xntm{i}")))
    st["tp_psum"] = (stack.enter_context(nc.psum_tensor(f"{pfx}tpp", [128, 1024], BF16)), Buf(excl=True))
    return st


def psum_f32(stack, nc, name, cols):
    assert cols % 512 == 0
    return (stack.enter_context(nc.psum_tensor(name, [128, cols], F32)), Buf(name, excl=True))


def emit_setup(p, stack, ins, C, pos_stack):
    nc = p.nc

    def sbuf(name, shape, dtype):
        t = stack.enter_context(nc.sbuf_tensor(name, shape, dtype))
        b = Buf(name)
        C[name] = (t, b)
        return t, b

    ident_bf, ident_bf_b = sbuf("ident_bf", [128, 128], BF16)
    ident_f32, ident_f32_b = sbuf("ident_f32", [128, 128], F32)
    iota_row_bc, iota_row_bc_b = sbuf("iota_row_bc", [128, 128], F32)
    iota_pcol, iota_pcol_b = sbuf("iota_pcol", [128, 1], F32)
    with contextlib.ExitStack() as tst:
        tmp_i = tst.enter_context(nc.sbuf_tensor("setup_tmpi", [128, T], I32))
        tmp_i_b = Buf("setup_tmpi")
        p.op("gpsimd", lambda e: e.iota(out=tmp_i[:, 0:128], pattern=[[1, 128]], base=0,
                                        channel_multiplier=0),
             writes=(tmp_i_b,))
        p.op("vector", lambda e: e.tensor_copy(out=iota_row_bc[:], in_=tmp_i[:, 0:128]),
             reads=(tmp_i_b,), writes=(iota_row_bc_b,))
        p.op("gpsimd", lambda e: e.iota(out=tmp_i[:, 0:1], pattern=[[1, 1]], base=0,
                                        channel_multiplier=1),
             reads=(iota_row_bc_b,), writes=(tmp_i_b,))
        p.op("vector", lambda e: e.tensor_copy(out=iota_pcol[:], in_=tmp_i[:, 0:1]),
             reads=(tmp_i_b,), writes=(iota_pcol_b,))
        p.op("gpsimd", lambda e: e.iota(out=tmp_i[:, 0:128], pattern=[[1, 128]], base=0,
                                        channel_multiplier=-1),
             reads=(iota_pcol_b,), writes=(tmp_i_b,))
        p.op("vector", lambda e: e.tensor_scalar(out=ident_bf[:], in0=tmp_i[:, 0:128], scalar1=0,
                                                 scalar2=None, op0=ALU.is_equal),
             reads=(tmp_i_b,), writes=(ident_bf_b,))
        p.op("vector", lambda e: e.tensor_scalar(out=ident_f32[:], in0=tmp_i[:, 0:128], scalar1=0,
                                                 scalar2=None, op0=ALU.is_equal),
             reads=(tmp_i_b,), writes=(ident_f32_b,))
        p.barrier()

    ones_row_bf, ones_row_bf_b = sbuf("ones_row_bf", [1, 128], BF16)
    p.op("vector", lambda e: e.memset(ones_row_bf[:], 1.0), writes=(ones_row_bf_b,))
    ones_col_bf, ones_col_bf_b = sbuf("ones_col_bf", [128, 1], BF16)
    p.op("vector", lambda e: e.memset(ones_col_bf[:], 1.0), writes=(ones_col_bf_b,))
    ones_f32, ones_f32_b = sbuf("ones_f32", [1, 64], F32)
    p.op("vector", lambda e: e.memset(ones_f32[:], 1.0), writes=(ones_f32_b,))

    dt_raw, dt_raw_b = sbuf("dt_raw", [128, HT], F32)
    phs, phs_b = sbuf("phs", [128, HT], F32)
    for j in range(HT):
        p.op("sync", lambda e, j=j: e.dma_start(out=dt_raw[:, j:j + 1],
                                                in_=ins["L0_attn_dt_proj"][ts2(j)].unsqueeze(1)),
             writes=(dt_raw_b,), inc=16)
        p.op("sync", lambda e, j=j: e.dma_start(out=phs[:, j:j + 1],
                                                in_=ins["L0_attn_floquet_phases"][ts2(j)].unsqueeze(1)),
             writes=(phs_b,), inc=16)
    p.op("vector", lambda e: e.tensor_scalar(out=phs[:], in0=phs[:], scalar1=INV_2PI,
                                             scalar2=None, op0=ALU.mult),
         reads=(phs_b,), writes=(phs_b,))
    dtv, dtv_b = sbuf("dtv", [128, HT], F32)
    t1, t1_b = sbuf("sp_t1", [128, HT], F32)
    t2, t2_b = sbuf("sp_t2", [128, HT], F32)
    p.op("scalar", lambda e: e.activation(out=t1[:], in_=dt_raw[:], func=AF.Abs),
         reads=(dt_raw_b,), writes=(t1_b,))
    p.op("scalar", lambda e: e.activation(out=t1[:], in_=t1[:], func=AF.Exp, scale=-1.0),
         reads=(t1_b,), writes=(t1_b,))
    p.op("scalar", lambda e: e.activation(out=t1[:], in_=t1[:], func=AF.Ln, bias=1.0),
         reads=(t1_b,), writes=(t1_b,))
    p.op("scalar", lambda e: e.activation(out=t2[:], in_=dt_raw[:], func=AF.Relu),
         reads=(dt_raw_b,), writes=(t2_b,))
    p.op("vector", lambda e: e.tensor_tensor(out=dtv[:], in0=t1[:], in1=t2[:], op=ALU.add),
         reads=(t1_b, t2_b), writes=(dtv_b,))

    arow, arow_b = sbuf("arow", [1, NST], F32)
    p.op("sync", lambda e: e.dma_start(out=arow[:], in_=ins["L0_attn_a_log"].unsqueeze(0)),
         writes=(arow_b,), inc=16)
    p.op("scalar", lambda e: e.activation(out=arow[:], in_=arow[:], func=AF.Exp),
         reads=(arow_b,), writes=(arow_b,))
    p.op("vector", lambda e: e.tensor_scalar(out=arow[:], in0=arow[:], scalar1=-1.0,
                                             scalar2=None, op0=ALU.mult),
         reads=(arow_b,), writes=(arow_b,))
    arow_sc = nc.dram_tensor("arow_sc", [1, NST], F32, kind="Internal").ap()
    arow_sc_b = Buf("arow_sc")
    p.op("sync", lambda e: e.dma_start(out=arow_sc, in_=arow[:]),
         reads=(arow_b,), writes=(arow_sc_b,), inc=16)
    abc, abc_b = sbuf("abc", [128, NST], F32)
    p.op("sync", lambda e: e.dma_start(out=abc[:], in_=arow_sc.broadcast_to((128, NST))),
         reads=(arow_sc_b,), writes=(abc_b,), inc=16)

    decay, decay_b = sbuf("decay", [128, HT * NST], F32)
    for j in range(HT):
        p.op("vector", lambda e, j=j: e.tensor_scalar(
            out=decay[:, j * NST:(j + 1) * NST], in0=abc[:], scalar1=dtv[:, j:j + 1],
            scalar2=None, op0=ALU.mult),
            reads=(abc_b, dtv_b), writes=(decay_b,))
    p.op("scalar", lambda e: e.activation(out=decay[:], in_=decay[:], func=AF.Exp),
         reads=(decay_b,), writes=(decay_b,))

    bc_lhsT, bc_lhsT_b = sbuf("bc_lhsT", [128, HT, 2 * NST], BF16)
    bpf, bpf_b = sbuf("bpf", [128, HT, NST], F32)
    p.op("sync", lambda e: e.dma_start(
        out=bpf[:], in_=ins["L0_attn_b_proj"].rearrange("(j p) n -> p j n", p=128)),
        writes=(bpf_b,), inc=16)
    for j in range(HT):
        p.op("vector", lambda e, j=j: e.tensor_copy(out=bc_lhsT[:, j, 0:NST], in_=bpf[:, j, :]),
             reads=(bpf_b,), writes=(bc_lhsT_b,))
    p.op("sync", lambda e: e.dma_start(
        out=bpf[:], in_=ins["L0_attn_c_proj"].rearrange("(j p) n -> p j n", p=128)),
        reads=(bpf_b,), writes=(bpf_b,), inc=16)
    for j in range(HT):
        p.op("vector", lambda e, j=j: e.tensor_copy(out=bc_lhsT[:, j, NST:2 * NST], in_=bpf[:, j, :]),
             reads=(bpf_b,), writes=(bc_lhsT_b,))

    pos_f = pos_stack.enter_context(nc.sbuf_tensor("pos_f", [128, T], F32))
    pos_f_b = Buf("pos_f")
    C["pos_f"] = (pos_f, pos_f_b)
    with contextlib.ExitStack() as tst2:
        tmp_i2 = tst2.enter_context(nc.sbuf_tensor("setup_tmpi2", [128, T], I32))
        tmp_i2_b = Buf("setup_tmpi2")
        p.op("gpsimd", lambda e: e.iota(out=tmp_i2[:], pattern=[[1, T]], base=0,
                                        channel_multiplier=0),
             writes=(tmp_i2_b,))
        p.op("vector", lambda e: e.tensor_copy(out=pos_f[:], in_=tmp_i2[:]),
             reads=(tmp_i2_b,), writes=(pos_f_b,))
        p.barrier()


def emit_skipw(p, stack, ins, C, skipw_sc, skipw_sc_b, wout_sc, wout_sc_b):
    nc = p.nc
    with contextlib.ExitStack() as st:
        wout_bf = st.enter_context(nc.sbuf_tensor("su_wout", [128, HT, D], BF16))
        wout_bf_b = Buf("su_wout")
        wf = st.enter_context(nc.sbuf_tensor("su_wf", [128, D], F32))
        wf_b = Buf()
        for ec in range(HT):
            p.op("sync", lambda e, ec=ec: e.dma_start(out=wf[:], in_=ins["L0_attn_output_proj"][ts2(ec), :]),
                 reads=(wf_b,), writes=(wf_b,), inc=16)
            p.op("vector", lambda e, ec=ec: e.tensor_copy(out=wout_bf[:, ec, :], in_=wf[:]),
                 reads=(wf_b,), writes=(wout_bf_b,))
            p.op("sync", lambda e, ec=ec: e.dma_start(out=wout_sc[ec], in_=wout_bf[:, ec, :]),
                 reads=(wout_bf_b,), writes=(wout_sc_b,), inc=16)
        skf = [(st.enter_context(nc.sbuf_tensor(f"su_skf{i}", [128, 128], F32)), Buf()) for i in range(2)]
        skb = [(st.enter_context(nc.sbuf_tensor(f"su_skb{i}", [128, 128], BF16)), Buf()) for i in range(2)]
        skT = [(st.enter_context(nc.sbuf_tensor(f"su_skT{i}", [128, 128], BF16)), Buf()) for i in range(2)]
        out_bf = st.enter_context(nc.sbuf_tensor("su_obf", [128, D], BF16))
        out_bf_b = Buf()
        ps_acc, ps_acc_b = psum_f32(st, nc, "su_ps", D)
        ps_tp = (st.enter_context(nc.psum_tensor("su_tp", [128, 1024], BF16)), Buf(excl=True))
        for ht in range(HT):
            mms = []
            for ec in range(HT):
                i = ec % 2
                f, f_b = skf[i]
                b, b_b = skb[i]
                tT, tT_b = skT[i]
                p.op("sync", lambda e, ht=ht, ec=ec, f=f: e.dma_start(
                    out=f[:], in_=ins["L0_attn_skip_proj"][ts2(ht), ts2(ec)]),
                    reads=(f_b,), writes=(f_b,), inc=16)
                p.op("vector", lambda e, f=f, b=b: e.tensor_copy(out=b[:], in_=f[:]),
                     reads=(f_b,), writes=(b_b,))
                p.op("tensor", lambda e, b=b: e.transpose(ps_tp[0][:, 0:128], b[:], C["ident_bf"][0][:]),
                     reads=(b_b, C["ident_bf"][1]), writes=(ps_tp[1],))
                p.op("vector", lambda e, tT=tT: e.tensor_copy(out=tT[:], in_=ps_tp[0][:, 0:128]),
                     reads=(ps_tp[1],), writes=(tT_b,))
                p.group("tensor", [
                    (lambda e, tT=tT, ec=ec, nn=nn: e.matmul(
                        ps_acc[:, nn * 512:(nn + 1) * 512], tT[:],
                        wout_bf[:, ec, nn * 512:(nn + 1) * 512],
                        start=(ec == 0), stop=(ec == HT - 1)))
                    for nn in range(2)],
                    reads=(tT_b, wout_bf_b), writes=(ps_acc_b,))
            p.op("scalar", lambda e: e.activation(out=out_bf[:], in_=ps_acc[:], func=AF.Identity),
                 reads=(ps_acc_b,), writes=(out_bf_b,))
            p.op("sync", lambda e, ht=ht: e.dma_start(out=skipw_sc[ht], in_=out_bf[:]),
                 reads=(out_bf_b,), writes=(skipw_sc_b,), inc=16)
        p.barrier()


FRONT_PARTS = {"ln", "hd", "cos", "enc", "bc", "enc_tt", "enc_hdbf", "enc_hencb", "enc_xdt", "enc_dma_x", "enc_dma_h"}


def emit_l0a_front(p, stack, ins, C, s, xdt_sc, xdt_sc_b, hd_sc, hd_sc_b,
                   bcrows_sc, bcrows_sc_b, x_src, x_src_b, dbg=None):
    nc = p.nc
    with contextlib.ExitStack() as st_:
        lnst = alloc_ln_scratch(p, st_, pfx=f"a1s{s}_")
        lnst["ident_bf"] = C["ident_bf"][0]
        lnst["ident_buf"] = C["ident_bf"][1]
        lnst["xsrc_buf"] = x_src_b
        xn_fm = st_.enter_context(nc.sbuf_tensor(f"a2s{s}_xnfm", [128, DC, T], BF16))
        xn_fm_b = Buf("xn_fm")
        sink = None
        if dbg is not None:
            xd_ap, xd_b_ = dbg["xn_dbg"]
            sink = lambda tt: (xd_ap[s, ts2(tt), :], xd_b_)
        emit_ln_to_fm(p, lnst, x_src[s], xn_fm, xn_fm_b, TT, xn_tm_sink=sink)
        if dbg is not None:
            fm_ap, fm_b_ = dbg["xnfm_dbg"]
            for dc in range(DC):
                p.op("sync", lambda e, dc=dc: e.dma_start(out=fm_ap[s, dc], in_=xn_fm[:, dc, :]),
                     reads=(xn_fm_b,), writes=(fm_b_,), inc=16)
        if "hd" not in FRONT_PARTS:
            p.barrier()
            return

        wf = [(st_.enter_context(nc.sbuf_tensor(f"a2s{s}_wf{i}", [128, DC, 128], F32)), Buf()) for i in range(2)]
        wb = [(st_.enter_context(nc.sbuf_tensor(f"a2s{s}_wb{i}", [128, DC, 128], BF16)), Buf()) for i in range(2)]
        cosb = (st_.enter_context(nc.sbuf_tensor(f"a2s{s}_cos", [128, T], F32)), Buf())
        cri = (st_.enter_context(nc.sbuf_tensor(f"a2s{s}_cri", [128, T], I32)), Buf())
        crf = (st_.enter_context(nc.sbuf_tensor(f"a2s{s}_crf", [128, T], F32)), Buf())
        henc = (st_.enter_context(nc.sbuf_tensor(f"a2s{s}_henc", [128, T], F32)), Buf())
        hencb = (st_.enter_context(nc.sbuf_tensor(f"a2s{s}_hencb", [128, T], BF16)), Buf())
        xdtb = [(st_.enter_context(nc.sbuf_tensor(f"a2s{s}_xdt{i}", [128, T], BF16)), Buf()) for i in range(2)]
        hdb = [(st_.enter_context(nc.sbuf_tensor(f"a2s{s}_hdb{i}", [128, T], BF16)), Buf()) for i in range(2)]
        bcbf = (st_.enter_context(nc.sbuf_tensor(f"a2s{s}_bcbf", [32, T], BF16)), Buf())
        ps_hd = [psum_f32(st_, nc, f"a2s{s}_pshd{i}", 512) for i in range(2)]
        ps_bc, ps_bc_b = psum_f32(st_, nc, f"a2s{s}_psbc", T)

        dtv, dtv_b = C["dtv"]
        phs, phs_b = C["phs"]
        pos_f, pos_f_b = C["pos_f"]
        for ht in range(HT):
            i = ht % 2
            f, f_b = wf[i]
            b, b_b = wb[i]
            p.op("sync", lambda e, ht=ht, f=f: e.dma_start(
                out=f[:], in_=ins["L0_attn_input_proj"][:, ts2(ht)].rearrange("(c p) h -> p c h", p=128)),
                reads=(f_b,), writes=(f_b,), inc=16)
            p.op("vector", lambda e, f=f, b=b: e.tensor_copy(
                out=b[:].rearrange("p c h -> p (c h)"), in_=f[:].rearrange("p c h -> p (c h)")),
                reads=(f_b,), writes=(b_b,))
            if "cos" not in FRONT_PARTS:
                continue
            p.op("vector", lambda e, ht=ht: e.tensor_scalar(
                out=crf[0][:], in0=pos_f[:], scalar1=phs[:, ht:ht + 1], scalar2=0.25,
                op0=ALU.mult, op1=ALU.add),
                reads=(pos_f_b, phs_b), writes=(crf[1],))
            p.op("vector", lambda e: e.tensor_copy(out=cri[0][:], in_=crf[0][:]),
                 reads=(crf[1],), writes=(cri[1],))
            p.op("vector", lambda e: e.tensor_copy(out=cosb[0][:], in_=cri[0][:]),
                 reads=(cri[1],), writes=(cosb[1],))
            p.op("vector", lambda e: e.tensor_tensor(out=crf[0][:], in0=crf[0][:], in1=cosb[0][:],
                                                     op=ALU.subtract),
                 reads=(crf[1], cosb[1]), writes=(crf[1],))
            p.op("scalar", lambda e: e.activation(out=cosb[0][:], in_=crf[0][:], func=AF.Sin,
                                                  scale=TWO_PI),
                 reads=(crf[1],), writes=(cosb[1],))
            hdt, hdt_b = hdb[i]
            if "enc" not in FRONT_PARTS:
                continue
            for hh in range(4):
                ph, ph_b = ps_hd[hh % 2]
                p.group("tensor", [
                    (lambda e, dc=dc, b=b, hh=hh, ph=ph: e.matmul(
                        ph[:], b[:, dc, :],
                        xn_fm[:, dc, hh * 512:(hh + 1) * 512],
                        start=(dc == 0), stop=(dc == DC - 1)))
                    for dc in range(DC)],
                    reads=(b_b, xn_fm_b), writes=(ph_b,))
                if "enc_tt" in FRONT_PARTS:
                    p.op("vector", lambda e, hh=hh, ph=ph: e.tensor_tensor(
                        out=henc[0][:, hh * 512:(hh + 1) * 512], in0=ph[:],
                        in1=cosb[0][:, hh * 512:(hh + 1) * 512], op=ALU.mult),
                        reads=(ph_b, cosb[1]), writes=(henc[1],))
                if "enc_hdbf" in FRONT_PARTS:
                    p.op("scalar", lambda e, hh=hh, ph=ph, hdt=hdt: e.activation(
                        out=hdt[:, hh * 512:(hh + 1) * 512], in_=ph[:], func=AF.Identity),
                        reads=(ph_b,), writes=(hdt_b,))
            if "enc_hencb" in FRONT_PARTS:
                p.op("vector", lambda e: e.tensor_copy(out=hencb[0][:], in_=henc[0][:]),
                     reads=(henc[1],), writes=(hencb[1],))
            xd, xd_b = xdtb[i]
            if "enc_xdt" in FRONT_PARTS:
                p.op("scalar", lambda e, ht=ht, xd=xd: e.activation(
                    out=xd[:], in_=henc[0][:], func=AF.Identity, scale=dtv[:, ht:ht + 1]),
                    reads=(henc[1], dtv_b), writes=(xd_b,))
            if "enc_dma_x" in FRONT_PARTS:
                p.op("sync", lambda e, ht=ht, xd=xd: e.dma_start(out=xdt_sc[s, ht], in_=xd[:]),
                     reads=(xd_b,), writes=(xdt_sc_b,), inc=16)
            if "enc_dma_h" in FRONT_PARTS:
                p.op("sync", lambda e, ht=ht, hdt=hdt: e.dma_start(out=hd_sc[s, ht], in_=hdt[:]),
                     reads=(hdt_b,), writes=(hd_sc_b,), inc=16)
            if "bc" not in FRONT_PARTS:
                continue
            p.group("tensor", [
                (lambda e, ht=ht, nn=nn: e.matmul(
                    ps_bc[0:32, nn * 512:(nn + 1) * 512], C["bc_lhsT"][0][:, ht, :],
                    hencb[0][:, nn * 512:(nn + 1) * 512],
                    start=(ht == 0), stop=(ht == HT - 1)))
                for nn in range(4)],
                reads=(hencb[1], C["bc_lhsT"][1]), writes=(ps_bc_b,))
        p.op("scalar", lambda e: e.activation(out=bcbf[0][:], in_=ps_bc[0:32, :], func=AF.Identity),
             reads=(ps_bc_b,), writes=(bcbf[1],))
        p.op("sync", lambda e: e.dma_start(out=bcrows_sc[s], in_=bcbf[0][:]),
             reads=(bcbf[1],), writes=(bcrows_sc_b,), inc=16)
        p.barrier()


def emit_l0a_scan(p, stack, ins, C, s, xdt_sc, xdt_sc_b, bcrows_sc, bcrows_sc_b,
                  y_sc, y_sc_b, bg_tick=None):
    nc = p.nc
    with contextlib.ExitStack() as st_:
        Bb = st_.enter_context(nc.sbuf_tensor(f"a3s{s}_Bb", [128, NST, T], BF16))
        Bb_b = Buf("Bb")
        Cb = st_.enter_context(nc.sbuf_tensor(f"a3s{s}_Cb", [128, NST, T], BF16))
        Cb_b = Buf("Cb")
        row = (st_.enter_context(nc.sbuf_tensor(f"a3s{s}_row", [1, T], BF16)), Buf())
        xdt = [(st_.enter_context(nc.sbuf_tensor(f"a3s{s}_xdt{i}", [128, T], BF16)), Buf()) for i in range(2)]
        ub = [(st_.enter_context(nc.sbuf_tensor(f"a3s{s}_u{i}", [128, T], BF16)), Buf()) for i in range(2)]
        hb = [(st_.enter_context(nc.sbuf_tensor(f"a3s{s}_h{i}", [128, T], BF16)), Buf()) for i in range(2)]
        hcb = [(st_.enter_context(nc.sbuf_tensor(f"a3s{s}_hc{i}", [128, T], BF16)), Buf()) for i in range(2)]
        yb = [(st_.enter_context(nc.sbuf_tensor(f"a3s{s}_y{i}", [128, T], BF16)), Buf()) for i in range(2)]
        ps_y = [psum_f32(st_, nc, f"a3s{s}_psy0", T)]
        ps_y = [ps_y[0], ps_y[0]]
        decay, decay_b = C["decay"]

        for n in range(2 * NST):
            p.op("sync", lambda e, n=n: e.dma_start(out=row[0][:], in_=bcrows_sc[s, n].unsqueeze(0)),
                 reads=(bcrows_sc_b, row[1]), writes=(row[1],), inc=16)
            ph, ph_b = ps_y[0]
            p.group("tensor", [
                (lambda e, nn=nn: e.matmul(
                    ph[:, nn * 512:(nn + 1) * 512], C["ones_row_bf"][0][:],
                    row[0][:, nn * 512:(nn + 1) * 512], start=True, stop=True))
                for nn in range(4)],
                reads=(row[1], C["ones_row_bf"][1]), writes=(ph_b,))
            dst = Bb if n < NST else Cb
            dst_b = Bb_b if n < NST else Cb_b
            nn0 = n % NST
            p.op("scalar", lambda e, dst=dst, nn0=nn0, ph=ph: e.activation(
                out=dst[:, nn0, :], in_=ph[:], func=AF.Identity),
                reads=(ph_b,), writes=(dst_b,))

        for ht in range(HT):
            if bg_tick is not None:
                for _ in range(6):
                    bg_tick()
            i = ht % 2
            xd, xd_b = xdt[i]
            p.op("sync", lambda e, ht=ht, xd=xd: e.dma_start(out=xd[:], in_=xdt_sc[s, ht]),
                 reads=(xdt_sc_b, xd_b), writes=(xd_b,), inc=16)
            py, py_b = ps_y[i]
            for n in range(NST):
                u, u_b = ub[n % 2]
                h, h_b = hb[n % 2]
                hc, hc_b = hcb[n % 2]
                p.op("vector", lambda e, u=u, xd=xd, n=n: e.tensor_tensor(
                    out=u[:], in0=xd[:], in1=Bb[:, n, :], op=ALU.mult),
                    reads=(xd_b, Bb_b), writes=(u_b,))
                p.op("vector", lambda e, h=h, u=u, ht=ht, n=n: e.tensor_tensor_scan(
                    out=h[:], data0=decay[:, ht * NST + n: ht * NST + n + 1].broadcast_to((128, T)),
                    data1=u[:], initial=0.0, op0=ALU.mult, op1=ALU.add),
                    reads=(u_b, decay_b), writes=(h_b,))
                p.op("vector", lambda e, hc=hc, h=h, n=n: e.tensor_tensor(
                    out=hc[:], in0=h[:], in1=Cb[:, n, :], op=ALU.mult),
                    reads=(h_b, Cb_b), writes=(hc_b,))
                p.group("tensor", [
                    (lambda e, nn=nn, hc=hc, n=n, py=py: e.matmul(
                        py[:, nn * 512:(nn + 1) * 512], C["ident_bf"][0][:],
                        hc[:, nn * 512:(nn + 1) * 512],
                        start=(n == 0), stop=(n == NST - 1)))
                    for nn in range(4)],
                    reads=(hc_b, C["ident_bf"][1]), writes=(py_b,))
            y, y_b = yb[i]
            p.op("scalar", lambda e, y=y, py=py: e.activation(out=y[:], in_=py[:], func=AF.Identity),
                 reads=(py_b,), writes=(y_b,))
            p.op("sync", lambda e, ht=ht, y=y: e.dma_start(out=y_sc[s, ht], in_=y[:]),
                 reads=(y_b,), writes=(y_sc_b,), inc=16)
        p.barrier()


def emit_l0a_out(p, stack, ins, C, skipw_sc, skipw_sc_b, wout_sc, wout_sc_b,
                 hd_sc, hd_sc_b, y_sc, y_sc_b, x_src, x_src_b, x1_sc, x1_sc_b):
    nc = p.nc
    with contextlib.ExitStack() as st_:
        wo = [(st_.enter_context(nc.sbuf_tensor(f"a4_wo{i}", [128, HT, 128], BF16)), Buf()) for i in range(2)]
        sw = [(st_.enter_context(nc.sbuf_tensor(f"a4_sw{i}", [128, HT, 128], BF16)), Buf()) for i in range(2)]
        y_tq = (st_.enter_context(nc.sbuf_tensor("a4_ytq", [128, HT, 512], BF16)), Buf())
        hd_tq = (st_.enter_context(nc.sbuf_tensor("a4_hdtq", [128, HT, 512], BF16)), Buf())
        x_tq = (st_.enter_context(nc.sbuf_tensor("a4_xtq", [128, 4, D], F32)), Buf())
        o_sb = (st_.enter_context(nc.sbuf_tensor("a4_osb", [128, 512], F32)), Buf())
        x1a = (st_.enter_context(nc.sbuf_tensor("a4_x1a", [128, 4, D], F32)), Buf())
        ps_o = [psum_f32(st_, nc, f"a4_pso{i}", 512) for i in range(2)]
        ps_tp = psum_f32(st_, nc, "a4_pstp", 512)
        for s in range(SEQ):
            for tq in range(4):
                p.op("sync", lambda e, s=s, tq=tq: e.dma_start(
                    out=y_tq[0][:], in_=y_sc[s, :, :, tq * 512:(tq + 1) * 512].rearrange("h p t -> p h t")),
                    reads=(y_sc_b, y_tq[1]), writes=(y_tq[1],), inc=16)
                p.op("sync", lambda e, s=s, tq=tq: e.dma_start(
                    out=hd_tq[0][:], in_=hd_sc[s, :, :, tq * 512:(tq + 1) * 512].rearrange("h p t -> p h t")),
                    reads=(hd_sc_b, hd_tq[1]), writes=(hd_tq[1],), inc=16)
                p.op("sync", lambda e, s=s, tq=tq: e.dma_start(
                    out=x_tq[0][:], in_=x_src[s, tq * 512:(tq + 1) * 512, :].rearrange("(b p) d -> p b d", p=128)),
                    reads=(x_src_b, x_tq[1]), writes=(x_tq[1],), inc=16)
                for dc in range(DC):
                    wi = dc % 2
                    p.op("sync", lambda e, dc=dc, wi=wi: e.dma_start(
                        out=wo[wi][0][:], in_=wout_sc[:, :, ts2(dc)].rearrange("h p d -> p h d")),
                        reads=(wout_sc_b, wo[wi][1]), writes=(wo[wi][1],), inc=16)
                    p.op("sync", lambda e, dc=dc, wi=wi: e.dma_start(
                        out=sw[wi][0][:], in_=skipw_sc[:, :, ts2(dc)].rearrange("h p d -> p h d")),
                        reads=(skipw_sc_b, sw[wi][1]), writes=(sw[wi][1],), inc=16)
                    po, po_b = ps_o[dc % 2]
                    fns = []
                    for ht in range(HT):
                        fns.append(lambda e, ht=ht, wi=wi, po=po: e.matmul(
                            po[:], wo[wi][0][:, ht, :], y_tq[0][:, ht, :],
                            start=(ht == 0), stop=False))
                        fns.append(lambda e, ht=ht, wi=wi, po=po: e.matmul(
                            po[:], sw[wi][0][:, ht, :], hd_tq[0][:, ht, :],
                            start=False, stop=(ht == HT - 1)))
                    p.group("tensor", fns, reads=(wo[wi][1], sw[wi][1], y_tq[1], hd_tq[1]),
                            writes=(po_b,))
                    p.op("scalar", lambda e, po=po: e.activation(out=o_sb[0][:], in_=po[:],
                                                                 func=AF.Identity),
                         reads=(po_b,), writes=(o_sb[1],))
                    for b in range(4):
                        p.op("tensor", lambda e, b=b: e.transpose(
                            ps_tp[0][:, 0:128], o_sb[0][:, ts2(b)], C["ident_f32"][0][:]),
                            reads=(o_sb[1], C["ident_f32"][1]), writes=(ps_tp[1],))
                        p.op("vector", lambda e, b=b, dc=dc: e.tensor_tensor(
                            out=x1a[0][:, b, ts2(dc)], in0=ps_tp[0][:, 0:128],
                            in1=x_tq[0][:, b, ts2(dc)], op=ALU.add),
                            reads=(ps_tp[1], x_tq[1]), writes=(x1a[1],))
                p.op("sync", lambda e, s=s, tq=tq: e.dma_start(
                    out=x1_sc[s, tq * 512:(tq + 1) * 512, :].rearrange("(b p) d -> p b d", p=128),
                    in_=x1a[0][:]),
                    reads=(x1a[1],), writes=(x1_sc_b,), inc=16)
        p.barrier()



INPUT_SPECS = [
    ("L0_attn_x", [SEQ, T, D]), ("L0_attn_input_proj", [D, HD]),
    ("L0_attn_output_proj", [HD, D]), ("L0_attn_a_log", [NST]),
    ("L0_attn_b_proj", [HD, NST]), ("L0_attn_c_proj", [HD, NST]),
    ("L0_attn_dt_proj", [HD]), ("L0_attn_skip_proj", [HD, HD]),
    ("L0_attn_floquet_phases", [HD]),
    ("L0_ffn_gate_w", [E0, D]), ("L0_ffn_w_gate", [E0, D, H0]),
    ("L0_ffn_w_up", [E0, D, H0]), ("L0_ffn_w_down", [E0, H0, D]),
    ("L0_ffn_sw_gate", [D, H0]), ("L0_ffn_sw_up", [D, H0]), ("L0_ffn_sw_down", [H0, D]),
    ("L1_attn_wq", [D, D]), ("L1_attn_bq", [D]), ("L1_attn_wk", [D, D]),
    ("L1_attn_bk", [D]), ("L1_attn_wv", [D, D]), ("L1_attn_bv", [D]),
    ("L1_attn_wo", [D, D]), ("L1_attn_bo", [D]), ("L1_attn_alpha", [1]),
    ("L1_ffn_gate_w", [E1, D]), ("L1_ffn_w_gate", [E1, I1, D]),
    ("L1_ffn_w_up", [E1, I1, D]), ("L1_ffn_w_down", [E1, D, I1]),
]


def emit_l1f_wtp(p, wtp_stack, ins, C):
    nc = p.nc
    IC = I1 // 128
    gsc = nc.dram_tensor("f1_wgT_sc", [E1, IC, 128, DC, 128], BF16, kind="Internal").ap()
    usc = nc.dram_tensor("f1_wuT_sc", [E1, IC, 128, DC, 128], BF16, kind="Internal").ap()
    dsc = nc.dram_tensor("f1_wdT_sc", [E1, DC, IC, 128, 128], BF16, kind="Internal").ap()
    gsc_b, usc_b, dsc_b = Buf("wgT_sc"), Buf("wuT_sc"), Buf("wdT_sc")
    wf = [(wtp_stack.enter_context(nc.sbuf_tensor(f"wtp_wf{i}", [128, D], F32)), Buf())
          for i in range(2)]
    wb = [(wtp_stack.enter_context(nc.sbuf_tensor(f"wtp_wb{i}", [128, D], BF16)), Buf())
          for i in range(2)]
    wt = [(wtp_stack.enter_context(nc.sbuf_tensor(f"wtp_wt{i}", [128, DC, 128], BF16)), Buf())
          for i in range(2)]
    pt, pt_b = psum_f32(wtp_stack, nc, "wtp_ps", 512)
    tpo = pt[:].bitcast(BF16)[:, 0:128]
    jobs = []
    for ee in range(E1):
        for ic in range(IC):
            jobs.append((ins["L1_ffn_w_gate"][ee, ts2(ic), :], gsc[ee, ic], gsc_b))
            jobs.append((ins["L1_ffn_w_up"][ee, ts2(ic), :], usc[ee, ic], usc_b))
        for dc in range(DC):
            for q in range(4):
                jobs.append((ins["L1_ffn_w_down"][ee, ts2(dc), q * 1024:(q + 1) * 1024],
                             dsc[ee, dc, q * 8:(q + 1) * 8].rearrange("c p d -> p c d"),
                             dsc_b))
    state = [0]

    def emit_one():
        if state[0] >= len(jobs):
            return False
        ji = state[0]
        state[0] += 1
        srcap, dstap, dbuf = jobs[ji]
        i = ji % 2
        f, fb = wf[i]
        b, bb = wb[i]
        t, tb = wt[i]
        p.op("gpsimd", lambda e, srcap=srcap, f=f: e.dma_start(out=f[:], in_=srcap),
             reads=(fb,), writes=(fb,), inc=16)
        p.op("scalar", lambda e, f=f, b=b: e.activation(out=b[:], in_=f[:], func=AF.Identity),
             reads=(fb,), writes=(bb,))
        for dc in range(DC):
            p.op("tensor", lambda e, b=b, dc=dc: e.transpose(tpo, b[:, ts2(dc)], C["ident_bf"][0][:]),
                 reads=(bb, C["ident_bf"][1]), writes=(pt_b,))
            p.op("scalar", lambda e, t=t, dc=dc: e.activation(
                out=t[:, dc, :], in_=tpo, func=AF.Identity),
                reads=(pt_b,), writes=(tb,))
        ncols = dstap.shape[1] if len(dstap.shape) == 3 else DC
        p.op("gpsimd", lambda e, t=t, dstap=dstap, ncols=ncols: e.dma_start(
            out=dstap, in_=t[:, 0:ncols, :]),
            reads=(tb,), writes=(dbuf,), inc=16)
        return True

    return ((gsc, gsc_b), (usc, usc_b), (dsc, dsc_b)), emit_one


def build_core(stop_after="ALL", debug_out=()):
    nc = bass.Bass("TRN2", target_bir_lowering=False)
    ins = {}
    for name, shape in INPUT_SPECS:
        ins[name] = nc.dram_tensor(name, shape, F32, kind="ExternalInput").ap()
    out_d = nc.dram_tensor("out", [SEQ, T, D], F32, kind="ExternalOutput").ap()
    out_b = Buf("out")

    def scr(name, shape, dtype):
        kind = "ExternalOutput" if name in debug_out else "Internal"
        return nc.dram_tensor(name, shape, dtype, kind=kind).ap(), Buf(name)

    S = {}
    S["xdt_sc"] = scr("xdt_sc", [SEQ, HT, 128, T], BF16)
    S["hd_sc"] = scr("hd_sc", [SEQ, HT, 128, T], BF16)
    S["y_sc"] = scr("y_sc", [SEQ, HT, 128, T], BF16)
    S["bcrows_sc"] = scr("bcrows_sc", [SEQ, 32, T], BF16)
    S["skipw_sc"] = scr("skipw_sc", [HT, 128, D], BF16)
    S["wout_sc"] = scr("wout_sc", [HT, 128, D], BF16)
    S["x1_sc"] = scr("x1_sc", [SEQ, T, D], F32)
    S["x2_sc"] = scr("x2_sc", [SEQ, T, D], F32)
    S["x3_sc"] = scr("x3_sc", [SEQ, T, D], F32)
    S["xn3_sc"] = scr("xn3_sc", [SEQ, T, D], BF16)
    S["xn_dbg"] = scr("xn_dbg", [SEQ, T, D], BF16)
    S["xnfm_dbg"] = scr("xnfm_dbg", [SEQ, DC, 128, T], BF16)
    cin_list = [[scr(f"cin_{si}_{ei}", [K1, D], BF16) for ei in range(E1)] for si in range(SEQ)]
    S["cin_sc"] = ([[t for t, _ in row] for row in cin_list],
                   [[b for _, b in row] for row in cin_list])
    cout_list = [[scr(f"cout_{si}_{ei}", [K1, D], BF16) for ei in range(E1)] for si in range(SEQ)]
    S["cout_sc"] = ([[t for t, _ in row] for row in cout_list],
                    [[b for _, b in row] for row in cout_list])
    S["cmb_sc"] = scr("cmb_sc", [SEQ, E0 + 1, T], BF16)
    S["gidx_sc"] = scr("gidx_sc", [SEQ, 2, E1, T], F32)
    S["out"] = (out_d, out_b)
    x_src = ins["L0_attn_x"]
    x_src_b = Buf("x_in")
    S["x_src"] = (x_src, x_src_b)

    with contextlib.ExitStack() as stack:
        p = Prog(nc, stack)
        C = {}
        pos_stack = contextlib.ExitStack()
        emit_setup(p, stack, ins, C, pos_stack)
        p.barrier()
        if stop_after != "SETUP":
            emit_skipw(p, stack, ins, C, *S["skipw_sc"], *S["wout_sc"])
        if stop_after not in ("SETUP", "SKIPW"):
            for s in range(SEQ):
                emit_l0a_front(p, stack, ins, C, s, *S["xdt_sc"], *S["hd_sc"],
                               *S["bcrows_sc"], x_src, x_src_b,
                               dbg=S if debug_out else None)
        pos_stack.close()
        wtp_stack = contextlib.ExitStack()
        S["wtp"], wtp_tick = emit_l1f_wtp(p, wtp_stack, ins, C)
        if stop_after not in ("SETUP", "SKIPW", "FRONT"):
            for s in range(SEQ):
                emit_l0a_scan(p, stack, ins, C, s, *S["xdt_sc"], *S["bcrows_sc"],
                              *S["y_sc"], bg_tick=wtp_tick)
        while wtp_tick():
            pass
        wtp_stack.close()
        if stop_after in ("L0A",) or stop_after == "ALL" or stop_after not in ("SETUP", "SKIPW", "FRONT", "SCAN"):
            x1_dst = S["out"] if stop_after == "L0A" else S["x1_sc"]
            emit_l0a_out(p, stack, ins, C, *S["skipw_sc"], *S["wout_sc"], *S["hd_sc"],
                         *S["y_sc"], x_src, x_src_b, *x1_dst)
        if stop_after in ("SETUP", "SKIPW", "FRONT", "SCAN"):
            with contextlib.ExitStack() as dst_:
                z = dst_.enter_context(nc.sbuf_tensor("dummyz", [128, D], F32))
                zb = Buf("z")
                p.op("vector", lambda e: e.memset(z[:], 0.0), writes=(zb,))
                p.op("sync", lambda e: e.dma_start(out=out_d[0, 0:128, :], in_=z[:]),
                     reads=(zb,), writes=(out_b,), inc=16)
        if stop_after in ("L0F", "L1A", "L1F", "ALL"):
            x2_dst = S["out"] if stop_after == "L0F" else S["x2_sc"]
            emit_l0f(p, stack, ins, C, *S["x1_sc"], *x2_dst, *S["cmb_sc"])
        if stop_after in ("L1A", "L1F", "ALL"):
            x3_dst = S["out"] if stop_after == "L1A" else S["x3_sc"]
            emit_l1a(p, stack, ins, C, *S["x2_sc"], *x3_dst)
        if stop_after in ("L1F", "ALL"):
            emit_l1f(p, stack, ins, C, S)
        p.barrier()
        p.replay()
    return nc


def emit_l0f(p, stack, ins, C, x1_sc, x1_sc_b, x2_sc, x2_sc_b, cmb_sc, cmb_sc_b):
    nc = p.nc
    with contextlib.ExitStack() as st_:
        gwT = st_.enter_context(nc.sbuf_tensor("f0_gwT", [128, DC, E0], BF16))
        gwT_b = Buf("gwT")
        gwf = (st_.enter_context(nc.sbuf_tensor("f0_gwf", [E0, D], F32)), Buf())
        gwb = (st_.enter_context(nc.sbuf_tensor("f0_gwb", [128, D], BF16)), Buf())
        gn = (st_.enter_context(nc.sbuf_tensor("f0_gn", [E0, 1], F32)), Buf())
        gsq = (st_.enter_context(nc.sbuf_tensor("f0_gsq", [E0, D], F32)), Buf())
        gpsum_stack = contextlib.ExitStack()
        ps_tp, ps_tp_b = psum_f32(gpsum_stack, nc, "f0_pstp", 512)
        p.op("sync", lambda e: e.dma_start(out=gwf[0][:], in_=ins["L0_ffn_gate_w"]),
             writes=(gwf[1],), inc=16)
        p.op("scalar", lambda e: e.activation(out=gsq[0][:], in_=gwf[0][:], func=AF.Square,
                                              accum_out=gn[0][:]),
             reads=(gwf[1],), writes=(gsq[1], gn[1]))
        p.op("scalar", lambda e: e.activation(out=gn[0][:], in_=gn[0][:], func=AF.Sqrt),
             reads=(gn[1],), writes=(gn[1],))
        p.op("vector", lambda e: e.tensor_scalar(out=gn[0][:], in0=gn[0][:], scalar1=1e-12,
                                                 scalar2=None, op0=ALU.max),
             reads=(gn[1],), writes=(gn[1],))
        p.op("vector", lambda e: e.reciprocal(out=gn[0][:], in_=gn[0][:]),
             reads=(gn[1],), writes=(gn[1],))
        p.op("vector", lambda e: e.memset(gwb[0][:], 0.0), writes=(gwb[1],))
        p.op("vector", lambda e: e.tensor_scalar(out=gwb[0][0:E0, :], in0=gwf[0][:],
                                                 scalar1=gn[0][:, 0:1], scalar2=None, op0=ALU.mult),
             reads=(gwf[1], gn[1]), writes=(gwb[1],))
        tp_out_ap = ps_tp[:].bitcast(BF16)[:, 0:128]
        tp_in_ap = ps_tp[:].bitcast(BF16)[:, 0:E0]
        for dc in range(DC):
            p.op("tensor", lambda e, dc=dc, a=tp_out_ap: e.transpose(
                a, gwb[0][:, ts2(dc)], C["ident_bf"][0][:]),
                reads=(gwb[1], C["ident_bf"][1]), writes=(ps_tp_b,))
            p.op("vector", lambda e, dc=dc, a=tp_in_ap: e.tensor_copy(
                out=gwT[:, dc, :], in_=a),
                reads=(ps_tp_b,), writes=(gwT_b,))

        p.barrier()
        gpsum_stack.close()
        for s in range(SEQ):
            _emit_l0f_seq(p, nc, ins, C, s, gwT, gwT_b, x1_sc, x1_sc_b, x2_sc, x2_sc_b,
                          cmb_sc, cmb_sc_b)
        p.barrier()


def _emit_l0f_seq(p, nc, ins, C, s, gwT, gwT_b, x1_sc, x1_sc_b, x2_sc, x2_sc_b, cmb_sc, cmb_sc_b):
    with contextlib.ExitStack() as st_:
        xn_fm = st_.enter_context(nc.sbuf_tensor(f"f0s{s}_xnfm", [128, DC, T], BF16))
        xn_fm_b = Buf("xn_fm")
        racc = st_.enter_context(nc.sbuf_tensor(f"f0s{s}_racc", [128, DC, T], F32))
        racc_b = Buf("racc")
        abuf = st_.enter_context(nc.sbuf_tensor(f"f0s{s}_abuf", [128, 4, T], BF16))
        abuf_b = Buf("abuf")
        cmbf = (st_.enter_context(nc.sbuf_tensor(f"f0s{s}_cmbf", [32, T], BF16)), Buf())
        nsq = st_.enter_context(nc.sbuf_tensor(f"f0s{s}_nsq", [128, TT], F32))
        nsq_b = Buf("nsq")
        rn = (st_.enter_context(nc.sbuf_tensor(f"f0s{s}_rn", [128, TT], F32)), Buf())
        ps_tp, ps_tp_b = psum_f32(st_, nc, f"f0s{s}_pstp", 512)
        ps_g, ps_g_b = psum_f32(st_, nc, f"f0s{s}_psg", 1024)
        ps_u, ps_u_b = psum_f32(st_, nc, f"f0s{s}_psu", 1024)

        with contextlib.ExitStack() as rt_:
            ps_a, ps_a_b = psum_f32(rt_, nc, f"f0s{s}_psa", 1024)
            lnst = alloc_ln_scratch(p, rt_, pfx=f"f0s{s}_")
            lnst["ident_bf"] = C["ident_bf"][0]
            lnst["ident_buf"] = C["ident_bf"][1]
            lnst["xsrc_buf"] = x1_sc_b
            emit_ln_to_fm(p, lnst, x1_sc[s], xn_fm, xn_fm_b, TT, nsq_dst=nsq, nsq_dst_buf=nsq_b)
            p.op("scalar", lambda e: e.activation(out=rn[0][:], in_=nsq[:], func=AF.Sqrt),
                 reads=(nsq_b,), writes=(rn[1],))
            p.op("vector", lambda e: e.tensor_scalar(out=rn[0][:], in0=rn[0][:], scalar1=1e-12,
                                                     scalar2=None, op0=ALU.max),
                 reads=(rn[1],), writes=(rn[1],))
            p.op("vector", lambda e: e.reciprocal(out=rn[0][:], in_=rn[0][:]),
                 reads=(rn[1],), writes=(rn[1],))

            lgf = (rt_.enter_context(nc.sbuf_tensor(f"f0s{s}_lgf", [128, T], F32)), Buf())
            sct = (rt_.enter_context(nc.sbuf_tensor(f"f0s{s}_sct", [128, 16], F32)), Buf())
            cmp_t = (rt_.enter_context(nc.sbuf_tensor(f"f0s{s}_cmp", [128, 16], F32)), Buf())
            rankt = (rt_.enter_context(nc.sbuf_tensor(f"f0s{s}_rank", [128, 16], F32)), Buf())
            maskt = (rt_.enter_context(nc.sbuf_tensor(f"f0s{s}_mask", [128, 16], F32)), Buf())
            ws = (rt_.enter_context(nc.sbuf_tensor(f"f0s{s}_ws", [128, 1], F32)), Buf())
            p.op("vector", lambda e: e.memset(lgf[0][:], 0.0), writes=(lgf[1],))
            for th in range(2):
                p.group("tensor", [
                    (lambda e, dc=dc, tq=tq, th=th: e.matmul(
                        ps_a[0:E0, tq * 512:(tq + 1) * 512], gwT[:, dc, :],
                        xn_fm[:, dc, th * 1024 + tq * 512: th * 1024 + (tq + 1) * 512],
                        start=(dc == 0), stop=(dc == DC - 1)))
                    for tq in range(2) for dc in range(DC)],
                    reads=(gwT_b, xn_fm_b), writes=(ps_a_b,))
                p.op("scalar", lambda e, th=th: e.activation(
                    out=lgf[0][0:E0, th * 1024:(th + 1) * 1024], in_=ps_a[0:E0, :], func=AF.Identity),
                    reads=(ps_a_b,), writes=(lgf[1],))
            p.op("vector", lambda e: e.memset(cmbf[0][:], 1.0), writes=(cmbf[1],))
            for tt in range(TT):
                p.op("tensor", lambda e, tt=tt: e.transpose(
                    ps_tp[:, 0:128], lgf[0][:, ts2(tt)], C["ident_f32"][0][:]),
                    reads=(lgf[1], C["ident_f32"][1]), writes=(ps_tp_b,))
                p.op("vector", lambda e, tt=tt: e.tensor_scalar(
                    out=sct[0][:], in0=ps_tp[:, 0:E0], scalar1=rn[0][:, tt:tt + 1], scalar2=None,
                    op0=ALU.mult),
                    reads=(ps_tp_b, rn[1]), writes=(sct[1],))
                p.op("scalar", lambda e: e.activation(out=sct[0][:], in_=sct[0][:], func=AF.Sigmoid),
                     reads=(sct[1],), writes=(sct[1],))
                for ee in range(E0):
                    p.op("vector", lambda e, ee=ee: e.tensor_scalar(
                        out=cmp_t[0][:], in0=sct[0][:], scalar1=sct[0][:, ee:ee + 1], scalar2=0.0,
                        op0=ALU.is_gt, op1=ALU.add, accum_out=rankt[0][:, ee:ee + 1]),
                        reads=(sct[1],), writes=(cmp_t[1], rankt[1]))
                p.op("vector", lambda e: e.tensor_scalar(out=maskt[0][:], in0=rankt[0][:],
                                                         scalar1=7.5, scalar2=None, op0=ALU.is_lt),
                     reads=(rankt[1],), writes=(maskt[1],))
                p.op("vector", lambda e: e.tensor_tensor(out=maskt[0][:], in0=maskt[0][:],
                                                         in1=sct[0][:], op=ALU.mult),
                     reads=(maskt[1], sct[1]), writes=(maskt[1],))
                p.op("vector", lambda e: e.tensor_scalar(
                    out=cmp_t[0][:], in0=maskt[0][:], scalar1=1.0, scalar2=0.0,
                    op0=ALU.mult, op1=ALU.add, accum_out=ws[0][:]),
                    reads=(maskt[1],), writes=(cmp_t[1], ws[1]))
                p.op("vector", lambda e: e.reciprocal(out=ws[0][:], in_=ws[0][:]),
                     reads=(ws[1],), writes=(ws[1],))
                p.op("vector", lambda e: e.tensor_scalar(out=maskt[0][:], in0=maskt[0][:],
                                                         scalar1=ws[0][:, 0:1], scalar2=None, op0=ALU.mult),
                     reads=(maskt[1], ws[1]), writes=(maskt[1],))
                p.op("tensor", lambda e, tt=tt: e.transpose(
                    ps_tp[0:E0, 0:128], maskt[0][:], C["ident_f32"][0][:]),
                    reads=(maskt[1], C["ident_f32"][1]), writes=(ps_tp_b,))
                p.op("scalar", lambda e, tt=tt: e.activation(
                    out=cmbf[0][0:E0, ts2(tt)], in_=ps_tp[0:E0, 0:128], func=AF.Identity),
                    reads=(ps_tp_b,), writes=(cmbf[1],))
            p.barrier()

        p.op("vector", lambda e: e.memset(racc[:].rearrange("p c t -> p (c t)"), 0.0),
             writes=(racc_b,))
        cmbb = (st_.enter_context(nc.sbuf_tensor(f"f0s{s}_cmbb", [128, T], BF16)), Buf())
        crow = (st_.enter_context(nc.sbuf_tensor(f"f0s{s}_crow", [1, T], BF16)), Buf())
        sg = (st_.enter_context(nc.sbuf_tensor(f"f0s{s}_sg", [128, T], F32)), Buf())
        wgf = [(st_.enter_context(nc.sbuf_tensor(f"f0s{s}_wgf{i}", [128, DC, 128], F32)), Buf()) for i in range(2)]
        wgb = [(st_.enter_context(nc.sbuf_tensor(f"f0s{s}_wgb{i}", [128, DC, 128], BF16)), Buf()) for i in range(2)]
        wuf = [(st_.enter_context(nc.sbuf_tensor(f"f0s{s}_wuf{i}", [128, DC, 128], F32)), Buf()) for i in range(2)]
        wub = [(st_.enter_context(nc.sbuf_tensor(f"f0s{s}_wub{i}", [128, DC, 128], BF16)), Buf()) for i in range(2)]
        wdf = (st_.enter_context(nc.sbuf_tensor(f"f0s{s}_wdf", [128, 2, D], F32)), Buf())
        wdb = (st_.enter_context(nc.sbuf_tensor(f"f0s{s}_wdb", [128, 4, D], BF16)), Buf())
        H0C = H0 // 128

        p.op("sync", lambda e: e.dma_start(out=cmb_sc[s], in_=cmbf[0][0:E0 + 1, :]),
             reads=(cmbf[1],), writes=(cmb_sc_b,), inc=16)
        for ee in range(E0 + 1):
            p.op("sync", lambda e, ee=ee: e.dma_start(out=crow[0][:], in_=cmb_sc[s, ee].unsqueeze(0)),
                 reads=(cmb_sc_b, crow[1]), writes=(crow[1],), inc=16)
            for th in range(2):
                p.group("tensor", [
                    (lambda e, nn=nn, th=th: e.matmul(
                        ps_g[:, nn * 512:(nn + 1) * 512], C["ones_row_bf"][0][:],
                        crow[0][0:1, th * 1024 + nn * 512: th * 1024 + (nn + 1) * 512],
                        start=True, stop=True))
                    for nn in range(2)],
                    reads=(crow[1], C["ones_row_bf"][1]), writes=(ps_g_b,))
                p.op("scalar", lambda e, th=th: e.activation(
                    out=cmbb[0][:, th * 1024:(th + 1) * 1024], in_=ps_g[:], func=AF.Identity),
                    reads=(ps_g_b,), writes=(cmbb[1],))
            for hc in range(H0C):
                i = hc % 2
                if ee < E0:
                    gsrc = ins["L0_ffn_w_gate"][ee, :, ts2(hc)]
                    usrc = ins["L0_ffn_w_up"][ee, :, ts2(hc)]
                else:
                    gsrc = ins["L0_ffn_sw_gate"][:, ts2(hc)]
                    usrc = ins["L0_ffn_sw_up"][:, ts2(hc)]
                p.op("sync", lambda e, gsrc=gsrc, i=i: e.dma_start(
                    out=wgf[i][0][:], in_=gsrc.rearrange("(c p) h -> p c h", p=128)),
                    reads=(wgf[i][1],), writes=(wgf[i][1],), inc=16)
                p.op("vector", lambda e, i=i: e.tensor_copy(
                    out=wgb[i][0][:].rearrange("p c h -> p (c h)"),
                    in_=wgf[i][0][:].rearrange("p c h -> p (c h)")),
                    reads=(wgf[i][1],), writes=(wgb[i][1],))
                p.op("sync", lambda e, usrc=usrc, i=i: e.dma_start(
                    out=wuf[i][0][:], in_=usrc.rearrange("(c p) h -> p c h", p=128)),
                    reads=(wuf[i][1],), writes=(wuf[i][1],), inc=16)
                p.op("vector", lambda e, i=i: e.tensor_copy(
                    out=wub[i][0][:].rearrange("p c h -> p (c h)"),
                    in_=wuf[i][0][:].rearrange("p c h -> p (c h)")),
                    reads=(wuf[i][1],), writes=(wub[i][1],))
                for th in range(2):
                    p.group("tensor", [
                        (lambda e, dc=dc, tq=tq, i=i, th=th: e.matmul(
                            ps_g[:, tq * 512:(tq + 1) * 512], wgb[i][0][:, dc, :],
                            xn_fm[:, dc, th * 1024 + tq * 512: th * 1024 + (tq + 1) * 512],
                            start=(dc == 0), stop=(dc == DC - 1)))
                        for tq in range(2) for dc in range(DC)],
                        reads=(wgb[i][1], xn_fm_b), writes=(ps_g_b,))
                    p.group("tensor", [
                        (lambda e, dc=dc, tq=tq, i=i, th=th: e.matmul(
                            ps_u[:, tq * 512:(tq + 1) * 512], wub[i][0][:, dc, :],
                            xn_fm[:, dc, th * 1024 + tq * 512: th * 1024 + (tq + 1) * 512],
                            start=(dc == 0), stop=(dc == DC - 1)))
                        for tq in range(2) for dc in range(DC)],
                        reads=(wub[i][1], xn_fm_b), writes=(ps_u_b,))
                    p.op("scalar", lambda e, th=th: e.activation(
                        out=sg[0][:, th * 1024:(th + 1) * 1024], in_=ps_g[:], func=AF.Silu),
                        reads=(ps_g_b,), writes=(sg[1],))
                    p.op("vector", lambda e, hc=hc, th=th: e.tensor_tensor(
                        out=abuf[:, hc, th * 1024:(th + 1) * 1024], in0=ps_u[:],
                        in1=sg[0][:, th * 1024:(th + 1) * 1024], op=ALU.mult),
                        reads=(ps_u_b, sg[1]), writes=(abuf_b,))
                p.op("vector", lambda e, hc=hc: e.tensor_tensor(
                    out=abuf[:, hc, :], in0=abuf[:, hc, :], in1=cmbb[0][:], op=ALU.mult),
                    reads=(abuf_b, cmbb[1]), writes=(abuf_b,))
            dsrc = ins["L0_ffn_w_down"][ee] if ee < E0 else ins["L0_ffn_sw_down"]
            for half in range(2):
                p.op("sync", lambda e, dsrc=dsrc, half=half: e.dma_start(
                    out=wdf[0][:], in_=dsrc[half * 256:(half + 1) * 256, :].rearrange("(c p) d -> p c d", p=128)),
                    reads=(wdf[1],), writes=(wdf[1],), inc=16)
                p.op("vector", lambda e, half=half: e.tensor_copy(
                    out=wdb[0][:, half * 2:(half + 1) * 2, :].rearrange("p c d -> p (c d)"),
                    in_=wdf[0][:].rearrange("p c d -> p (c d)")),
                    reads=(wdf[1],), writes=(wdb[1],))
            for dc in range(DC):
                for th in range(2):
                    pd, pd_b = (ps_g, ps_g_b) if (dc * 2 + th) % 2 == 0 else (ps_u, ps_u_b)
                    p.group("tensor", [
                        (lambda e, hc=hc, tq=tq, dc=dc, pd=pd, th=th: e.matmul(
                            pd[:, tq * 512:(tq + 1) * 512], wdb[0][:, hc, ts2(dc)],
                            abuf[:, hc, th * 1024 + tq * 512: th * 1024 + (tq + 1) * 512],
                            start=(hc == 0), stop=(hc == H0C - 1)))
                        for tq in range(2) for hc in range(H0C)],
                        reads=(wdb[1], abuf_b), writes=(pd_b,))
                    p.op("vector", lambda e, dc=dc, pd=pd, th=th: e.tensor_tensor(
                        out=racc[:, dc, th * 1024:(th + 1) * 1024],
                        in0=racc[:, dc, th * 1024:(th + 1) * 1024], in1=pd[:], op=ALU.add),
                        reads=(pd_b, racc_b), writes=(racc_b,))

        x1t = (st_.enter_context(nc.sbuf_tensor(f"f0s{s}_x1t", [128, D], F32)), Buf())
        x2t = (st_.enter_context(nc.sbuf_tensor(f"f0s{s}_x2t", [128, D], F32)), Buf())
        for tt in range(TT):
            p.op("sync", lambda e, tt=tt: e.dma_start(out=x1t[0][:], in_=x1_sc[s, ts2(tt), :]),
                 reads=(x1_sc_b, x1t[1]), writes=(x1t[1],), inc=16)
            for dc in range(DC):
                p.op("tensor", lambda e, tt=tt, dc=dc: e.transpose(
                    ps_tp[:, 0:128], racc[:, dc, ts2(tt)], C["ident_f32"][0][:]),
                    reads=(racc_b, C["ident_f32"][1]), writes=(ps_tp_b,))
                p.op("vector", lambda e, dc=dc: e.tensor_tensor(
                    out=x2t[0][:, ts2(dc)], in0=ps_tp[:, 0:128], in1=x1t[0][:, ts2(dc)],
                    op=ALU.add),
                    reads=(ps_tp_b, x1t[1]), writes=(x2t[1],))
            p.op("sync", lambda e, tt=tt: e.dma_start(out=x2_sc[s, ts2(tt), :], in_=x2t[0][:]),
                 reads=(x2t[1],), writes=(x2_sc_b,), inc=16)


_NC_CACHE = {}




def emit_l1a(p, stack, ins, C, x2_sc, x2_sc_b, x3_sc, x3_sc_b):
    nc = p.nc
    with contextlib.ExitStack() as st_:
        alc = (st_.enter_context(nc.sbuf_tensor("a1_alc", [128, 1], F32)), Buf())
        abo = (st_.enter_context(nc.sbuf_tensor("a1_abo", [128, DC], F32)), Buf())
        p.op("sync", lambda e: e.dma_start(
            out=alc[0][:], in_=ins["L1_attn_alpha"].unsqueeze(0).broadcast_to((128, 1))),
            writes=(alc[1],), inc=16)
        for dc in range(DC):
            p.op("sync", lambda e, dc=dc: e.dma_start(
                out=abo[0][:, dc:dc + 1], in_=ins["L1_attn_bo"][ts2(dc)].unsqueeze(1)),
                writes=(abo[1],), inc=16)
        p.op("vector", lambda e: e.tensor_scalar(out=abo[0][:], in0=abo[0][:],
                                                 scalar1=alc[0][:, 0:1], scalar2=None, op0=ALU.mult),
             reads=(abo[1], alc[1]), writes=(abo[1],))
        brow = {}
        for nm in ("bq", "bk", "bv"):
            brow[nm] = (st_.enter_context(nc.sbuf_tensor(f"a1_{nm}b", [1, D], BF16)), Buf())
        with contextlib.ExitStack() as bs_:
            f = (bs_.enter_context(nc.sbuf_tensor("a1_bf_tmp", [1, D], F32)), Buf())
            for nm in ("bq", "bk", "bv"):
                b = brow[nm]
                p.op("sync", lambda e, f=f, nm=nm: e.dma_start(
                    out=f[0][:], in_=ins["L1_attn_" + nm].unsqueeze(0)),
                    reads=(f[1],), writes=(f[1],), inc=16)
                p.op("vector", lambda e, f=f, b=b: e.tensor_copy(out=b[0][:], in_=f[0][:]),
                     reads=(f[1],), writes=(b[1],))
            p.barrier()
        ones512 = (st_.enter_context(nc.sbuf_tensor("a1_ones512", [1, 512], BF16)), Buf())
        p.op("vector", lambda e: e.memset(ones512[0][:], 1.0), writes=(ones512[1],))
        wT_sc = nc.dram_tensor("a1_wT_sc", [4, 128, DC, D], BF16, kind="Internal").ap()
        wT_sc_b = Buf("wT_sc")
        for wi, wname in enumerate(("wq", "wk", "wv", "wo")):
            with contextlib.ExitStack() as w2_:
                wT0, wT0_b = _emit_l1a_wt(p, nc, C, w2_, wname,
                                          ins["L1_attn_" + wname], f"a1w_{wname}")
                p.op("sync", lambda e, wi=wi, wT0=wT0: e.dma_start(out=wT_sc[wi], in_=wT0[:]),
                     reads=(wT0_b,), writes=(wT_sc_b,), inc=16)
                p.barrier()
        for s in range(SEQ):
            _emit_l1a_seq(p, nc, ins, C, s, x2_sc, x2_sc_b, x3_sc, x3_sc_b, alc, abo,
                          brow, ones512, wT_sc, wT_sc_b)
        p.barrier()


def _emit_l1a_wt(p, nc, C, st_, name, src_ap, pfx):
    wT = st_.enter_context(nc.sbuf_tensor(pfx + "_wT", [128, DC, D], BF16))
    wT_b = Buf(pfx)
    wf = (st_.enter_context(nc.sbuf_tensor(pfx + "_wf", [128, D], F32)), Buf())
    wb = (st_.enter_context(nc.sbuf_tensor(pfx + "_wb", [128, D], BF16)), Buf())
    ps_tp, ps_tp_b = psum_f32(st_, nc, pfx + "_tp", 512)
    tp_out = ps_tp[:].bitcast(BF16)[:, 0:128]
    for ec in range(DC):
        p.op("sync", lambda e, ec=ec: e.dma_start(out=wf[0][:], in_=src_ap[ts2(ec), :]),
             reads=(wf[1],), writes=(wf[1],), inc=16)
        p.op("vector", lambda e: e.tensor_copy(out=wb[0][:], in_=wf[0][:]),
             reads=(wf[1],), writes=(wb[1],))
        for dc in range(DC):
            p.op("tensor", lambda e, dc=dc, a=tp_out: e.transpose(
                a, wb[0][:, ts2(dc)], C["ident_bf"][0][:]),
                reads=(wb[1], C["ident_bf"][1]), writes=(ps_tp_b,))
            p.op("vector", lambda e, dc=dc, ec=ec, a=tp_out: e.tensor_copy(
                out=wT[:, dc, ts2(ec)], in_=a),
                reads=(ps_tp_b,), writes=(wT_b,))
    return wT, wT_b


def _emit_l1a_seq(p, nc, ins, C, s, x2_sc, x2_sc_b, x3_sc, x3_sc_b, alc, abo, brow,
                  ones512, wT_sc, wT_sc_b):
    NHP = NH // 2
    with contextlib.ExitStack() as st_:
        Vt = st_.enter_context(nc.sbuf_tensor(f"a1s{s}_V", [128, TT, NH, 65], BF16))
        Vt_b = Buf("V")
        att = st_.enter_context(nc.sbuf_tensor(f"a1s{s}_att", [128, NH // 2, T], BF16))
        att_b = Buf("att")
        p.op("vector", lambda e: e.memset(Vt[:].rearrange("p t h c -> p (t h c)"), 1.0),
             writes=(Vt_b,))
        qk_stack = contextlib.ExitStack()
        Qb = qk_stack.enter_context(nc.sbuf_tensor(f"a1s{s}_Q", [128, NHP, T], BF16))
        Qb_b = Buf("Q")
        Kb = qk_stack.enter_context(nc.sbuf_tensor(f"a1s{s}_K", [128, NHP, T], BF16))
        Kb_b = Buf("K")
        with contextlib.ExitStack() as qs_:
            xn_fm = qs_.enter_context(nc.sbuf_tensor(f"l1s{s}_xnfm", [128, DC, T], BF16))
            xn_fm_b = Buf("xn_fm")
            with contextlib.ExitStack() as ls_:
                lnst = alloc_ln_scratch(p, ls_, pfx=f"l1s{s}_")
                lnst["ident_bf"] = C["ident_bf"][0]
                lnst["ident_buf"] = C["ident_bf"][1]
                lnst["xsrc_buf"] = x2_sc_b
                emit_ln_to_fm(p, lnst, x2_sc[s], xn_fm, xn_fm_b, TT)
                p.barrier()
            ps_q, ps_q_b = psum_f32(qs_, nc, f"a1s{s}_psq", T)
            wT = qs_.enter_context(nc.sbuf_tensor(f"a1s{s}_wT", [128, DC, D], BF16))
            wT_b = Buf("wT")
            for wi, (wname, bias, dst, dst_b) in enumerate((("wq", brow["bq"], Qb, Qb_b),
                                            ("wk", brow["bk"], Kb, Kb_b),
                                            ("wv", brow["bv"], Vt, Vt_b))):
                if True:
                    p.op("sync", lambda e, wi=wi: e.dma_start(out=wT[:], in_=wT_sc[wi]),
                         reads=(wT_sc_b, wT_b), writes=(wT_b,), inc=16)
                    if wname != "wv":
                        for hp in range(NHP):
                            fns = []
                            for tq in range(4):
                                for dc in range(DC):
                                    fns.append(lambda e, hp=hp, tq=tq, dc=dc, wT=wT: e.matmul(
                                        ps_q[:, tq * 512:(tq + 1) * 512], wT[:, dc, ts2(hp)],
                                        xn_fm[:, dc, tq * 512:(tq + 1) * 512],
                                        start=(dc == 0), stop=False))
                                fns.append(lambda e, hp=hp, tq=tq, bias=bias: e.matmul(
                                    ps_q[:, tq * 512:(tq + 1) * 512], bias[0][0:1, ts2(hp)],
                                    ones512[0][:], start=False, stop=True))
                            p.group("tensor", fns, reads=(wT_b, xn_fm_b, bias[1], ones512[1]),
                                    writes=(ps_q_b,))
                            p.op("scalar", lambda e, hp=hp, dst=dst, dst_b=dst_b: e.activation(
                                out=dst[:, hp, :], in_=ps_q[:], func=AF.Identity),
                                reads=(ps_q_b,), writes=(dst_b,))
                    else:
                        for tt in range(TT):
                            for eh in range(2):
                                fns = []
                                for dc in range(DC):
                                    fns.append(lambda e, tt=tt, eh=eh, dc=dc, wT=wT: e.matmul(
                                        ps_q[:, eh * 512:(eh + 1) * 512], xn_fm[:, dc, ts2(tt)],
                                        wT[:, dc, eh * 512:(eh + 1) * 512],
                                        start=(dc == 0), stop=False))
                                fns.append(lambda e, tt=tt, eh=eh: e.matmul(
                                    ps_q[:, eh * 512:(eh + 1) * 512], C["ones_row_bf"][0][:],
                                    brow["bv"][0][0:1, eh * 512:(eh + 1) * 512],
                                    start=False, stop=True))
                                p.group("tensor", fns,
                                        reads=(wT_b, xn_fm_b, brow["bv"][1], C["ones_row_bf"][1]),
                                        writes=(ps_q_b,))
                            for hh in range(0, NH, 2):
                                p.op("scalar", lambda e, tt=tt, hh=hh: e.activation(
                                    out=Vt[:, tt, hh:hh + 2, 0:64], in_=ps_q[:, hh * 64:(hh + 2) * 64],
                                    func=AF.Identity),
                                    reads=(ps_q_b,), writes=(Vt_b,))
            p.barrier()
        with contextlib.ExitStack() as as_:
            G = [(as_.enter_context(nc.sbuf_tensor(f"a1s{s}_G{i}", [128, 1024], BF16)), Buf()) for i in range(2)]
            dn = (as_.enter_context(nc.sbuf_tensor(f"a1s{s}_dn", [1, T], F32)), Buf())
            dnh = (as_.enter_context(nc.sbuf_tensor(f"a1s{s}_dnh", [1, T], BF16)), Buf())
            dnl = (as_.enter_context(nc.sbuf_tensor(f"a1s{s}_dnl", [1, T], F32)), Buf())
            dnl2 = (as_.enter_context(nc.sbuf_tensor(f"a1s{s}_dnl2", [1, T], BF16)), Buf())
            rb = (as_.enter_context(nc.sbuf_tensor(f"a1s{s}_rb", [64, T], F32)), Buf())
            ps_s2 = [psum_f32(as_, nc, f"a1s{s}_pss{i}", 1024) for i in range(2)]
            ps_o, ps_o_b = psum_f32(as_, nc, f"a1s{s}_pso", T)
            for h in range(NH):
                hp, off = h // 2, (h % 2) * 64

                def qk(kt, qh, h=h, hp=hp, off=off):
                    ps, ps_b = ps_s2[kt % 2]
                    p.group("tensor", [
                        (lambda e, kt=kt, nn=nn, qh=qh, ps=ps: e.matmul(
                            ps[:, nn * 512:(nn + 1) * 512],
                            Kb[off:off + 64, hp, ts2(kt)],
                            Qb[off:off + 64, hp, qh * 1024 + nn * 512: qh * 1024 + (nn + 1) * 512],
                            start=True, stop=True))
                        for nn in range(2)],
                        reads=(Kb_b, Qb_b), writes=(ps_b,))

                def sig(kt, qh):
                    i = kt % 2
                    ps, ps_b = ps_s2[i]
                    p.op("scalar", lambda e, i=i, ps=ps: e.activation(
                        out=G[i][0][:], in_=ps[:], func=AF.Sigmoid, scale=0.125),
                        reads=(ps_b,), writes=(G[i][1],))

                def av(kt, qh, h=h):
                    i = kt % 2
                    p.group("tensor", [
                        (lambda e, kt=kt, nn=nn, qh=qh, i=i: e.matmul(
                            ps_o[0:65, qh * 1024 + nn * 512: qh * 1024 + (nn + 1) * 512],
                            Vt[:, kt, h, :], G[i][0][:, nn * 512:(nn + 1) * 512],
                            start=(kt == 0), stop=(kt == TT - 1)))
                        for nn in range(2)],
                        reads=(Vt_b, G[i][1]), writes=(ps_o_b,))

                for qh in range(2):
                    qk(0, qh)
                    sig(0, qh)
                    for kt in range(1, TT):
                        qk(kt, qh)
                        sig(kt, qh)
                        av(kt - 1, qh)
                    av(TT - 1, qh)
                p.op("scalar", lambda e: e.activation(out=dn[0][:], in_=ps_o[64:65, :],
                                                      func=AF.Copy, bias=1e-6),
                     reads=(ps_o_b,), writes=(dn[1],))
                p.op("scalar", lambda e: e.activation(out=dn[0][:], in_=dn[0][:], func=AF.Ln),
                     reads=(dn[1],), writes=(dn[1],))
                p.op("scalar", lambda e: e.activation(out=dn[0][:], in_=dn[0][:], func=AF.Exp,
                                                      scale=-1.0),
                     reads=(dn[1],), writes=(dn[1],))
                p.op("vector", lambda e: e.tensor_copy(out=dnh[0][:], in_=dn[0][:]),
                     reads=(dn[1],), writes=(dnh[1],))
                p.op("vector", lambda e: e.tensor_tensor(out=dnl[0][:], in0=dn[0][:],
                                                         in1=dnh[0][:],
                                                         op=ALU.subtract),
                     reads=(dn[1], dnh[1]), writes=(dnl[1],))
                p.op("vector", lambda e: e.tensor_copy(out=dnl2[0][:], in_=dnl[0][:]),
                     reads=(dnl[1],), writes=(dnl2[1],))
                for half in range(2):
                    ps, ps_b = ps_s2[half]
                    p.group("tensor", [
                        (lambda e, nn=nn, half=half, ps=ps: e.matmul(
                            ps[0:64, nn * 512:(nn + 1) * 512], C["ones_row_bf"][0][0:1, 0:64],
                            dnh[0][0:1, half * 1024 + nn * 512: half * 1024 + (nn + 1) * 512],
                            start=True, stop=False))
                        for nn in range(2)] + [
                        (lambda e, nn=nn, half=half, ps=ps: e.matmul(
                            ps[0:64, nn * 512:(nn + 1) * 512], C["ones_row_bf"][0][0:1, 0:64],
                            dnl2[0][0:1, half * 1024 + nn * 512: half * 1024 + (nn + 1) * 512],
                            start=False, stop=True))
                        for nn in range(2)],
                        reads=(dnh[1], dnl2[1], C["ones_row_bf"][1]), writes=(ps_b,))
                    p.op("scalar", lambda e, half=half, ps=ps: e.activation(
                        out=rb[0][:, half * 1024:(half + 1) * 1024], in_=ps[0:64, :],
                        func=AF.Identity),
                        reads=(ps_b,), writes=(rb[1],))
                p.op("vector", lambda e, hp=hp, off=off: e.tensor_tensor(
                    out=att[off:off + 64, hp, :], in0=ps_o[0:64, :], in1=rb[0][:],
                    op=ALU.mult),
                    reads=(ps_o_b, rb[1]), writes=(att_b,))
            p.barrier()
        qk_stack.close()
        with contextlib.ExitStack() as os_:
            woT = os_.enter_context(nc.sbuf_tensor(f"a1s{s}_woT", [128, DC, D], BF16))
            woT_b = Buf("woT")
            p.op("sync", lambda e: e.dma_start(out=woT[:], in_=wT_sc[3]),
                 reads=(wT_sc_b,), writes=(woT_b,), inc=16)
            ps_c, ps_c_b = psum_f32(os_, nc, f"a1s{s}_psc", 512)
            ps_t2l = [psum_f32(os_, nc, f"a1s{s}_pst2{i}", 512) for i in range(2)]
            o_sb = (os_.enter_context(nc.sbuf_tensor(f"a1s{s}_osb", [128, 512], F32)), Buf())
            x2t = (os_.enter_context(nc.sbuf_tensor(f"a1s{s}_x2t", [128, 4, D], F32)), Buf())
            x3a = (os_.enter_context(nc.sbuf_tensor(f"a1s{s}_x3a", [128, 4, D], F32)), Buf())
            lsq = (os_.enter_context(nc.sbuf_tensor(f"a1s{s}_lsq", [128, D], F32)), Buf())
            lsm = (os_.enter_context(nc.sbuf_tensor(f"a1s{s}_lsm", [128, 1], F32)), Buf())
            lqq = (os_.enter_context(nc.sbuf_tensor(f"a1s{s}_lqq", [128, 1], F32)), Buf())
            lmn = (os_.enter_context(nc.sbuf_tensor(f"a1s{s}_lmn", [128, 1], F32)), Buf())
            lrs = (os_.enter_context(nc.sbuf_tensor(f"a1s{s}_lrs", [128, 1], F32)), Buf())
            lt1 = (os_.enter_context(nc.sbuf_tensor(f"a1s{s}_lt1", [128, 1], F32)), Buf())
            for tq in range(4):
                p.op("sync", lambda e, tq=tq: e.dma_start(
                    out=x2t[0][:], in_=x2_sc[s, tq * 512:(tq + 1) * 512, :].rearrange("(b p) d -> p b d", p=128)),
                    reads=(x2_sc_b, x2t[1]), writes=(x2t[1],), inc=16)
                for b in range(4):
                    p.op("vector", lambda e, b=b: e.reduce_sum(
                        out=lsm[0][:], in_=x2t[0][:, b, :], axis=AX.X),
                        reads=(x2t[1],), writes=(lsm[1],))
                    p.op("scalar", lambda e, b=b: e.activation(
                        out=lsq[0][:], in_=x2t[0][:, b, :], func=AF.Square, accum_out=lqq[0][:]),
                        reads=(x2t[1],), writes=(lsq[1], lqq[1]))
                    p.op("vector", lambda e: e.tensor_scalar(
                        out=lmn[0][:], in0=lsm[0][:], scalar1=1.0 / D, scalar2=None, op0=ALU.mult),
                        reads=(lsm[1],), writes=(lmn[1],))
                    p.op("vector", lambda e: e.tensor_tensor(
                        out=lt1[0][:], in0=lmn[0][:], in1=lmn[0][:], op=ALU.mult),
                        reads=(lmn[1],), writes=(lt1[1],))
                    p.op("vector", lambda e: e.tensor_scalar(
                        out=lt1[0][:], in0=lt1[0][:], scalar1=-1.0, scalar2=1e-5,
                        op0=ALU.mult, op1=ALU.add),
                        reads=(lt1[1],), writes=(lt1[1],))
                    p.op("vector", lambda e: e.scalar_tensor_tensor(
                        out=lt1[0][:], in0=lqq[0][:], scalar=1.0 / D, in1=lt1[0][:],
                        op0=ALU.mult, op1=ALU.add),
                        reads=(lqq[1], lt1[1]), writes=(lt1[1],))
                    p.op("scalar", lambda e: e.activation(out=lt1[0][:], in_=lt1[0][:], func=AF.Ln),
                         reads=(lt1[1],), writes=(lt1[1],))
                    p.op("scalar", lambda e: e.activation(out=lrs[0][:], in_=lt1[0][:],
                                                          func=AF.Exp, scale=-0.5),
                         reads=(lt1[1],), writes=(lrs[1],))
                    p.op("vector", lambda e, b=b: e.tensor_scalar(
                        out=lsq[0][:], in0=x2t[0][:, b, :], scalar1=lmn[0][:, 0:1],
                        scalar2=None, op0=ALU.subtract),
                        reads=(x2t[1], lmn[1]), writes=(lsq[1],))
                    p.op("vector", lambda e: e.tensor_scalar(
                        out=lsq[0][:], in0=lsq[0][:], scalar1=lrs[0][:, 0:1],
                        scalar2=None, op0=ALU.mult),
                        reads=(lsq[1], lrs[1]), writes=(lsq[1],))
                    p.op("vector", lambda e, b=b: e.tensor_tensor(
                        out=x2t[0][:, b, :], in0=x2t[0][:, b, :], in1=lsq[0][:], op=ALU.add),
                        reads=(x2t[1], lsq[1]), writes=(x2t[1],))
                for dc in range(DC):
                    p.group("tensor", [
                        (lambda e, tq=tq, dc=dc, dci=dci: e.matmul(
                            ps_c[:], woT[:, dci, ts2(dc)],
                            att[:, dci, tq * 512:(tq + 1) * 512],
                            start=(dci == 0), stop=(dci == DC - 1)))
                        for dci in range(DC)],
                        reads=(woT_b, att_b), writes=(ps_c_b,))
                    p.op("scalar", lambda e, dc=dc: e.activation(
                        out=o_sb[0][:], in_=ps_c[:], func=AF.Identity,
                        scale=alc[0][:, 0:1], bias=abo[0][:, dc:dc + 1]),
                        reads=(ps_c_b, alc[1], abo[1]), writes=(o_sb[1],))
                    for b in range(4):
                        ps_t2, ps_t2_b = ps_t2l[b % 2]
                        p.op("tensor", lambda e, b=b, ps_t2=ps_t2: e.transpose(
                            ps_t2[:, 0:128], o_sb[0][:, ts2(b)], C["ident_f32"][0][:]),
                            reads=(o_sb[1], C["ident_f32"][1]), writes=(ps_t2_b,))
                        p.op("vector", lambda e, b=b, dc=dc, ps_t2=ps_t2: e.tensor_tensor(
                            out=x3a[0][:, b, ts2(dc)], in0=ps_t2[:, 0:128],
                            in1=x2t[0][:, b, ts2(dc)], op=ALU.add),
                            reads=(ps_t2_b, x2t[1]), writes=(x3a[1],))
                p.op("sync", lambda e, tq=tq: e.dma_start(
                    out=x3_sc[s, tq * 512:(tq + 1) * 512, :].rearrange("(b p) d -> p b d", p=128),
                    in_=x3a[0][:]),
                    reads=(x3a[1],), writes=(x3_sc_b,), inc=16)
            p.barrier()


def emit_l1f(p, stack, ins, C, S):
    nc = p.nc
    x3_sc, x3_sc_b = S["x3_sc"]
    xn3_sc, xn3_sc_b = S["xn3_sc"]
    cin_sc, cin_sc_b = S["cin_sc"]
    cout_sc, cout_sc_b = S["cout_sc"]
    gidx_sc, gidx_sc_b = S["gidx_sc"]
    out_d, out_b = S["out"]
    with contextlib.ExitStack() as st_:
        gwT = st_.enter_context(nc.sbuf_tensor("f1_gwT", [128, DC, E1], BF16))
        gwT_b = Buf("f1gwT")
        with contextlib.ExitStack() as gs_:
            gwf = (gs_.enter_context(nc.sbuf_tensor("f1_gwf", [E1, D], F32)), Buf())
            gwb = (gs_.enter_context(nc.sbuf_tensor("f1_gwb", [128, D], BF16)), Buf())
            ps_tp, ps_tp_b = psum_f32(gs_, nc, "f1_pstp", 512)
            tpo = ps_tp[:].bitcast(BF16)[:, 0:128]
            tpi = ps_tp[:].bitcast(BF16)[:, 0:E1]
            p.op("sync", lambda e: e.dma_start(out=gwf[0][:], in_=ins["L1_ffn_gate_w"]),
                 writes=(gwf[1],), inc=16)
            p.op("vector", lambda e: e.memset(gwb[0][:], 0.0), writes=(gwb[1],))
            p.op("vector", lambda e: e.tensor_copy(out=gwb[0][0:E1, :], in_=gwf[0][:]),
                 reads=(gwf[1],), writes=(gwb[1],))
            for dc in range(DC):
                p.op("tensor", lambda e, dc=dc, a=tpo: e.transpose(
                    a, gwb[0][:, ts2(dc)], C["ident_bf"][0][:]),
                    reads=(gwb[1], C["ident_bf"][1]), writes=(ps_tp_b,))
                p.op("vector", lambda e, dc=dc, a=tpi: e.tensor_copy(out=gwT[:, dc, :], in_=a),
                     reads=(ps_tp_b,), writes=(gwT_b,))
            p.barrier()
        for s in range(SEQ):
            _emit_l1f_pre(p, nc, ins, C, s, gwT, gwT_b, x3_sc, x3_sc_b, xn3_sc, xn3_sc_b,
                          cin_sc, cin_sc_b, gidx_sc, gidx_sc_b)
        _emit_l1f_experts(p, nc, ins, C, cin_sc, cin_sc_b, cout_sc, cout_sc_b, S["wtp"])
        for s in range(SEQ):
            _emit_l1f_final(p, nc, C, s, x3_sc, x3_sc_b, cout_sc, cout_sc_b,
                            gidx_sc, gidx_sc_b, out_d, out_b)
        p.barrier()


def _emit_l1f_pre(p, nc, ins, C, s, gwT, gwT_b, x3_sc, x3_sc_b, xn3_sc, xn3_sc_b,
                  cin_sc, cin_sc_b, gidx_sc, gidx_sc_b):
    with contextlib.ExitStack() as st_:
        xn_fm = st_.enter_context(nc.sbuf_tensor(f"f1s{s}_xnfm", [128, DC, T], BF16))
        xn_fm_b = Buf("xn_fm")
        with contextlib.ExitStack() as ls_:
            lnst = alloc_ln_scratch(p, ls_, pfx=f"f1s{s}_")
            lnst["ident_bf"] = C["ident_bf"][0]
            lnst["ident_buf"] = C["ident_bf"][1]
            lnst["xsrc_buf"] = x3_sc_b
            emit_ln_to_fm(p, lnst, x3_sc[s], xn_fm, xn_fm_b, TT,
                          xn_tm_sink=lambda tt: (xn3_sc[s, ts2(tt), :], xn3_sc_b))
            p.barrier()
        ps_a, ps_a_b = psum_f32(st_, nc, f"f1s{s}_psa", 1024)
        ps_tp, ps_tp_b = psum_f32(st_, nc, f"f1s{s}_pstp", 512)
        lg = (st_.enter_context(nc.sbuf_tensor(f"f1s{s}_lg", [128, T], F32)), Buf())
        aff = (st_.enter_context(nc.sbuf_tensor(f"f1s{s}_aff", [128, T], F32)), Buf())
        at = (st_.enter_context(nc.sbuf_tensor(f"f1s{s}_at", [128, E1], F32)), Buf())
        rmx = (st_.enter_context(nc.sbuf_tensor(f"f1s{s}_rmx", [128, 1], F32)), Buf())
        p.op("vector", lambda e: e.memset(lg[0][:], 0.0), writes=(lg[1],))
        p.op("vector", lambda e: e.memset(aff[0][:], 0.0), writes=(aff[1],))
        for th in range(2):
            p.group("tensor", [
                (lambda e, dc=dc, tq=tq, th=th: e.matmul(
                    ps_a[0:E1, tq * 512:(tq + 1) * 512], gwT[:, dc, :],
                    xn_fm[:, dc, th * 1024 + tq * 512: th * 1024 + (tq + 1) * 512],
                    start=(dc == 0), stop=(dc == DC - 1)))
                for tq in range(2) for dc in range(DC)],
                reads=(gwT_b, xn_fm_b), writes=(ps_a_b,))
            p.op("scalar", lambda e, th=th: e.activation(
                out=lg[0][0:E1, th * 1024:(th + 1) * 1024], in_=ps_a[0:E1, :], func=AF.Identity),
                reads=(ps_a_b,), writes=(lg[1],))
        for tt in range(TT):
            p.op("tensor", lambda e, tt=tt: e.transpose(
                ps_tp[:, 0:128], lg[0][:, ts2(tt)], C["ident_f32"][0][:]),
                reads=(lg[1], C["ident_f32"][1]), writes=(ps_tp_b,))
            p.op("vector", lambda e: e.reduce_max(out=rmx[0][:], in_=ps_tp[:, 0:E1], axis=AX.X),
                 reads=(ps_tp_b,), writes=(rmx[1],))
            p.op("vector", lambda e: e.tensor_scalar(out=at[0][:], in0=ps_tp[:, 0:E1],
                                                     scalar1=rmx[0][:, 0:1], scalar2=None,
                                                     op0=ALU.subtract),
                 reads=(ps_tp_b, rmx[1]), writes=(at[1],))
            p.op("scalar", lambda e: e.activation(out=at[0][:], in_=at[0][:], func=AF.Exp),
                 reads=(at[1],), writes=(at[1],))
            p.op("vector", lambda e: e.reduce_sum(out=rmx[0][:], in_=at[0][:], axis=AX.X),
                 reads=(at[1],), writes=(rmx[1],))
            p.op("vector", lambda e: e.reciprocal(out=rmx[0][:], in_=rmx[0][:]),
                 reads=(rmx[1],), writes=(rmx[1],))
            p.op("vector", lambda e: e.tensor_scalar(out=at[0][:], in0=at[0][:],
                                                     scalar1=rmx[0][:, 0:1], scalar2=None,
                                                     op0=ALU.mult),
                 reads=(at[1], rmx[1]), writes=(at[1],))
            p.op("tensor", lambda e, tt=tt: e.transpose(
                ps_tp[0:E1, 0:128], at[0][:], C["ident_f32"][0][:]),
                reads=(at[1], C["ident_f32"][1]), writes=(ps_tp_b,))
            p.op("scalar", lambda e, tt=tt: e.activation(
                out=aff[0][0:E1, ts2(tt)], in_=ps_tp[0:E1, 0:128], func=AF.Identity),
                reads=(ps_tp_b,), writes=(aff[1],))
        lo = (st_.enter_context(nc.sbuf_tensor(f"f1s{s}_lo", [E1, 1], F32)), Buf())
        hi = (st_.enter_context(nc.sbuf_tensor(f"f1s{s}_hi", [E1, 1], F32)), Buf())
        mid = (st_.enter_context(nc.sbuf_tensor(f"f1s{s}_mid", [E1, 1], F32)), Buf())
        cnt = (st_.enter_context(nc.sbuf_tensor(f"f1s{s}_cnt", [E1, 1], F32)), Buf())
        sel = (st_.enter_context(nc.sbuf_tensor(f"f1s{s}_sel", [E1, 1], F32)), Buf())
        tmp = (st_.enter_context(nc.sbuf_tensor(f"f1s{s}_tmp", [E1, 1], F32)), Buf())
        cbuf = (st_.enter_context(nc.sbuf_tensor(f"f1s{s}_cbuf", [E1, T], F32)), Buf())
        p.op("vector", lambda e: e.memset(lo[0][:], 0.0), writes=(lo[1],))
        p.op("vector", lambda e: e.memset(hi[0][:], 1.0), writes=(hi[1],))
        for it in range(44):
            p.op("vector", lambda e: e.tensor_tensor(out=mid[0][:], in0=lo[0][:], in1=hi[0][:],
                                                     op=ALU.add),
                 reads=(lo[1], hi[1]), writes=(mid[1],))
            p.op("vector", lambda e: e.tensor_scalar(out=mid[0][:], in0=mid[0][:], scalar1=0.5,
                                                     scalar2=None, op0=ALU.mult),
                 reads=(mid[1],), writes=(mid[1],))
            p.op("vector", lambda e: e.tensor_scalar(
                out=cbuf[0][:], in0=aff[0][0:E1, :], scalar1=mid[0][:, 0:1], scalar2=0.0,
                op0=ALU.is_gt, op1=ALU.add, accum_out=cnt[0][:]),
                reads=(aff[1], mid[1]), writes=(cbuf[1], cnt[1]))
            p.op("vector", lambda e: e.tensor_scalar(out=sel[0][:], in0=cnt[0][:], scalar1=511.5,
                                                     scalar2=None, op0=ALU.is_gt),
                 reads=(cnt[1],), writes=(sel[1],))
            p.op("vector", lambda e: e.tensor_tensor(out=tmp[0][:], in0=mid[0][:], in1=lo[0][:],
                                                     op=ALU.subtract),
                 reads=(mid[1], lo[1]), writes=(tmp[1],))
            p.op("vector", lambda e: e.tensor_tensor(out=tmp[0][:], in0=tmp[0][:], in1=sel[0][:],
                                                     op=ALU.mult),
                 reads=(tmp[1], sel[1]), writes=(tmp[1],))
            p.op("vector", lambda e: e.tensor_tensor(out=lo[0][:], in0=lo[0][:], in1=tmp[0][:],
                                                     op=ALU.add),
                 reads=(lo[1], tmp[1]), writes=(lo[1],))
            p.op("vector", lambda e: e.tensor_tensor(out=tmp[0][:], in0=mid[0][:], in1=hi[0][:],
                                                     op=ALU.subtract),
                 reads=(mid[1], hi[1]), writes=(tmp[1],))
            p.op("vector", lambda e: e.tensor_scalar(out=sel[0][:], in0=sel[0][:], scalar1=-1.0,
                                                     scalar2=1.0, op0=ALU.mult, op1=ALU.add),
                 reads=(sel[1],), writes=(sel[1],))
            p.op("vector", lambda e: e.tensor_tensor(out=tmp[0][:], in0=tmp[0][:], in1=sel[0][:],
                                                     op=ALU.mult),
                 reads=(tmp[1], sel[1]), writes=(tmp[1],))
            p.op("vector", lambda e: e.tensor_tensor(out=hi[0][:], in0=hi[0][:], in1=tmp[0][:],
                                                     op=ALU.add),
                 reads=(hi[1], tmp[1]), writes=(hi[1],))
        msk = (st_.enter_context(nc.sbuf_tensor(f"f1s{s}_msk", [E1, T], F32)), Buf())
        gat = (st_.enter_context(nc.sbuf_tensor(f"f1s{s}_gat", [128, T], F32)), Buf())
        rnk = (st_.enter_context(nc.sbuf_tensor(f"f1s{s}_rnk", [E1, T], F32)), Buf())
        isc = (st_.enter_context(nc.sbuf_tensor(f"f1s{s}_isc", [128, T], F32)), Buf())
        ig = (st_.enter_context(nc.sbuf_tensor(f"f1s{s}_ig", [128, T], F32)), Buf())
        onec = (st_.enter_context(nc.sbuf_tensor(f"f1s{s}_onec", [E1, 1], F32)), Buf())
        p.op("vector", lambda e: e.memset(onec[0][:], 1.0), writes=(onec[1],))
        p.op("vector", lambda e: e.memset(gat[0][:], 0.0), writes=(gat[1],))
        p.op("vector", lambda e: e.memset(isc[0][:], 0.0), writes=(isc[1],))
        p.op("vector", lambda e: e.memset(ig[0][:], 0.0), writes=(ig[1],))
        p.op("vector", lambda e: e.tensor_scalar(out=msk[0][:], in0=aff[0][0:E1, :],
                                                 scalar1=lo[0][:, 0:1], scalar2=None, op0=ALU.is_gt),
             reads=(aff[1], lo[1]), writes=(msk[1],))
        p.op("vector", lambda e: e.tensor_tensor(out=gat[0][0:E1, :], in0=aff[0][0:E1, :],
                                                 in1=msk[0][:], op=ALU.mult),
             reads=(aff[1], msk[1]), writes=(gat[1],))
        p.op("vector", lambda e: e.tensor_tensor_scan(
            out=rnk[0][:], data0=onec[0][:, 0:1].broadcast_to((E1, T)), data1=msk[0][:],
            initial=0.0, op0=ALU.mult, op1=ALU.add),
            reads=(msk[1], onec[1]), writes=(rnk[1],))
        p.op("vector", lambda e: e.tensor_tensor(out=rnk[0][:], in0=rnk[0][:], in1=msk[0][:],
                                                 op=ALU.subtract),
             reads=(rnk[1], msk[1]), writes=(rnk[1],))
        p.op("vector", lambda e: e.tensor_tensor(out=ig[0][0:E1, :], in0=rnk[0][:], in1=msk[0][:],
                                                 op=ALU.mult),
             reads=(rnk[1], msk[1]), writes=(ig[1],))
        p.op("vector", lambda e: e.tensor_scalar(out=msk[0][:], in0=msk[0][:], scalar1=-4096.0,
                                                 scalar2=4096.0, op0=ALU.mult, op1=ALU.add),
             reads=(msk[1],), writes=(msk[1],))
        p.op("vector", lambda e: e.tensor_tensor(out=isc[0][0:E1, :], in0=rnk[0][:], in1=msk[0][:],
                                                 op=ALU.add),
             reads=(rnk[1], msk[1]), writes=(isc[1],))
        p.op("sync", lambda e: e.dma_start(out=gidx_sc[s, 0], in_=isc[0][0:E1, :]),
             reads=(isc[1],), writes=(gidx_sc_b,), inc=16)
        p.op("sync", lambda e: e.dma_start(out=gidx_sc[s, 1], in_=gat[0][0:E1, :]),
             reads=(gat[1],), writes=(gidx_sc_b,), inc=16)
        isct = (st_.enter_context(nc.sbuf_tensor(f"f1s{s}_isct", [128, TT, E1], F32)), Buf())
        for tt in range(TT):
            p.op("tensor", lambda e, tt=tt: e.transpose(
                ps_tp[:, 0:128], isc[0][:, ts2(tt)], C["ident_f32"][0][:]),
                reads=(isc[1], C["ident_f32"][1]), writes=(ps_tp_b,))
            p.op("vector", lambda e, tt=tt: e.tensor_copy(
                out=isct[0][:, tt, :], in_=ps_tp[:, 0:E1]),
                reads=(ps_tp_b,), writes=(isct[1],))
        xna = (st_.enter_context(nc.sbuf_tensor(f"f1s{s}_xna", [128, TT, D], BF16)), Buf())
        for tt in range(TT):
            p.op("sync", lambda e, tt=tt: e.dma_start(
                out=xna[0][:, tt, :], in_=xn3_sc[s, ts2(tt), :]),
                reads=(xn3_sc_b,), writes=(xna[1],), inc=16)
        shc = (st_.enter_context(nc.sbuf_tensor(f"f1s{s}_shc", [128, 1], F32)), Buf())
        pj = (st_.enter_context(nc.sbuf_tensor(f"f1s{s}_pj", [128, 128], BF16)), Buf())
        cinb = (st_.enter_context(nc.sbuf_tensor(f"f1s{s}_cinb", [128, D], BF16)), Buf())
        ps_cp, ps_cp_b = psum_f32(st_, nc, f"f1s{s}_pscp", 1024)
        for ee in range(E1):
            for jt in range(4):
                for tt in range(TT):
                    p.op("vector", lambda e, ee=ee, jt=jt, tt=tt: e.tensor_scalar(
                        out=shc[0][:], in0=isct[0][:, tt, ee:ee + 1], scalar1=float(-128 * jt),
                        scalar2=None, op0=ALU.add),
                        reads=(isct[1],), writes=(shc[1],))
                    p.op("vector", lambda e: e.tensor_scalar(
                        out=pj[0][:], in0=C["iota_row_bc"][0][:], scalar1=shc[0][:, 0:1],
                        scalar2=None, op0=ALU.is_equal),
                        reads=(shc[1], C["iota_row_bc"][1]), writes=(pj[1],))
                    p.group("tensor", [
                        (lambda e, tt=tt, nn=nn: e.matmul(
                            ps_cp[:, nn * 512:(nn + 1) * 512], pj[0][:],
                            xna[0][:, tt, nn * 512:(nn + 1) * 512],
                            start=(tt == 0), stop=(tt == TT - 1)))
                        for nn in range(2)],
                        reads=(pj[1], xna[1]), writes=(ps_cp_b,))
                p.op("scalar", lambda e: e.activation(out=cinb[0][:], in_=ps_cp[:], func=AF.Identity),
                     reads=(ps_cp_b,), writes=(cinb[1],))
                p.op("sync", lambda e, ee=ee, jt=jt: e.dma_start(
                    out=cin_sc[s][ee][ts2(jt), :], in_=cinb[0][:]),
                    reads=(cinb[1],), writes=(cin_sc_b[s][ee],), inc=16)
        p.barrier()


def _emit_l1f_experts(p, nc, ins, C, cin_sc, cin_sc_b, cout_sc, cout_sc_b, wtp):
    (gsc, gsc_b), (usc, usc_b), (dsc, dsc_b) = wtp
    IC = I1 // 128
    with contextlib.ExitStack() as st_:
        cfm = st_.enter_context(nc.sbuf_tensor("f1e_cfm", [128, DC, T], BF16))
        cfm_hb = [Buf("cfm_h0"), Buf("cfm_h1")]
        cts = [(st_.enter_context(nc.sbuf_tensor("f1e_cts0", [128, 4, D], BF16)), Buf())]
        abuf = st_.enter_context(nc.sbuf_tensor("f1e_abuf", [128, IC, 1024], BF16))
        abuf_b = Buf("abuf")
        wgT2 = [(st_.enter_context(nc.sbuf_tensor(f"f1e_wgT{i}", [128, DC, 128], BF16)), Buf())
                for i in range(2)]
        wuT2 = [(st_.enter_context(nc.sbuf_tensor(f"f1e_wuT{i}", [128, DC, 128], BF16)), Buf())
                for i in range(2)]
        sg = (st_.enter_context(nc.sbuf_tensor("f1e_sg", [128, 1024], F32)), Buf())
        wdT2 = [(st_.enter_context(nc.sbuf_tensor(f"f1e_wdT{i}", [128, IC, 128], BF16)), Buf())
                for i in range(2)]
        osb = (st_.enter_context(nc.sbuf_tensor("f1e_osb", [128, 1024], F32)), Buf())
        cot = (st_.enter_context(nc.sbuf_tensor("f1e_cot", [128, 8, D], BF16)), Buf())
        ps_g, ps_g_b = psum_f32(st_, nc, "f1e_psg", 1024)
        ps_u, ps_u_b = psum_f32(st_, nc, "f1e_psu", 1024)
        ps_tp = [psum_f32(st_, nc, f"f1e_pstp{i}", 512) for i in range(2)]
        tctr = [0]

        def tp128(src_ap, src_bufs, dst_ap, dst_buf, bf16=True):
            pt, pt_b = ps_tp[tctr[0] % 2]
            tctr[0] += 1
            a = pt[:].bitcast(BF16)[:, 0:128] if bf16 else pt[:, 0:128]
            idn = C["ident_bf"] if bf16 else C["ident_f32"]
            p.op("tensor", lambda e, a=a, src_ap=src_ap, idn=idn: e.transpose(a, src_ap, idn[0][:]),
                 reads=tuple(src_bufs) + (idn[1],), writes=(pt_b,))
            p.op("vector", lambda e, a=a, dst_ap=dst_ap: e.tensor_copy(out=dst_ap, in_=a),
                 reads=(pt_b,), writes=(dst_buf,))

        def gather(ee, half):
            for s in (2 * half, 2 * half + 1):
                ct, ct_b = cts[0]
                p.op("gpsimd", lambda e, s=s, ee=ee, ct=ct: e.dma_start(
                    out=ct[:], in_=cin_sc[s][ee].rearrange("(b p) d -> p b d", p=128)),
                    reads=(cin_sc_b[s][ee], ct_b), writes=(ct_b,), inc=16)
                for blk in range(4):
                    tb = s * 4 + blk
                    for dc in range(DC):
                        tp128(ct[:, blk, ts2(dc)], (ct_b,), cfm[:, dc, ts2(tb)], cfm_hb[half])

        def load_gu(ee, ic):
            i = ic % 2
            for (t, tb), sc, scb in ((wgT2[i], gsc, gsc_b), (wuT2[i], usc, usc_b)):
                p.op("sync", lambda e, t=t, sc=sc, ic=ic, ee=ee: e.dma_start(
                    out=t[:].rearrange("p c d -> p (c d)"),
                    in_=sc[ee, ic].rearrange("p c d -> p (c d)")),
                    reads=(scb, tb), writes=(tb,), inc=16)

        gather(0, 0)
        gather(0, 1)
        for ee in range(E1):
            for kh in range(2):
                load_gu(ee, 0)
                load_gu(ee, 1)
                for ic in range(IC):
                    i = ic % 2
                    for (wt, wtb), pg, pg_b in ((wgT2[i], ps_g, ps_g_b), (wuT2[i], ps_u, ps_u_b)):
                        p.group("tensor", [
                            (lambda e, dc=dc, tq=tq, kh=kh, wt=wt, pg=pg: e.matmul(
                                pg[:, tq * 512:(tq + 1) * 512], wt[:, dc, :],
                                cfm[:, dc, kh * 1024 + tq * 512: kh * 1024 + (tq + 1) * 512],
                                start=(dc == 0), stop=(dc == DC - 1)))
                            for tq in range(2) for dc in range(DC)],
                            reads=(wtb, cfm_hb[kh]), writes=(pg_b,))
                    p.op("scalar", lambda e: e.activation(out=sg[0][:], in_=ps_g[:], func=AF.Silu),
                         reads=(ps_g_b,), writes=(sg[1],))
                    p.op("vector", lambda e, ic=ic: e.tensor_tensor(
                        out=abuf[:, ic, :], in0=ps_u[:], in1=sg[0][:], op=ALU.mult),
                        reads=(ps_u_b, sg[1]), writes=(abuf_b,))
                    if ic + 2 < IC:
                        load_gu(ee, ic + 2)
                if ee + 1 < E1:
                    gather(ee + 1, kh)
                def load_wd(dc):
                    t, tb = wdT2[dc % 2]
                    p.op("sync", lambda e, dc=dc, ee=ee, t=t: e.dma_start(
                        out=t[:], in_=dsc[ee, dc].rearrange("c p d -> p c d")),
                        reads=(dsc_b, tb), writes=(tb,), inc=16)

                load_wd(0)
                for dc in range(DC):
                    wdT, wdT_b = wdT2[dc % 2]
                    if dc + 1 < DC:
                        load_wd(dc + 1)
                    p.group("tensor", [
                        (lambda e, ic=ic, tq=tq, wdT=wdT: e.matmul(
                            ps_g[:, tq * 512:(tq + 1) * 512], wdT[:, ic, :],
                            abuf[:, ic, tq * 512:(tq + 1) * 512],
                            start=(ic == 0), stop=(ic == IC - 1)))
                        for tq in range(2) for ic in range(IC)],
                        reads=(wdT_b, abuf_b), writes=(ps_g_b,))
                    p.op("scalar", lambda e: e.activation(out=osb[0][:], in_=ps_g[:],
                                                          func=AF.Identity),
                         reads=(ps_g_b,), writes=(osb[1],))
                    for blk in range(8):
                        tp128(osb[0][:, ts2(blk)], (osb[1],), cot[0][:, blk, ts2(dc)], cot[1],
                              bf16=False)
                for half in range(2):
                    s_out = kh * 2 + half
                    p.op("gpsimd", lambda e, s_out=s_out, half=half, ee=ee: e.dma_start(
                        out=cout_sc[s_out][ee].rearrange("(b p) d -> p b d", p=128),
                        in_=cot[0][:, half * 4:(half + 1) * 4, :]),
                        reads=(cot[1],), writes=(cout_sc_b[s_out][ee],), inc=16)
        p.barrier()

def _emit_l1f_final(p, nc, C, s, x3_sc, x3_sc_b, cout_sc, cout_sc_b,
                    gidx_sc, gidx_sc_b, out_d, out_b):
    with contextlib.ExitStack() as st_:
        acc = st_.enter_context(nc.sbuf_tensor(f"f1fs{s}_acc", [128, TT, D], F32))
        acc_b = Buf("acc")
        irow = (st_.enter_context(nc.sbuf_tensor(f"f1fs{s}_irow", [1, T], F32)), Buf())
        irh = (st_.enter_context(nc.sbuf_tensor(f"f1fs{s}_irh", [1, T], BF16)), Buf())
        irl = (st_.enter_context(nc.sbuf_tensor(f"f1fs{s}_irl", [1, T], F32)), Buf())
        irl2 = (st_.enter_context(nc.sbuf_tensor(f"f1fs{s}_irl2", [1, T], BF16)), Buf())
        grow = (st_.enter_context(nc.sbuf_tensor(f"f1fs{s}_grow", [1, T], BF16)), Buf())
        iscb = (st_.enter_context(nc.sbuf_tensor(f"f1fs{s}_iscb", [128, T], F32)), Buf())
        gatb = (st_.enter_context(nc.sbuf_tensor(f"f1fs{s}_gatb", [128, T], BF16)), Buf())
        shc = (st_.enter_context(nc.sbuf_tensor(f"f1fs{s}_shc", [128, 1], F32)), Buf())
        eq = (st_.enter_context(nc.sbuf_tensor(f"f1fs{s}_eq", [128, T], BF16)), Buf())
        Pg = [(st_.enter_context(nc.sbuf_tensor(f"f1fs{s}_Pg{j}", [128, T], BF16)), Buf())
              for j in range(4)]
        cob = (st_.enter_context(nc.sbuf_tensor(f"f1fs{s}_cob", [128, 4, D], BF16)), Buf())
        ps_b, ps_b_b = psum_f32(st_, nc, f"f1fs{s}_psb", T)
        ps_x, ps_x_b = psum_f32(st_, nc, f"f1fs{s}_psx", 1024)
        for tt in range(TT):
            p.op("sync", lambda e, tt=tt: e.dma_start(out=acc[:, tt, :], in_=x3_sc[s, ts2(tt), :]),
                 reads=(x3_sc_b,), writes=(acc_b,), inc=16)
        for ee in range(E1):
            p.op("sync", lambda e, ee=ee: e.dma_start(out=irow[0][:], in_=gidx_sc[s, 0, ee].unsqueeze(0)),
                 reads=(gidx_sc_b, irow[1]), writes=(irow[1],), inc=16)
            p.op("vector", lambda e: e.tensor_copy(out=irh[0][:], in_=irow[0][:]),
                 reads=(irow[1],), writes=(irh[1],))
            p.op("vector", lambda e: e.tensor_tensor(out=irl[0][:], in0=irow[0][:], in1=irh[0][:],
                                                     op=ALU.subtract),
                 reads=(irow[1], irh[1]), writes=(irl[1],))
            p.op("vector", lambda e: e.tensor_copy(out=irl2[0][:], in_=irl[0][:]),
                 reads=(irl[1],), writes=(irl2[1],))
            p.group("tensor", [
                (lambda e, nn=nn: e.matmul(
                    ps_b[:, nn * 512:(nn + 1) * 512], C["ones_row_bf"][0][:],
                    irh[0][0:1, nn * 512:(nn + 1) * 512], start=True, stop=False))
                for nn in range(4)] + [
                (lambda e, nn=nn: e.matmul(
                    ps_b[:, nn * 512:(nn + 1) * 512], C["ones_row_bf"][0][:],
                    irl2[0][0:1, nn * 512:(nn + 1) * 512], start=False, stop=True))
                for nn in range(4)],
                reads=(irh[1], irl2[1], C["ones_row_bf"][1]), writes=(ps_b_b,))
            p.op("vector", lambda e: e.tensor_copy(out=iscb[0][:], in_=ps_b[:]),
                 reads=(ps_b_b,), writes=(iscb[1],))
            p.op("sync", lambda e, ee=ee: e.dma_start(out=irow[0][:], in_=gidx_sc[s, 1, ee].unsqueeze(0)),
                 reads=(gidx_sc_b, irow[1]), writes=(irow[1],), inc=16)
            p.op("vector", lambda e: e.tensor_copy(out=grow[0][:], in_=irow[0][:]),
                 reads=(irow[1],), writes=(grow[1],))
            p.group("tensor", [
                (lambda e, nn=nn: e.matmul(
                    ps_b[:, nn * 512:(nn + 1) * 512], C["ones_row_bf"][0][:],
                    grow[0][0:1, nn * 512:(nn + 1) * 512], start=True, stop=True))
                for nn in range(4)],
                reads=(grow[1], C["ones_row_bf"][1]), writes=(ps_b_b,))
            p.op("scalar", lambda e: e.activation(out=gatb[0][:], in_=ps_b[:], func=AF.Identity),
                 reads=(ps_b_b,), writes=(gatb[1],))
            for jt in range(4):
                p.op("vector", lambda e, jt=jt: e.tensor_scalar(
                    out=shc[0][:], in0=C["iota_pcol"][0][:], scalar1=float(128 * jt),
                    scalar2=None, op0=ALU.add),
                    reads=(C["iota_pcol"][1],), writes=(shc[1],))
                p.op("vector", lambda e, jt=jt: e.tensor_scalar(
                    out=eq[0][:], in0=iscb[0][:], scalar1=shc[0][:, 0:1], scalar2=None,
                    op0=ALU.is_equal),
                    reads=(iscb[1], shc[1]), writes=(eq[1],))
                p.op("vector", lambda e, jt=jt: e.tensor_tensor(
                    out=Pg[jt][0][:], in0=eq[0][:], in1=gatb[0][:], op=ALU.mult),
                    reads=(eq[1], gatb[1]), writes=(Pg[jt][1],))
            p.op("sync", lambda e, ee=ee: e.dma_start(
                out=cob[0][:], in_=cout_sc[s][ee].rearrange("(b p) d -> p b d", p=128)),
                reads=(cout_sc_b[s][ee], cob[1]), writes=(cob[1],), inc=16)
            for tt in range(TT):
                p.group("tensor", [
                    (lambda e, tt=tt, jt=jt, nn=nn: e.matmul(
                        ps_x[:, nn * 512:(nn + 1) * 512], Pg[jt][0][:, ts2(tt)],
                        cob[0][:, jt, nn * 512:(nn + 1) * 512],
                        start=(jt == 0), stop=(jt == 3)))
                    for jt in range(4) for nn in range(2)],
                    reads=(Pg[0][1], Pg[1][1], Pg[2][1], Pg[3][1], cob[1]), writes=(ps_x_b,))
                p.op("vector", lambda e, tt=tt: e.tensor_tensor(
                    out=acc[:, tt, :], in0=acc[:, tt, :], in1=ps_x[:], op=ALU.add),
                    reads=(ps_x_b, acc_b), writes=(acc_b,))
        for tt in range(TT):
            p.op("sync", lambda e, tt=tt: e.dma_start(out=out_d[s, ts2(tt), :], in_=acc[:, tt, :]),
                 reads=(acc_b,), writes=(out_b,), inc=16)
        p.barrier()


_NC_CACHE = {}


def kernel(**inputs):
    if "nc" not in _NC_CACHE:
        _NC_CACHE["nc"] = build_core(stop_after="ALL")
    nc = _NC_CACHE["nc"]
    base = {}
    for name, shape in INPUT_SPECS:
        if name == "L0_attn_x":
            continue
        a = np.ascontiguousarray(np.asarray(inputs[name], dtype=np.float32).reshape(shape))
        base[name] = a
    x = np.ascontiguousarray(np.asarray(inputs["L0_attn_x"], dtype=np.float32))
    in_maps = []
    for c in range(NCORES):
        m = dict(base)
        m["L0_attn_x"] = np.ascontiguousarray(x[c * SEQ:(c + 1) * SEQ])
        in_maps.append(m)
    res = run_bass_kernel_spmd(nc, in_maps, core_ids=list(range(NCORES)))
    return np.concatenate([res.results[c]["out"] for c in range(NCORES)], axis=0)
